# version 03
# baseline (speedup 1.0000x reference)
import numpy as np
import ml_dtypes
from contextlib import ExitStack

import concourse.bass as bass
import concourse.mybir as mybir
from concourse.bass_utils import run_bass_kernel_spmd

F32 = mybir.dt.float32
BF16 = mybir.dt.bfloat16
I32 = mybir.dt.int32
U32 = mybir.dt.uint32
AF = mybir.ActivationFunctionType
ALU = mybir.AluOpType
AX = mybir.AxisListType

D = 1024
T = 4096
NCORES = 8
NDT = D // 128
TC = 512
NCHUNK = T // TC
LN_EPS = 1e-5


class Ctx:

    def __init__(self, nc, blk, stack):
        self.nc = nc
        self.blk = blk
        self.stack = stack
        self.sems = {}
        self.cnt = {}

    def sem(self, name):
        if name not in self.sems:
            self.sems[name] = self.stack.enter_context(self.nc.semaphore(name))
            self.cnt[name] = 0
        return self.sems[name]

    def inc(self, inst, name, n=1):
        s = self.sem(name)
        inst.then_inc(s, n)
        self.cnt[name] += n
        return self.cnt[name]

    def val(self, name):
        return self.cnt.get(name, 0)

    def wait(self, eng, name, v=None):
        if v is None:
            v = self.cnt[name]
        assert v <= self.cnt.get(name, 0), \
            f"wait {name}>={v} but only {self.cnt.get(name, 0)} emitted"
        if v > 0:
            eng.wait_ge(self.sems[name], v)

    def sb(self, name, shape, dtype):
        return self.stack.enter_context(self.nc.sbuf_tensor(name, shape, dtype))


def _bcast16(t, g):
    free = 1
    for s in t.shape[1:]:
        free *= s
    return bass.AP(tensor=t.ap().tensor, offset=0, ap=[[free, 128], [1, g], [0, 16]])


def _view3(t, ncols, inner, outer_stride=None):
    free = 1
    for s in t.shape[1:]:
        free *= s
    return bass.AP(tensor=t.ap().tensor, offset=0,
                   ap=[[free, 128], [inner, ncols // inner], [1, inner]])



def emit_fence(K, engines=("tensor", "vector", "scalar")):
    blk = K.blk
    snap = {n: K.val(n) for n in ("mm", "dv", "ac", "gp")}

    def waits(eng, skip):
        for n, v in snap.items():
            if n != skip and v > 0:
                K.wait(eng, n, v)

    if "tensor" in engines:
        @blk.tensor
        def _(tx):
            waits(tx, "mm")
    if "vector" in engines:
        @blk.vector
        def _(v):
            waits(v, "dv")
    if "scalar" in engines:
        @blk.scalar
        def _(sc):
            waits(sc, "ac")

def emit_ln_stats(K, rhs_tiles, rhs_sl, ones_rhs, sq_buf, ones_sq,
                  ps_sum, ps_ss, mu, tmp, ps_mur, ps_rsr, ones_r,
                  dep_data=()):
    blk = K.blk

    ac_pre = K.val("ac")

    @blk.scalar
    def _(sc):
        for s, v in dep_data:
            K.wait(sc, s, v)
        K.wait(sc, "mm", K.val("mm"))
        if DEV_LN != -1:
            for k in range(NDT):
                i = sc.activation(sq_buf[:, k * TC:(k + 1) * TC], rhs_tiles[k][:, rhs_sl],
                                  AF.Square)
            K.inc(i, "ac")
    ac_sq = K.val("ac")

    @blk.tensor
    def _(tx):
        K.wait(tx, "dv", K.val("dv"))
        K.wait(tx, "ac", ac_pre)
        for s, v in dep_data:
            K.wait(tx, s, v)
        if DEV_LN != 0:
            for k in range(NDT):
                i = tx.matmul(ps_sum, ones_rhs[:], rhs_tiles[k][:, rhs_sl],
                              start=(k == 0), stop=(k == NDT - 1))
            K.inc(i, "mm")
        K.wait(tx, "ac", ac_sq)
        if DEV_LN >= 1:
            for k in range(NDT):
                i = tx.matmul(ps_ss, ones_sq[:], sq_buf[:, k * TC:(k + 1) * TC],
                              start=(k == 0), stop=(k == NDT - 1))
            K.inc(i, "mm")
    mm_stats = K.val("mm")
    if DEV_LN < 2:
        return mm_stats

    @blk.vector
    def _(v):
        K.wait(v, "mm", mm_stats)
        i = v.tensor_scalar(mu[:], ps_sum, 1.0 / D, None, ALU.mult)
        K.inc(i, "dv")
    dv_mu = K.val("dv")
    if DEV_LN < 3:
        return K.val("mm")

    @blk.tensor
    def _(tx):
        K.wait(tx, "dv", dv_mu)
        i = tx.matmul(ps_mur, ones_r[:] if hasattr(ones_r, "ap") else ones_r, mu[:], start=True, stop=True)
        K.inc(i, "mm")
    mm_mur = K.val("mm")
    if DEV_LN < 4:
        return mm_mur

    @blk.vector
    def _(v):
        v.tensor_tensor(tmp, mu[:], mu[:], ALU.mult)
        K.wait(v, "mm", mm_mur)
        v.tensor_scalar(mu[:], ps_ss, 1.0 / D, LN_EPS, ALU.mult, ALU.add)
        i = v.tensor_tensor(mu[:], mu[:], tmp, ALU.subtract)
        K.inc(i, "dv")
    dv_var = K.val("dv")

    @blk.scalar
    def _(sc):
        K.wait(sc, "dv", dv_var)
        i = sc.activation(mu[:], mu[:], AF.Sqrt)
        K.inc(i, "ac")
    ac_sd = K.val("ac")

    @blk.vector
    def _(v):
        K.wait(v, "ac", ac_sd)
        i = v.reciprocal(mu[:], mu[:])
        K.inc(i, "dv")
    dv_rs = K.val("dv")
    if DEV_LN < 5:
        return K.val("mm")

    @blk.tensor
    def _(tx):
        K.wait(tx, "dv", dv_rs)
        i = tx.matmul(ps_rsr, ones_r[:], mu[:], start=True, stop=True)
        K.inc(i, "mm")
    return K.val("mm")




def emit_ln_x(K, X, sl, XH, XHB, G1, P, PS_SUM, PS_SS, MU, PS_MUR, PS_RSR, dep_dv,
              ones_sq=None):
    blk = K.blk
    mm_rep = emit_ln_stats(K, [x[:] for x in X], sl, P["ONES_F"], G1,
                           ones_sq if ones_sq is not None else P["ONES_B"],
                           PS_SUM, PS_SS, MU, PS_SUM, PS_MUR, PS_RSR,
                           P["ONES_R"], dep_data=[("dv", dep_dv)])

    @blk.vector
    def _(v):
        K.wait(v, "mm", mm_rep)
        for k in range(NDT):
            hk = XH[:, k * TC:(k + 1) * TC]
            v.tensor_tensor(hk, X[k][:, sl], PS_MUR, ALU.subtract)
            v.tensor_tensor(hk, hk, PS_RSR, ALU.mult)
            i = v.tensor_copy(XHB[:, k * TC:(k + 1) * TC], hk)
        K.inc(i, "dv")
    return K.val("dv")



def emit_l0ffn(K, X, wtc, latc, lbtc, P, nchunk=NCHUNK):
    nc, blk = K.nc, K.blk
    with ExitStack() as SS:
        E = SS.enter_context
        H = E(nc.sbuf_tensor("l0f_H", [128, NDT * TC], F32))
        HB = E(nc.sbuf_tensor("l0f_HB", [128, NDT * TC], BF16))
        Y1B = E(nc.sbuf_tensor("l0f_Y1B", [128, NDT * TC], BF16))
        G1 = E(nc.sbuf_tensor("l0f_G1", [128, NDT * TC], BF16))
        WB = E(nc.sbuf_tensor("l0f_WB", [128, 2 * NDT * 1024], BF16))
        LBT2 = E(nc.sbuf_tensor("l0f_LBT2", [32, 1024], BF16))
        LAT2 = E(nc.sbuf_tensor("l0f_LAT2", [128, NDT * 32], BF16))
        Z1 = E(nc.sbuf_tensor("l0f_Z1", [32, TC], BF16))
        MU = E(nc.sbuf_tensor("l0f_MU", [1, TC], F32))
        PS_SUM = E(nc.psum_tensor("l0f_ps_sum", [1, TC], F32))
        PS_SS = E(nc.psum_tensor("l0f_ps_ss", [1, TC], F32))
        PS_MUR = E(nc.psum_tensor("l0f_ps_mur", [128, TC], F32))
        PS_RSR = E(nc.psum_tensor("l0f_ps_rsr", [128, TC], F32))
        PS_Y = [E(nc.psum_tensor(f"l0f_ps_y{i}", [128, TC], F32)) for i in range(3)]
        PS_Z = E(nc.psum_tensor("l0f_ps_z", [32, TC], F32))

        def wslot(slot, dtile, otile):
            base = slot * (NDT * 1024) + dtile * 1024 + otile * 128
            return WB[:, base:base + 128]

        slot_ready = [0, 0]

        def load_w(q, m, slot):
            i = q.dma_start(out=WB[:, slot * (NDT * 1024):(slot + 1) * (NDT * 1024)],
                            in_=wtc[m * 128:(m + 1) * 128, :])
            slot_ready[slot] = K.inc(i, f"lw{slot}", 16)

        lora_ready = [0]

        def load_lora(q, m):
            i = q.dma_start(out=LAT2[:], in_=latc[m * 128:(m + 1) * 128, :])
            K.inc(i, "lwl", 16)
            i = q.dma_start(out=LBT2[:], in_=lbtc[m * 32:(m + 1) * 32, :])
            lora_ready[0] = K.inc(i, "lwl", 16)

        @blk.sync
        def _(q):
            load_w(q, 0, 0)
            load_w(q, 1, 1)
            load_lora(q, 0)

        yread = {0: 0, 1: 0, 2: 0}
        dv_h = K.val("dv")

        def emit_qlora(m, slot, rhs_buf, dep_rhs, out_kind):
            @blk.tensor
            def _(tx):
                K.wait(tx, "dv", dep_rhs)
                K.wait(tx, "dv", K.val("dv"))
                K.wait(tx, "lwl", lora_ready[0])
                for k in range(NDT):
                    i = tx.matmul(PS_Z[:], LAT2[:, k * 32:(k + 1) * 32],
                                  rhs_buf[:, k * TC:(k + 1) * TC],
                                  start=(k == 0), stop=(k == NDT - 1))
                K.inc(i, "mm")
            mm_z = K.val("mm")

            @blk.vector
            def _(v):
                K.wait(v, "mm", mm_z)
                i = v.tensor_copy(Z1[:], PS_Z[:])
                K.inc(i, "dv")
            dv_z = K.val("dv")

            for o in range(NDT):
                bank = o % 3
                @blk.tensor
                def _(tx, o=o, bank=bank):
                    K.wait(tx, f"lw{slot}", slot_ready[slot])
                    if yread[bank] > 0:
                        K.wait(tx, "dv", yread[bank])
                    for k in range(NDT):
                        tx.matmul(PS_Y[bank][:], wslot(slot, k, o),
                                  rhs_buf[:, k * TC:(k + 1) * TC],
                                  start=(k == 0), stop=False)
                    K.wait(tx, "dv", dv_z)
                    i = tx.matmul(PS_Y[bank][:], LBT2[:, o * 128:(o + 1) * 128],
                                  Z1[:], start=False, stop=True)
                    K.inc(i, "mm")
                mm_y = K.val("mm")

                @blk.vector
                def _(v, o=o, bank=bank, mm_y=mm_y):
                    K.wait(v, "mm", mm_y)
                    if out_kind == "y1b":
                        i = v.tensor_scalar(Y1B[:, o * TC:(o + 1) * TC], PS_Y[bank][:],
                                            P["BIAS"][:, m * 8 + o:m * 8 + o + 1], None, ALU.add)
                    else:
                        ho = H[:, o * TC:(o + 1) * TC]
                        v.scalar_tensor_tensor(ho, PS_Y[bank][:],
                                               P["BIAS"][:, m * 8 + o:m * 8 + o + 1],
                                               ho, ALU.add, ALU.add)
                        i = v.tensor_copy(HB[:, o * TC:(o + 1) * TC], ho)
                    yread[bank] = K.inc(i, "dv")
            return K.val("dv")

        for c in range(nchunk):
            sl = slice(c * TC, (c + 1) * TC)
            mm_rep = emit_ln_stats(K, [x[:] for x in X], sl, P["ONES_F"], G1, P["ONES_B"],
                                   PS_SUM[:], PS_SS[:], MU, PS_SUM[:], PS_MUR[:], PS_RSR[:],
                                   P["ONES_R"], dep_data=[("dv", dv_h)])

            @blk.vector
            def _(v):
                K.wait(v, "mm", mm_rep)
                for k in range(NDT):
                    hk = H[:, k * TC:(k + 1) * TC]
                    if DEV_LN >= 5:
                        v.tensor_tensor(hk, X[k][:, sl], PS_MUR[:], ALU.subtract)
                        v.tensor_tensor(hk, hk, PS_RSR[:], ALU.mult)
                    else:
                        v.tensor_copy(hk, X[k][:, sl])
                    i = v.tensor_copy(HB[:, k * TC:(k + 1) * TC], hk)
                K.inc(i, "dv")
            dv_h = K.val("dv")

            for b in range(3):
                if DEV_SUB < 2:
                    break
                m0, m1 = 2 * b, 2 * b + 1
                dv_y1 = emit_qlora(m0, 0, HB, dv_h, "y1b")

                mm_now0 = K.val("mm")
                @blk.sync
                def _(q, m1=m1, mm_now0=mm_now0):
                    K.wait(q, "mm", mm_now0)
                    load_lora(q, m1)

                if DEV_SUB < 3:
                    break
                mm_rep2 = emit_ln_stats(K, [Y1B[:, k * TC:(k + 1) * TC] for k in range(NDT)],
                                        slice(None), P["ONES_B"], G1, P["ONES_B"],
                                        PS_SUM[:], PS_SS[:], MU, PS_SUM[:], PS_MUR[:], PS_RSR[:],
                                        P["ONES_R"], dep_data=[("dv", dv_y1)])

                @blk.vector
                def _(v, b=b):
                    K.wait(v, "mm", mm_rep2)
                    for o in range(NDT):
                        g = G1[:, o * TC:(o + 1) * TC]
                        v.tensor_tensor(g, Y1B[:, o * TC:(o + 1) * TC], PS_MUR[:], ALU.subtract)
                        i = v.tensor_tensor(g, g, PS_RSR[:], ALU.mult)
                    K.inc(i, "dv")
                dv_n = K.val("dv")

                @blk.scalar
                def _(sc, b=b):
                    K.wait(sc, "dv", dv_n)
                    for o in range(NDT):
                        g = G1[:, o * TC:(o + 1) * TC]
                        i = sc.activation(g, g, AF.Gelu,
                                          scale=P["GAM"][:, b * 8 + o:b * 8 + o + 1],
                                          bias=P["BET"][:, b * 8 + o:b * 8 + o + 1])
                    K.inc(i, "ac")
                ac_g = K.val("ac")

                @blk.tensor
                def _(tx):
                    K.wait(tx, "ac", ac_g)

                if DEV_SUB < 4:
                    break
                dv_h = emit_qlora(m1, 1, G1, K.val("dv"), "h")

                nxt = b + 1
                if not (c == nchunk - 1 and nxt == 3):
                    nm0, nm1 = (2 * nxt, 2 * nxt + 1) if nxt < 3 else (0, 1)
                    mm_now = K.val("mm")
                    @blk.sync
                    def _(q, nm0=nm0, nm1=nm1, mm_now=mm_now):
                        K.wait(q, "mm", mm_now)
                        load_w(q, nm0, 0)
                        load_w(q, nm1, 1)
                        load_lora(q, nm0)

            if DEV_SUB < 5:
                continue
            @blk.vector
            def _(v):
                for k in range(NDT):
                    i = v.scalar_tensor_tensor(X[k][:, sl], H[:, k * TC:(k + 1) * TC],
                                               0.5, X[k][:, sl], ALU.mult, ALU.add)
                K.inc(i, "dv")
            dv_h = K.val("dv")


def emit_l0ffn_dequant(K, qw, qs, wtc, ident_bf):
    nc, blk = K.nc, K.blk
    with ExitStack() as SS:
        E = SS.enter_context
        QW = [E(nc.sbuf_tensor(f"dq_qw{i}", [128, 1024], I32)) for i in range(2)]
        SC = [E(nc.sbuf_tensor(f"dq_sc{i}", [128, 64], F32)) for i in range(2)]
        WD = [E(nc.sbuf_tensor(f"dq_wd{i}", [128, 1024], BF16)) for i in range(2)]
        WT = E(nc.sbuf_tensor("dq_wt", [128, NDT * 1024], BF16))
        PT = [E(nc.psum_tensor(f"dq_pt{i}", [128, 128], BF16)) for i in range(2)]

        steps = [(m, ot) for m in range(6) for ot in range(NDT)]
        dv_done = {}
        mm_done = {}
        ac_done = {}
        st_done = {}
        for idx, (m, ot) in enumerate(steps):
            p = idx % 2

            @blk.sync
            def _(q, m=m, ot=ot, p=p, idx=idx):
                if idx >= 2:
                    K.wait(q, "dv", dv_done[idx - 2])
                i = q.dma_start(out=QW[p][:], in_=qw[m, ot * 128:(ot + 1) * 128, :])
                K.inc(i, "lqw", 16)
                i = q.dma_start(out=SC[p][:], in_=qs[m, ot * 128:(ot + 1) * 128, :])
                K.inc(i, "lqw", 16)
            lx_now = K.val("lqw")

            @blk.vector
            def _(v, p=p, idx=idx, lx_now=lx_now):
                K.wait(v, "lqw", lx_now)
                if idx >= 2:
                    K.wait(v, "mm", mm_done[idx - 2])
                v.tensor_scalar(WD[p][:], QW[p][:], 1.0 / 7.5, -1.0, ALU.mult, ALU.add)
                i = v.tensor_tensor(_view3(WD[p], 1024, 16), _view3(WD[p], 1024, 16),
                                    _bcast16(SC[p], 64), ALU.mult)
                dv_done[idx] = K.inc(i, "dv")

            for k in range(NDT):
                @blk.tensor
                def _(tx, p=p, idx=idx, k=k):
                    if k == 0:
                        K.wait(tx, "dv", dv_done[idx])
                    if idx * NDT + k >= 2:
                        K.wait(tx, "ac", K.val("ac") - 1)
                    i = tx.transpose(PT[k % 2][:], WD[p][:, k * 128:(k + 1) * 128],
                                     ident_bf[:])
                    K.inc(i, "mm")
                mmv = K.val("mm")

                @blk.scalar
                def _(sc, p=p, idx=idx, m=m, ot=ot, k=k, mmv=mmv):
                    K.wait(sc, "mm", mmv)
                    if k == 0 and ot == 0 and m >= 1:
                        K.wait(sc, "lqs", st_done[m - 1])
                    i = sc.copy(WT[:, k * 1024 + ot * 128:k * 1024 + (ot + 1) * 128],
                                PT[k % 2][:])
                    K.inc(i, "ac")
            ac_done[idx] = K.val("ac")
            mm_done[idx] = K.val("mm")

            if ot == NDT - 1:
                @blk.sync
                def _(q, m=m, idx=idx):
                    K.wait(q, "ac", ac_done[idx])
                    i = q.dma_start(out=wtc[m * 128:(m + 1) * 128, :], in_=WT[:])
                    st_done[m] = K.inc(i, "lqs", 16)


def emit_transpose_small(K, dst_ap, src_ap, pt, ident, np_, evac_rows):
    blk = K.blk

    @blk.tensor
    def _(tx):
        K.wait(tx, "ac", K.val("ac"))
        i = tx.transpose(pt, src_ap, ident)
        K.inc(i, "mm")
    mmv = K.val("mm")

    @blk.scalar
    def _(sc):
        K.wait(sc, "mm", mmv)
        i = sc.copy(dst_ap, pt if evac_rows is None else pt)
        K.inc(i, "ac")


def emit_l0ffn_params(K, dram, ident_bf, ident_f, P):
    nc, blk = K.nc, K.blk
    BIAS, GAM, BET = P["BIAS"], P["GAM"], P["BET"]
    ONES_F, ONES_B, ONES_R = P["ONES_F"], P["ONES_B"], P["ONES_R"]

    with ExitStack() as SS:
        E = SS.enter_context
        LA_ST = E(nc.sbuf_tensor("p_LAST", [32, 6 * 1024], BF16))
        LA_EV = E(nc.sbuf_tensor("p_LAEV", [128, 256], BF16))
        LB_EV = E(nc.sbuf_tensor("p_LBEV", [32, 1024], BF16))
        IDF_ = None
        LB_ST = E(nc.sbuf_tensor("p_LBST", [128, 6 * 8 * 32], BF16))
        QB_ST = E(nc.sbuf_tensor("p_QBST", [48, 128], F32))
        LG_ST = E(nc.sbuf_tensor("p_LGST", [24, 128], F32))
        LB2_ST = E(nc.sbuf_tensor("p_LB2ST", [24, 128], F32))
        PTB = E(nc.psum_tensor("p_PTB", [128, 128], BF16))
        PTF = E(nc.psum_tensor("p_PTF", [128, 128], F32))

        @blk.vector
        def _(v):
            v.memset(ONES_F[:], 1.0)
            v.memset(ONES_B[:], 1.0)
            i = v.memset(ONES_R[:], 1.0)
            K.inc(i, "dv")

        @blk.sync
        def _(q):
            i = q.dma_start(out=QB_ST[:], in_=dram["qb"].rearrange("m (a p) -> (m a) p", p=128))
            K.inc(i, "lpr", 16)
            i = q.dma_start(out=LG_ST[:], in_=dram["lg"].rearrange("m (a p) -> (m a) p", p=128))
            K.inc(i, "lpr", 16)
            i = q.dma_start(out=LB2_ST[:], in_=dram["lbeta"].rearrange("m (a p) -> (m a) p", p=128))
            K.inc(i, "lpr", 16)

        @blk.gpsimd
        def _(g):
            for m in range(6):
                i = g.dma_start(out=LA_ST[:, m * 1024:(m + 1) * 1024], in_=dram["la"][m])
                K.inc(i, "lw", 16)
                i = g.dma_start(out=LB_ST[:, m * 256:(m + 1) * 256],
                                in_=bass.AP(tensor=dram["lb"].tensor, offset=m * D * 32,
                                            ap=[[32, 128], [128 * 32, 8], [1, 32]]))
                K.inc(i, "lw", 16)
        lw_all = K.val("lw")
        lx_all = K.val("lpr")

        for (src, dst, rows) in ((QB_ST, BIAS, 48), (LG_ST, GAM, 24), (LB2_ST, BET, 24)):
            @blk.tensor
            def _(tx, src=src, rows=rows):
                K.wait(tx, "lpr", lx_all)
                K.wait(tx, "ac", K.val("ac"))
                i = tx.transpose(PTF[:, 0:rows], src[:], ident_f[0:rows, 0:rows])
                K.inc(i, "mm")
            mmv = K.val("mm")

            @blk.scalar
            def _(sc, dst=dst, rows=rows, mmv=mmv):
                K.wait(sc, "mm", mmv)
                i = sc.copy(dst[:], PTF[:, 0:rows])
                K.inc(i, "ac")

        for m in range(6):
            for k in range(NDT):
                @blk.tensor
                def _(tx, m=m, k=k):
                    K.wait(tx, "lw", lw_all)
                    K.wait(tx, "ac", K.val("ac"))
                    if k == 0:
                        K.wait(tx, "lps", K.val("lps"))
                    i = tx.transpose(PTB[:, 0:32],
                                     LA_ST[:, m * 1024 + k * 128:m * 1024 + (k + 1) * 128],
                                     ident_bf[0:32, 0:32])
                    K.inc(i, "mm")
                mmv = K.val("mm")

                @blk.scalar
                def _(sc, m=m, k=k, mmv=mmv):
                    K.wait(sc, "mm", mmv)
                    i = sc.copy(LA_EV[:, k * 32:(k + 1) * 32], PTB[:, 0:32])
                    K.inc(i, "ac")
            acv = K.val("ac")

            @blk.sync
            def _(q, m=m, acv=acv):
                K.wait(q, "ac", acv)
                i = q.dma_start(out=dram["latc"][m * 128:(m + 1) * 128, :], in_=LA_EV[:])
                K.inc(i, "lps", 16)

        for m in range(6):
            for ot in range(NDT):
                @blk.tensor
                def _(tx, m=m, ot=ot):
                    K.wait(tx, "lw", lw_all)
                    K.wait(tx, "ac", K.val("ac"))
                    if ot == 0:
                        K.wait(tx, "lps", K.val("lps"))
                    i = tx.transpose(PTB[0:32, 0:128],
                                     LB_ST[:, m * 256 + ot * 32:m * 256 + (ot + 1) * 32],
                                     ident_bf[:])
                    K.inc(i, "mm")
                mmv = K.val("mm")

                @blk.scalar
                def _(sc, m=m, ot=ot, mmv=mmv):
                    K.wait(sc, "mm", mmv)
                    i = sc.copy(LB_EV[:, ot * 128:(ot + 1) * 128], PTB[0:32, 0:128])
                    K.inc(i, "ac")
            acv = K.val("ac")

            @blk.sync
            def _(q, m=m, acv=acv):
                K.wait(q, "ac", acv)
                i = q.dma_start(out=dram["lbtc"][m * 32:(m + 1) * 32, :], in_=LB_EV[:])
                K.inc(i, "lps", 16)

        @blk.tensor
        def _(tx):
            K.wait(tx, "ac", K.val("ac"))

    return P



DEV_PHASE = 99
DEV_SUB = 99
DEV_LN = 99

def _build_l0ffn_launch():
    nc = bass.Bass("TRN2", target_bir_lowering=False, detect_race_conditions=False)
    xT = nc.dram_tensor("xT", [D, T], F32, kind="ExternalInput").ap()
    qw = nc.dram_tensor("qw", [6, D, D], I32, kind="ExternalInput").ap()
    qs = nc.dram_tensor("qs", [6, D, 64], F32, kind="ExternalInput").ap()
    qb = nc.dram_tensor("qb", [6, D], F32, kind="ExternalInput").ap()
    la = nc.dram_tensor("la", [6, 32, D], F32, kind="ExternalInput").ap()
    lb = nc.dram_tensor("lb", [6, D, 32], F32, kind="ExternalInput").ap()
    lg = nc.dram_tensor("lg", [3, D], F32, kind="ExternalInput").ap()
    lbeta = nc.dram_tensor("lbeta", [3, D], F32, kind="ExternalInput").ap()
    idenb = nc.dram_tensor("idenb", [128, 128], BF16, kind="ExternalInput").ap()
    idenf = nc.dram_tensor("idenf", [128, 128], F32, kind="ExternalInput").ap()
    yT = nc.dram_tensor("yT", [D, T], F32, kind="ExternalOutput").ap()
    wtc = nc.dram_tensor("wtc", [6 * 128, NDT * 1024], BF16, kind="Internal").ap()
    lbtc = nc.dram_tensor("lbtc", [6 * 32, D], BF16, kind="Internal").ap()
    latc = nc.dram_tensor("latc", [6 * 128, 256], BF16, kind="Internal").ap()

    with ExitStack() as stack:
        blk = stack.enter_context(nc.Block())
        K = Ctx(nc, blk, stack)
        for s in ("lx", "lw", "lwl", "lw0", "lw1", "lpr", "lps", "lqw", "lqs", "mm", "dv", "ac", "so"):
            K.sem(s)
        X = [K.sb(f"X{k}", [128, T], F32) for k in range(NDT)]
        IDB = K.sb("IDB", [128, 128], BF16)
        IDF = K.sb("IDF", [128, 128], F32)

        @blk.sync
        def _(q):
            i = q.dma_start(out=IDB[:], in_=idenb)
            K.inc(i, "lx", 16)
            i = q.dma_start(out=IDF[:], in_=idenf)
            K.inc(i, "lx", 16)
            for k in range(NDT):
                i = q.dma_start(out=X[k][:], in_=xT[k * 128:(k + 1) * 128, :])
                K.inc(i, "lx", 16)
        lx_x = K.val("lx")

        @blk.tensor
        def _(tx):
            K.wait(tx, "lx", lx_x)
        @blk.vector
        def _(v):
            K.wait(v, "lx", lx_x)
        @blk.scalar
        def _(sc):
            K.wait(sc, "lx", lx_x)

        P0 = dict(
            BIAS=K.sb("p_BIAS", [128, 48], F32),
            GAM=K.sb("p_GAM", [128, 24], F32),
            BET=K.sb("p_BET", [128, 24], F32),
            ONES_F=K.sb("p_ONESF", [128, 1], F32),
            ONES_B=K.sb("p_ONESB", [128, 1], BF16),
            ONES_R=K.sb("p_ONESR", [1, 128], F32),
        )
        prm = emit_l0ffn_params(K, dict(la=la, lb=lb, qb=qb, lg=lg, lbeta=lbeta,
                                        lbtc=lbtc, latc=latc), IDB, IDF, P0)
        if DEV_PHASE >= 2:
            emit_l0ffn_dequant(K, qw, qs, wtc, IDB)
        emit_fence(K)

        @blk.sync
        def _(q):
            K.wait(q, "lqs", K.val("lqs"))
            K.wait(q, "lps", K.val("lps"))

        if DEV_PHASE >= 3:
            emit_l0ffn(K, X, wtc, latc, lbtc, prm,
                       nchunk=(1 if DEV_PHASE == 3 else NCHUNK))

        dv_done = K.val("dv")

        @blk.sync
        def _(q):
            K.wait(q, "dv", dv_done)
            for k in range(NDT):
                i = q.dma_start(out=yT[k * 128:(k + 1) * 128, :], in_=X[k][:])
                K.inc(i, "so", 16)
            K.wait(q, "so", K.val("so"))
    return nc


def dev_run_l0ffn(x, inp):
    nc = _build_l0ffn_launch()
    idenb = np.eye(128).astype(ml_dtypes.bfloat16)
    idenf = np.eye(128).astype(np.float32)
    in_maps = []
    for b in range(NCORES):
        in_maps.append({
            "xT": np.ascontiguousarray(x[b].T),
            "qw": np.asarray(inp["L0_ffn_qweights"]), "qs": np.asarray(inp["L0_ffn_scales"]),
            "qb": np.asarray(inp["L0_ffn_biases"]), "la": np.asarray(inp["L0_ffn_lora_a"]),
            "lb": np.asarray(inp["L0_ffn_lora_b"]), "lg": np.asarray(inp["L0_ffn_ln_gamma"]),
            "lbeta": np.asarray(inp["L0_ffn_ln_beta"]), "idenb": idenb, "idenf": idenf,
        })
    res = run_bass_kernel_spmd(nc, in_maps, core_ids=list(range(NCORES)))
    return np.stack([np.asarray(res.results[b]["yT"], np.float32).T for b in range(NCORES)])


def kernel(**inputs):
    x = np.asarray(inputs["L0_attn_x"], np.float32)
    x = dev_run_l0ffn(x, inputs)
    x = dev_run_l0attn(x, inputs)
    x = dev_run_l0ffn(x, inputs)
    x = dev_run_l1ffn(x, inputs)
    x = dev_run_l1attn(x, inputs)
    x = dev_run_l1ffn(x, inputs)
    x = dev_run_l2ffn(x, inputs)
    x = dev_run_l2attn(x, inputs)
    x = dev_run_l2ffn(x, inputs)
    x = dev_run_l3ffn(x, inputs)
    x = dev_run_l3attn(x, inputs)
    x = dev_run_l3ffn(x, inputs)
    return x



_RPC = [0]

def emit_repack_up(K, raw, cache, E_, H_):
    nc, blk = K.nc, K.blk
    np_ = H_ // 512
    _RPC[0] += 1
    uid = _RPC[0]
    with ExitStack() as SS:
        En = SS.enter_context
        ST = [En(nc.sbuf_tensor(f"rp{uid}_st{i}", [128, H_], BF16)) for i in range(2)]
        ld_done = {}
        st_done = {}
        for e in range(E_):
            for k in range(NDT):
                idx = e * NDT + k
                p = idx % 2
                @blk.gpsimd
                def _(g, e=e, k=k, p=p, idx=idx):
                    if idx >= 2:
                        K.wait(g, "lqs", st_done[idx - 2])
                    i = g.dma_start(out=ST[p][:], in_=raw[e, k * 128:(k + 1) * 128, :])
                    ld_done[idx] = K.inc(i, "lqw", 16)

                @blk.sync
                def _(q, e=e, k=k, p=p, idx=idx):
                    K.wait(q, "lqw", ld_done[idx])
                    for qq in range(np_):
                        i = q.dma_start(
                            out=cache[(e * np_ + qq) * 128:(e * np_ + qq + 1) * 128,
                                      k * 512:(k + 1) * 512],
                            in_=ST[p][:, qq * 512:(qq + 1) * 512])
                        K.inc(i, "lqs", 16)
                    st_done[idx] = K.val("lqs")


def emit_repack_dn(K, raw, cache, E_, IN_, OUT_):
    nc, blk = K.nc, K.blk
    NI = IN_ // 128
    _RPC[0] += 1
    uid = _RPC[0]
    with ExitStack() as SS:
        En = SS.enter_context
        ST = [En(nc.sbuf_tensor(f"rd{uid}_st{i}", [128, OUT_], BF16)) for i in range(2)]
        ld_done = {}
        st_done = {}
        for e in range(E_):
            for k in range(NI):
                idx = e * NI + k
                p = idx % 2
                @blk.gpsimd
                def _(g, e=e, k=k, p=p, idx=idx):
                    if idx >= 2:
                        K.wait(g, "lqs", st_done[idx - 2])
                    i = g.dma_start(out=ST[p][:], in_=raw[e, k * 128:(k + 1) * 128, :])
                    ld_done[idx] = K.inc(i, "lqw", 16)

                @blk.sync
                def _(q, e=e, k=k, p=p, idx=idx):
                    K.wait(q, "lqw", ld_done[idx])
                    i = q.dma_start(out=cache[e * 128:(e + 1) * 128, k * OUT_:(k + 1) * OUT_],
                                    in_=ST[p][:])
                    st_done[idx] = K.inc(i, "lqs", 16)


def emit_repack_tr(K, raw, cache_up, e_range, H_, ident_bf):
    nc, blk = K.nc, K.blk
    HQ = H_ // 4
    NO = H_ // 128
    _RPC[0] += 1
    uid = _RPC[0]
    with ExitStack() as SS:
        En = SS.enter_context
        ST = [En(nc.sbuf_tensor(f"rt{uid}_st{i}", [128, 1024], BF16)) for i in range(2)]
        PT = [En(nc.psum_tensor(f"rt{uid}_pt{i}", [128, 128], BF16)) for i in range(2)]
        EV = En(nc.sbuf_tensor(f"rt{uid}_ev", [128, 128], BF16))
        ld_done = {}
        st_done = {}
        pt_free = {0: 0, 1: 0}
        for e in e_range:
            for ko in range(NO):
                idx = e * NO + ko
                p = idx % 2
                @blk.gpsimd
                def _(g, e=e, ko=ko, p=p, idx=idx):
                    if idx >= 2:
                        K.wait(g, "lqs", st_done[idx - 2])
                    i = g.dma_start(out=ST[p][:], in_=raw[e, ko * 128:(ko + 1) * 128, :])
                    ld_done[idx] = K.inc(i, "lqw", 16)

                for k in range(NDT):
                    @blk.tensor
                    def _(tx, p=p, k=k, idx=idx):
                        if k == 0:
                            K.wait(tx, "lqw", ld_done[idx])
                        if pt_free[k % 2] > 0:
                            K.wait(tx, "ac", pt_free[k % 2])
                        i = tx.transpose(PT[k % 2][:], ST[p][:, k * 128:(k + 1) * 128],
                                         ident_bf[:])
                        K.inc(i, "mm")
                    mmv = K.val("mm")

                    @blk.scalar
                    def _(sc, k=k, mmv=mmv):
                        K.wait(sc, "mm", mmv)
                        K.wait(sc, "lqs", K.val("lqs"))
                        i = sc.copy(EV[:], PT[k % 2][:])
                        pt_free[k % 2] = K.inc(i, "ac")
                    acv = K.val("ac")

                    @blk.sync
                    def _(q, e=e, ko=ko, k=k, acv=acv, idx=idx):
                        K.wait(q, "ac", acv)
                        qq = ko // (NO // 4)
                        olocal = (ko % (NO // 4)) * 128
                        i = q.dma_start(
                            out=cache_up[(e * 4 + qq) * 128:(e * 4 + qq + 1) * 128,
                                         k * HQ + olocal:k * HQ + olocal + 128],
                            in_=EV[:])
                        st_done[idx] = K.inc(i, "lqs", 16)



def emit_gating(K, LG, GT, GTB, SCR, PTF, ident_f, ones_r, dv_lg, E_, var,
                scale=0.5, keep_fn=None):
    nc, blk = K.nc, K.blk
    NT_ = TC // 128
    NE = NT_ * E_
    eqs = SCR[:, 0:NE]
    eq2s = SCR[:, NE:2 * NE]
    eq3s = SCR[:, 2 * NE:3 * NE]
    m1 = SCR[:, 3 * NE:3 * NE + NT_]
    m2 = SCR[:, 3 * NE + NT_:3 * NE + 2 * NT_]
    w1 = SCR[:, 3 * NE + 2 * NT_:3 * NE + 3 * NT_]
    w2 = SCR[:, 3 * NE + 3 * NT_:3 * NE + 4 * NT_]

    def step(v, i):
        K.inc(i, "dv")
        K.wait(v, "dv")

    @blk.vector
    def _(v):
        K.wait(v, "dv", dv_lg)
        for tt in range(NT_):
            i = v.tensor_reduce(m1[:, tt:tt + 1], LG[:, tt * E_:(tt + 1) * E_], AX.X, ALU.max)
        step(v, i)
        for tt in range(NT_):
            i = v.tensor_scalar(eqs[:, tt * E_:(tt + 1) * E_], LG[:, tt * E_:(tt + 1) * E_],
                                m1[:, tt:tt + 1], None, ALU.is_equal)
        step(v, i)
        for tt in range(NT_):
            i = v.scalar_tensor_tensor(eq3s[:, tt * E_:(tt + 1) * E_],
                                       eqs[:, tt * E_:(tt + 1) * E_], -1e30,
                                       LG[:, tt * E_:(tt + 1) * E_], ALU.mult, ALU.add)
        step(v, i)
        for tt in range(NT_):
            i = v.tensor_reduce(m2[:, tt:tt + 1], eq3s[:, tt * E_:(tt + 1) * E_], AX.X, ALU.max)
        step(v, i)
        for tt in range(NT_):
            i = v.tensor_scalar(eq2s[:, tt * E_:(tt + 1) * E_], eq3s[:, tt * E_:(tt + 1) * E_],
                                m2[:, tt:tt + 1], None, ALU.is_equal)
        K.inc(i, "dv")
    dv2 = K.val("dv")

    if var == "top2":
        @blk.vector
        def _(v):
            K.wait(v, "dv", dv2)
            i = v.tensor_tensor(w1[:], m1[:], m2[:], ALU.subtract)
            K.inc(i, "dv")
        dv3 = K.val("dv")

        @blk.scalar
        def _(sc):
            K.wait(sc, "dv", dv3)
            i = sc.activation(w2[:], w1[:], AF.Sigmoid)
            K.inc(i, "ac")
        ac2 = K.val("ac")

        @blk.vector
        def _(v):
            K.wait(v, "ac", ac2)
            for tt in range(NT_):
                i = v.tensor_tensor(eq3s[:, tt * E_:(tt + 1) * E_],
                                    eqs[:, tt * E_:(tt + 1) * E_],
                                    eq2s[:, tt * E_:(tt + 1) * E_], ALU.subtract)
            step(v, i)
            for tt in range(NT_):
                i = v.tensor_scalar(eqs[:, tt * E_:(tt + 1) * E_],
                                    eq3s[:, tt * E_:(tt + 1) * E_],
                                    w2[:, tt:tt + 1], scale, ALU.mult, ALU.mult)
            step(v, i)
            for tt in range(NT_):
                i = v.tensor_scalar(eq3s[:, tt * E_:(tt + 1) * E_],
                                    eq2s[:, tt * E_:(tt + 1) * E_], scale, None, ALU.mult)
            step(v, i)
            for tt in range(NT_):
                i = v.tensor_tensor(LG[:, tt * E_:(tt + 1) * E_],
                                    eqs[:, tt * E_:(tt + 1) * E_],
                                    eq3s[:, tt * E_:(tt + 1) * E_], ALU.add)
            K.inc(i, "dv")
        dv4 = K.val("dv")
    else:
        dvk, k0, k1 = keep_fn(K, eqs, eq2s, dv2)

        @blk.vector
        def _(v):
            K.wait(v, "dv", dvk)
            for tt in range(NT_):
                i = v.tensor_scalar(eq3s[:, tt * E_:(tt + 1) * E_],
                                    eqs[:, tt * E_:(tt + 1) * E_],
                                    m1[:, tt:tt + 1], k0[:, tt:tt + 1], ALU.mult, ALU.mult)
            step(v, i)
            for tt in range(NT_):
                i = v.tensor_scalar(eqs[:, tt * E_:(tt + 1) * E_],
                                    eq3s[:, tt * E_:(tt + 1) * E_], scale, None, ALU.mult)
            step(v, i)
            for tt in range(NT_):
                i = v.tensor_scalar(eq3s[:, tt * E_:(tt + 1) * E_],
                                    eq2s[:, tt * E_:(tt + 1) * E_],
                                    m2[:, tt:tt + 1], k1[:, tt:tt + 1], ALU.mult, ALU.mult)
            step(v, i)
            for tt in range(NT_):
                i = v.tensor_scalar(eq2s[:, tt * E_:(tt + 1) * E_],
                                    eq3s[:, tt * E_:(tt + 1) * E_], scale, None, ALU.mult)
            step(v, i)
            for tt in range(NT_):
                i = v.tensor_tensor(LG[:, tt * E_:(tt + 1) * E_],
                                    eqs[:, tt * E_:(tt + 1) * E_],
                                    eq2s[:, tt * E_:(tt + 1) * E_], ALU.add)
            K.inc(i, "dv")
        dv4 = K.val("dv")

    gat = LG

    for tt in range(NT_):
        @blk.tensor
        def _(tx, tt=tt):
            K.wait(tx, "dv", dv4)
            K.wait(tx, "ac", K.val("ac"))
            i = tx.transpose(PTF[0:E_, tt * 128:(tt + 1) * 128],
                             gat[:, tt * E_:(tt + 1) * E_], ident_f[:])
            K.inc(i, "mm")
        mmv = K.val("mm")

        @blk.scalar
        def _(sc, tt=tt, mmv=mmv):
            K.wait(sc, "mm", mmv)
            i = sc.copy(GTB[0:E_, tt * 128:(tt + 1) * 128], PTF[0:E_, tt * 128:(tt + 1) * 128])
            K.inc(i, "ac")
    for e in range(E_):
        for tt in range(NT_):
            @blk.tensor
            def _(tx, e=e, tt=tt):
                K.wait(tx, "ac", K.val("ac"))
                i = tx.transpose(PTF[0:1, (e % 4) * 128:(e % 4) * 128 + 128],
                                 gat[:, tt * E_ + e:tt * E_ + e + 1], ident_f[:])
                K.inc(i, "mm")
            mmv = K.val("mm")

            @blk.scalar
            def _(sc, e=e, tt=tt, mmv=mmv):
                K.wait(sc, "mm", mmv)
                i = sc.copy(GT[0:1, e * TC + tt * 128:e * TC + (tt + 1) * 128],
                            PTF[0:1, (e % 4) * 128:(e % 4) * 128 + 128])
                K.inc(i, "ac")
    return K.val("ac")



MOE_NCHUNK = NCHUNK

def emit_moe(K, xd_in, xd_out, P, cfg):
    nc, blk = K.nc, K.blk
    E_, H_, var = cfg["E"], cfg["H"], cfg["var"]
    NH = H_ // 128
    NPU = H_ // 512
    HGRP = 8 if NH >= 16 else NH
    NGRP = NH // HGRP
    DN_T = cfg.get("dn_tiles", HGRP)
    w_up, w_dn = cfg["w_up"], cfg["w_dn"]
    nup = len(w_up)
    scale = cfg.get("scale", 0.5)

    with ExitStack() as SS:
        E = SS.enter_context
        XC = [E(nc.sbuf_tensor(f"mo_XC{k}", [128, TC], F32)) for k in range(NDT)]
        XHB = E(nc.sbuf_tensor("mo_XHB", [128, NDT * TC], BF16))
        G1 = E(nc.sbuf_tensor("mo_G1", [128, NDT * TC], F32))
        ACC = E(nc.sbuf_tensor("mo_ACC", [128, NDT * TC], F32))
        ACT = E(nc.sbuf_tensor("mo_ACT", [128, HGRP * TC], BF16))
        S0 = E(nc.sbuf_tensor("mo_S0", [128, 2 * TC], BF16))
        WU = [E(nc.sbuf_tensor(f"mo_WU{j}", [128, 2 * NDT * 512], BF16)) for j in range(nup)]
        WD = E(nc.sbuf_tensor("mo_WD", [128, 2 * DN_T * 1024], BF16))
        RT = E(nc.sbuf_tensor("mo_RT", [128, NDT * E_], BF16))
        LG = E(nc.sbuf_tensor("mo_LG", [128, (TC // 128) * E_], F32))
        GT = E(nc.sbuf_tensor("mo_GT", [1, E_ * TC], F32))
        GTB = E(nc.sbuf_tensor("mo_GTB", [E_, TC], BF16))
        SCR = E(nc.sbuf_tensor("mo_SCR", [128, (TC // 128) * (3 * E_ + 6)], F32))
        MU = E(nc.sbuf_tensor("mo_MU", [1, TC], F32))
        PS_SUM = E(nc.psum_tensor("mo_ps_sum", [1, TC], F32))
        PS_SS = E(nc.psum_tensor("mo_ps_ss", [1, TC], F32))
        PS_MUR = E(nc.psum_tensor("mo_ps_mur", [128, TC], F32))
        PS_RSR = E(nc.psum_tensor("mo_ps_rsr", [128, TC], F32))
        PS_H = [E(nc.psum_tensor(f"mo_ps_h{i}", [128, TC], F32)) for i in range(2)]
        PS_O = [E(nc.psum_tensor(f"mo_ps_o{i}", [128, TC], F32)) for i in range(2)]
        K.moe_ps_scratch = PS_O[0]

        setup_done = cfg["setup"](K, RT)

        up_ready = {}
        dn_ready = {}

        def load_up(q, j, e, piece, slot):
            i = q.dma_start(out=WU[j][:, slot * NDT * 512:(slot + 1) * NDT * 512],
                            in_=w_up[j][(e * NPU + piece) * 128:(e * NPU + piece + 1) * 128, :])
            up_ready[(j, slot)] = K.inc(i, f"lu{j}", 16)

        def load_dn(q, e, grp, slot):
            i = q.dma_start(out=WD[:, slot * DN_T * 1024:(slot + 1) * DN_T * 1024],
                            in_=w_dn[e * 128:(e + 1) * 128,
                                     grp * DN_T * 1024:(grp + 1) * DN_T * 1024])
            dn_ready[slot] = K.inc(i, "ldn", 16)

        hread = {0: 0, 1: 0}
        oread = {0: 0, 1: 0}
        dv_prev = K.val("dv")

        for c in range(globals()['MOE_NCHUNK']):
            @blk.sync
            def _(q, c=c):
                K.wait(q, "so", K.val("so"))
                K.wait(q, "dv", K.val("dv"))
                for k in range(NDT):
                    i = q.dma_start(out=XC[k][:], in_=xd_in[k * 128:(k + 1) * 128,
                                                           c * TC:(c + 1) * TC])
                    K.inc(i, "lx", 16)
            lx_x = K.val("lx")

            @blk.vector
            def _(v):
                K.wait(v, "lx", lx_x)
            @blk.scalar
            def _(sc):
                K.wait(sc, "lx", lx_x)
            @blk.tensor
            def _(tx):
                K.wait(tx, "lx", lx_x)

            dv_x = emit_ln_x(K, XC, slice(0, TC), ACC, XHB, G1, P, PS_SUM[:], PS_SS[:],
                             MU, PS_MUR[:], PS_RSR[:], dv_prev, ones_sq=P["ONES_F"])

            for tt in range(TC // 128):
                @blk.tensor
                def _(tx, tt=tt):
                    K.wait(tx, "dv", dv_x)
                    K.wait(tx, "ac", setup_done)
                    K.wait(tx, "dv", K.val("dv"))
                    gb1 = cfg.get("gate_bias1")
                    for k in range(NDT):
                        i = tx.matmul(PS_H[0][0:128, tt * E_:(tt + 1) * E_],
                                      XHB[:, k * TC + tt * 128:k * TC + (tt + 1) * 128],
                                      RT[:, k * E_:(k + 1) * E_],
                                      start=(k == 0),
                                      stop=(k == NDT - 1 and gb1 is None))
                    if gb1 is not None:
                        i = tx.matmul(PS_H[0][0:128, tt * E_:(tt + 1) * E_],
                                      P["ONES_R"][:], gb1[:], start=False, stop=True)
                    K.inc(i, "mm")
                mmv = K.val("mm")

                @blk.vector
                def _(v, tt=tt, mmv=mmv):
                    K.wait(v, "mm", mmv)
                    i = v.tensor_copy(LG[:, tt * E_:(tt + 1) * E_],
                                      PS_H[0][0:128, tt * E_:(tt + 1) * E_])
                    K.inc(i, "dv")
            dv_lg = K.val("dv")

            if cfg.get("pre_gate"):
                dv_lg = cfg["pre_gate"](K, LG, SCR, MU, dv_lg)

            ac_gt = emit_gating(K, LG, GT, GTB, SCR, PS_RSR[:], P["IDF"], P["ONES_R"],
                                dv_lg, E_, "l1" if var == "l1" else "top2", scale,
                                keep_fn=cfg.get("keep_fn"))
            if c == 0 and globals().get("DEV_DBG") is not None:
                dbg = globals()["DEV_DBG"]
                acn = K.val("ac")
                @blk.sync
                def _(q, acn=acn):
                    K.wait(q, "ac", acn)
                    K.wait(q, "dv", K.val("dv"))
                    i = q.dma_start(out=dbg["gt"], in_=GT[:])
                    K.inc(i, "so", 16)
                    i = q.dma_start(out=dbg["gtb"], in_=GTB[:])
                    K.inc(i, "so", 16)
                    i = q.dma_start(out=dbg["lg"], in_=LG[:])
                    K.inc(i, "so", 16)
                    i = q.dma_start(out=dbg["scr"], in_=SCR[:])
                    K.inc(i, "so", 16)
            if globals().get("DEV_CONST_GATE"):
                @blk.vector
                def _(v):
                    K.wait(v, "ac", ac_gt)
                    v.memset(GT[:], 0.0625)
                    i = v.memset(GTB[:], 0.0625)
                    K.inc(i, "dv")
                dvg = K.val("dv")
                @blk.tensor
                def _(tx, dvg=dvg):
                    K.wait(tx, "dv", dvg)

            @blk.sync
            def _(q):
                K.wait(q, "mm", K.val("mm"))
                for j in range(nup):
                    load_up(q, j, 0, 0, 0)
                load_dn(q, 0, 0, 0)

            for e in range(E_):
                @blk.tensor
                def _(tx, e=e):
                    K.wait(tx, "ac", ac_gt)
                    K.wait(tx, "dv", K.val("dv"))
                    i = tx.matmul(PS_MUR[:], P["ONES_R"][:], GT[0:1, e * TC:(e + 1) * TC], start=True, stop=True)
                    K.inc(i, "mm")
                mm_g = K.val("mm")

                for grp in range(NGRP):
                    for oi in range(HGRP):
                        o = grp * HGRP + oi
                        piece, opos = o // 4, o % 4
                        uslot = piece % 2
                        hb = oi % 2
                        for j in range(nup):
                            @blk.tensor
                            def _(tx, j=j, o=o, piece=piece, opos=opos, uslot=uslot, hb=hb):
                                K.wait(tx, f"lu{j}", up_ready[(j, uslot)])
                                if j == 0 and hread[hb] > 0:
                                    K.wait(tx, "dv", hread[hb])
                                if j == 0 and hb == 0:
                                    K.wait(tx, "ac", K.val("ac"))
                                for k in range(NDT):
                                    i = tx.matmul(
                                        PS_H[hb][:] if nup == 1 else PS_H[j][:],
                                        WU[j][:, uslot * NDT * 512 + k * 512 + opos * 128:
                                              uslot * NDT * 512 + k * 512 + (opos + 1) * 128],
                                        XHB[:, k * TC:(k + 1) * TC],
                                        start=(k == 0), stop=(k == NDT - 1))
                                K.inc(i, "mm")
                        mm_h = K.val("mm")

                        a = ACT[:, oi * TC:(oi + 1) * TC]
                        if var == "l1":
                            @blk.vector
                            def _(v, e=e, o=o, hb=hb, mm_h=mm_h):
                                K.wait(v, "mm", mm_h)
                                v.tensor_scalar(a, PS_H[hb][:],
                                                cfg["b_up"][0][:, e * NH + o:e * NH + o + 1],
                                                0.0, ALU.add, ALU.max)
                                i = v.tensor_tensor(a, a, PS_MUR[:], ALU.mult)
                                hread[hb] = K.inc(i, "dv")
                        else:
                            sidx = 1 if var == "l2" else 0
                            oth = 1 - sidx
                            sp = oi % 2
                            s0 = S0[:, sp * TC:(sp + 1) * TC]
                            @blk.scalar
                            def _(sc, e=e, o=o, s0=s0, mm_h=mm_h):
                                K.wait(sc, "mm", mm_h)
                                if var == "l2":
                                    i = sc.activation(s0, PS_H[sidx][:], AF.Silu,
                                                      bias=cfg["b_up"][1][:, e * NH + o:e * NH + o + 1])
                                else:
                                    i = sc.activation(s0, PS_H[sidx][:], AF.Silu)
                                K.inc(i, "ac")
                            ac_s = K.val("ac")

                            @blk.vector
                            def _(v, e=e, o=o, oth=oth, s0=s0, ac_s=ac_s, mm_h=mm_h):
                                K.wait(v, "mm", mm_h)
                                K.wait(v, "ac", ac_s)
                                if var == "l2":
                                    v.scalar_tensor_tensor(a, PS_H[oth][:],
                                                           cfg["b_up"][0][:, e * NH + o:e * NH + o + 1],
                                                           s0, ALU.add, ALU.mult)
                                else:
                                    v.tensor_tensor(a, PS_H[oth][:], s0, ALU.mult)
                                i = v.tensor_tensor(a, a, PS_MUR[:], ALU.mult)
                                hread[0] = hread[1] = K.inc(i, "dv")
                        if opos == 3:
                            nxt = piece + 1
                            ne, npi = (e, nxt) if nxt < NPU else ((e + 1) % E_, 0)
                            if not (c == globals()['MOE_NCHUNK'] - 1 and e == E_ - 1 and nxt >= NPU):
                                mm_now = K.val("mm")
                                @blk.sync
                                def _(q, ne=ne, npi=npi, mm_now=mm_now):
                                    K.wait(q, "mm", mm_now)
                                    for j in range(nup):
                                        load_up(q, j, ne, npi, npi % 2)
                    dv_act = K.val("dv")

                    dslot = grp % 2
                    for o in range(NDT):
                        ob = o % 2
                        @blk.tensor
                        def _(tx, e=e, grp=grp, o=o, ob=ob, dslot=dslot, dv_act=dv_act):
                            K.wait(tx, "ldn", dn_ready[dslot])
                            K.wait(tx, "dv", dv_act)
                            if oread[ob] > 0:
                                K.wait(tx, "dv", oread[ob])
                            with_b2 = cfg.get("b_dn") is not None and grp == 0 and e == 0
                            for kk in range(HGRP):
                                i = tx.matmul(PS_O[ob][:],
                                              WD[:, dslot * DN_T * 1024 + kk * 1024 + o * 128:
                                                 dslot * DN_T * 1024 + kk * 1024 + (o + 1) * 128],
                                              ACT[:, kk * TC:(kk + 1) * TC],
                                              start=(kk == 0),
                                              stop=(kk == HGRP - 1 and not with_b2))
                            if with_b2:
                                i = tx.matmul(PS_O[ob][:],
                                              cfg["b_dn"][0:E_, o * 128:(o + 1) * 128],
                                              GTB[0:E_, :], start=False, stop=True)
                            K.inc(i, "mm")
                        mm_o = K.val("mm")

                        @blk.vector
                        def _(v, e=e, grp=grp, o=o, ob=ob, mm_o=mm_o):
                            K.wait(v, "mm", mm_o)
                            dst = ACC[:, o * TC:(o + 1) * TC]
                            if e == 0 and grp == 0:
                                i = v.tensor_copy(dst, PS_O[ob][:])
                            else:
                                i = v.tensor_tensor(dst, dst, PS_O[ob][:], ALU.add)
                            oread[ob] = K.inc(i, "dv")
                    ngrp = grp + 1
                    nde, ng = (e, ngrp) if ngrp < NGRP else ((e + 1) % E_, 0)
                    if not (c == globals()['MOE_NCHUNK'] - 1 and e == E_ - 1 and ngrp >= NGRP):
                        mm_now = K.val("mm")
                        @blk.sync
                        def _(q, nde=nde, ng=ng, mm_now=mm_now):
                            K.wait(q, "mm", mm_now)
                            load_dn(q, nde, ng, ng % 2)

            dv_out = K.val("dv")
            @blk.vector
            def _(v, dv_out=dv_out):
                K.wait(v, "dv", dv_out)
                for k in range(NDT):
                    i = v.tensor_tensor(XC[k][:], XC[k][:], ACC[:, k * TC:(k + 1) * TC],
                                        ALU.add)
                K.inc(i, "dv")
            dv_prev = K.val("dv")

            @blk.sync
            def _(q, c=c, dv_prev=dv_prev):
                K.wait(q, "dv", dv_prev)
                for k in range(NDT):
                    i = q.dma_start(out=xd_out[k * 128:(k + 1) * 128, c * TC:(c + 1) * TC],
                                    in_=XC[k][:])
                    K.inc(i, "so", 16)



def _build_l3ffn_launch():
    nc = bass.Bass("TRN2", target_bir_lowering=False, detect_race_conditions=False)
    xT = nc.dram_tensor("xT", [D, T], F32, kind="ExternalInput").ap()
    wr = nc.dram_tensor("wr", [D, 8], F32, kind="ExternalInput").ap()
    w0 = nc.dram_tensor("w0", [8, D, 2048], F32, kind="ExternalInput").ap()
    w1 = nc.dram_tensor("w1", [8, D, 2048], F32, kind="ExternalInput").ap()
    wo = nc.dram_tensor("wo", [8, 2048, D], F32, kind="ExternalInput").ap()
    idenf = nc.dram_tensor("idenf", [128, 128], F32, kind="ExternalInput").ap()
    yT = nc.dram_tensor("yT", [D, T], F32, kind="ExternalOutput").ap()
    c0 = nc.dram_tensor("c0", [8 * 4 * 128, NDT * 512], BF16, kind="Internal").ap()
    c1 = nc.dram_tensor("c1", [8 * 4 * 128, NDT * 512], BF16, kind="Internal").ap()
    cdn = nc.dram_tensor("cdn", [8 * 128, 16 * 1024], BF16, kind="Internal").ap()
    if globals().get("DEV_DBG") is not None:
        globals()["DEV_DBG"] = {
            "gt": nc.dram_tensor("dbg_gt", [1, 8 * TC], F32, kind="ExternalOutput").ap(),
            "gtb": nc.dram_tensor("dbg_gtb", [8, TC], BF16, kind="ExternalOutput").ap(),
            "lg": nc.dram_tensor("dbg_lg", [128, 4 * 8], F32, kind="ExternalOutput").ap(),
            "scr": nc.dram_tensor("dbg_scr", [128, 4 * 30], F32, kind="ExternalOutput").ap(),
        }

    with ExitStack() as stack:
        blk = stack.enter_context(nc.Block())
        K = Ctx(nc, blk, stack)
        for s_ in ("lx", "lw", "lu0", "lu1", "ldn", "lqw", "lqs", "mm", "dv", "ac", "so"):
            K.sem(s_)
        P0 = dict(
            ONES_F=K.sb("p_ONESF", [128, 1], F32),
            ONES_B=K.sb("p_ONESB", [128, 1], BF16),
            ONES_R=K.sb("p_ONESR", [1, 128], F32),
            IDF=K.sb("p_IDF", [128, 128], F32),
        )

        @blk.vector
        def _(v):
            v.memset(P0["ONES_F"][:], 1.0)
            v.memset(P0["ONES_B"][:], 1.0)
            i = v.memset(P0["ONES_R"][:], 1.0)
            K.inc(i, "dv")

        @blk.sync
        def _(q):
            i = q.dma_start(out=P0["IDF"][:], in_=idenf)
            K.inc(i, "lx", 16)

        emit_repack_up(K, w0, c0, 8, 2048)
        emit_repack_up(K, w1, c1, 8, 2048)
        emit_repack_dn(K, wo, cdn, 8, 2048, 1024)
        emit_fence(K)

        @blk.sync
        def _(q):
            K.wait(q, "lqs", K.val("lqs"))

        def setup(K2, RT):
            @blk.gpsimd
            def _(g):
                for k in range(NDT):
                    i = g.dma_start(out=RT[:, k * 8:(k + 1) * 8],
                                    in_=wr[k * 128:(k + 1) * 128, :])
                    K.inc(i, "lw", 16)
            lwv = K.val("lw")

            @blk.tensor
            def _(tx):
                K.wait(tx, "lw", lwv)
            @blk.vector
            def _(v):
                K.wait(v, "lw", lwv)
                K.wait(v, "lx", 16)
            return K.val("ac")

        cfg = dict(E=8, H=2048, var="l3", w_up=[c0, c1], w_dn=cdn, setup=setup, scale=0.5)
        emit_moe(K, xT, yT, P0, cfg)

        @blk.sync
        def _(q):
            K.wait(q, "so", K.val("so"))
    return nc


def dev_run_l3ffn(x, inp):
    nc = _build_l3ffn_launch()
    idenf = np.eye(128).astype(np.float32)
    in_maps = []
    for b in range(NCORES):
        in_maps.append({
            "xT": np.ascontiguousarray(x[b].T),
            "wr": np.asarray(inp["L3_ffn_w_router"]),
            "w0": np.asarray(inp["L3_ffn_w0"]),
            "w1": np.asarray(inp["L3_ffn_w1"]),
            "wo": np.asarray(inp["L3_ffn_wo"]),
            "idenf": idenf,
        })
    res = run_bass_kernel_spmd(nc, in_maps, core_ids=list(range(NCORES)))
    return np.stack([np.asarray(res.results[b]["yT"], np.float32).T for b in range(NCORES)])


def _build_l2ffn_launch():
    nc = bass.Bass("TRN2", target_bir_lowering=False, detect_race_conditions=False)
    xT = nc.dram_tensor("xT", [D, T], F32, kind="ExternalInput").ap()
    gw = nc.dram_tensor("gw", [8, D], F32, kind="ExternalInput").ap()
    gb = nc.dram_tensor("gb", [8], F32, kind="ExternalInput").ap()
    w0 = nc.dram_tensor("w0", [8, 2048, D], F32, kind="ExternalInput").ap()
    b0 = nc.dram_tensor("b0", [8, 2048], F32, kind="ExternalInput").ap()
    w1 = nc.dram_tensor("w1", [8, 2048, D], F32, kind="ExternalInput").ap()
    b1 = nc.dram_tensor("b1", [8, 2048], F32, kind="ExternalInput").ap()
    w2 = nc.dram_tensor("w2", [8, D, 2048], F32, kind="ExternalInput").ap()
    b2 = nc.dram_tensor("b2", [8, D], F32, kind="ExternalInput").ap()
    idenf = nc.dram_tensor("idenf", [128, 128], F32, kind="ExternalInput").ap()
    idenb = nc.dram_tensor("idenb", [128, 128], BF16, kind="ExternalInput").ap()
    yT = nc.dram_tensor("yT", [D, T], F32, kind="ExternalOutput").ap()
    c0 = nc.dram_tensor("c0", [8 * 4 * 128, NDT * 512], BF16, kind="Internal").ap()
    c1 = nc.dram_tensor("c1", [8 * 4 * 128, NDT * 512], BF16, kind="Internal").ap()
    cdn = nc.dram_tensor("cdn", [8 * 128, 16 * 1024], BF16, kind="Internal").ap()

    with ExitStack() as stack:
        blk = stack.enter_context(nc.Block())
        K = Ctx(nc, blk, stack)
        for s_ in ("lx", "lw", "lu0", "lu1", "ldn", "lqw", "lqs", "mm", "dv", "ac", "so"):
            K.sem(s_)
        P0 = dict(
            ONES_F=K.sb("p_ONESF", [128, 1], F32),
            ONES_B=K.sb("p_ONESB", [128, 1], BF16),
            ONES_R=K.sb("p_ONESR", [1, 128], F32),
            IDF=K.sb("p_IDF", [128, 128], F32),
        )
        IDB = K.sb("p_IDB", [128, 128], BF16)
        B0S = K.sb("p_B0S", [128, 128], F32)
        B1S = K.sb("p_B1S", [128, 128], F32)
        B2S = K.sb("p_B2S", [8, D], BF16)
        GWS = K.sb("p_GWS", [128, NDT * 8], BF16)
        GB1 = K.sb("p_GB1", [1, 8], F32)

        @blk.vector
        def _(v):
            v.memset(P0["ONES_F"][:], 1.0)
            v.memset(P0["ONES_B"][:], 1.0)
            i = v.memset(P0["ONES_R"][:], 1.0)
            K.inc(i, "dv")

        @blk.sync
        def _(q):
            i = q.dma_start(out=P0["IDF"][:], in_=idenf)
            K.inc(i, "lx", 16)
            i = q.dma_start(out=IDB[:], in_=idenb)
            K.inc(i, "lx", 16)

        @blk.gpsimd
        def _(g):
            for k in range(NDT):
                i = g.dma_start(out=GWS[:, k * 8:(k + 1) * 8],
                                in_=bass.AP(tensor=gw.tensor, offset=k * 128,
                                            ap=[[1, 128], [1024, 8]]),
                                allow_slow_non_contiguous=True)
                K.inc(i, "lw", 16)
            i = g.dma_start(out=B2S[:], in_=b2)
            K.inc(i, "lw", 16)
            i = g.dma_start(out=GB1[:], in_=bass.AP(tensor=gb.tensor, offset=0,
                                                    ap=[[8, 1], [1, 8]]))
            K.inc(i, "lw", 16)

        with ExitStack() as SS:
            En = SS.enter_context
            BST = En(nc.sbuf_tensor("p_BST", [128, 256], F32))
            PTB = En(nc.psum_tensor("p_PTB", [128, 256], F32))

            @blk.sync
            def _(q):
                i = q.dma_start(out=BST[:, 0:128], in_=b0.rearrange("e (o p) -> (e o) p", p=128))
                K.inc(i, "lx", 16)
                i = q.dma_start(out=BST[:, 128:256], in_=b1.rearrange("e (o p) -> (e o) p", p=128))
                K.inc(i, "lx", 16)
            lxv = K.val("lx")

            @blk.tensor
            def _(tx):
                K.wait(tx, "lx", lxv)
                tx.transpose(PTB[:, 0:128], BST[:, 0:128], P0["IDF"][:])
                i = tx.transpose(PTB[:, 128:256], BST[:, 128:256], P0["IDF"][:])
                K.inc(i, "mm")
            mmv = K.val("mm")

            @blk.scalar
            def _(sc):
                K.wait(sc, "mm", mmv)
                sc.copy(B0S[:], PTB[:, 0:128])
                i = sc.copy(B1S[:], PTB[:, 128:256])
                K.inc(i, "ac")

            emit_repack_tr(K, w0, c0, range(8), 2048, IDB)
            emit_repack_tr(K, w1, c1, range(8), 2048, IDB)
            emit_repack_tr_dn(K, w2, cdn, 8, 2048, IDB)
        emit_fence(K)

        @blk.sync
        def _(q):
            K.wait(q, "lqs", K.val("lqs"))

        def setup(K2, RT):
            lwv = K.val("lw")
            acv = K.val("ac")
            @blk.vector
            def _(v):
                K.wait(v, "lw", lwv)
                K.wait(v, "ac", acv)
                i = v.tensor_copy(RT[:], GWS[:])
                K.inc(i, "dv")
            @blk.tensor
            def _(tx):
                K.wait(tx, "dv", K.val("dv"))
            return K.val("ac")

        cfg = dict(E=8, H=2048, var="l2", w_up=[c0, c1], w_dn=cdn, setup=setup, scale=0.5,
                   b_up=[B0S, B1S], b_dn=B2S, gate_bias1=GB1)
        emit_moe(K, xT, yT, P0, cfg)

        @blk.sync
        def _(q):
            K.wait(q, "so", K.val("so"))
    return nc


def emit_repack_tr_dn(K, raw, cache, E_, H_, ident_bf):
    nc, blk = K.nc, K.blk
    NO = D // 128
    NI = H_ // 128
    _RPC[0] += 1
    uid = _RPC[0]
    with ExitStack() as SS:
        En = SS.enter_context
        ST = [En(nc.sbuf_tensor(f"rtd{uid}_st{i}", [128, H_], BF16)) for i in range(2)]
        PT = [En(nc.psum_tensor(f"rtd{uid}_pt{i}", [128, 128], BF16)) for i in range(2)]
        EV = En(nc.sbuf_tensor(f"rtd{uid}_ev", [128, 128], BF16))
        ld_done = {}
        st_done = {}
        pt_free = {0: 0, 1: 0}
        for e in range(E_):
            for ko in range(NO):
                idx = e * NO + ko
                p = idx % 2
                @blk.gpsimd
                def _(g, e=e, ko=ko, p=p, idx=idx):
                    if idx >= 2:
                        K.wait(g, "lqs", st_done[idx - 2])
                    i = g.dma_start(out=ST[p][:], in_=raw[e, ko * 128:(ko + 1) * 128, :])
                    ld_done[idx] = K.inc(i, "lqw", 16)

                for k in range(NI):
                    @blk.tensor
                    def _(tx, p=p, k=k, idx=idx):
                        if k == 0:
                            K.wait(tx, "lqw", ld_done[idx])
                        if pt_free[k % 2] > 0:
                            K.wait(tx, "ac", pt_free[k % 2])
                        i = tx.transpose(PT[k % 2][:], ST[p][:, k * 128:(k + 1) * 128],
                                         ident_bf[:])
                        K.inc(i, "mm")
                    mmv = K.val("mm")

                    @blk.scalar
                    def _(sc, k=k, mmv=mmv):
                        K.wait(sc, "mm", mmv)
                        K.wait(sc, "lqs", K.val("lqs"))
                        i = sc.copy(EV[:], PT[k % 2][:])
                        pt_free[k % 2] = K.inc(i, "ac")
                    acv = K.val("ac")

                    @blk.sync
                    def _(q, e=e, ko=ko, k=k, acv=acv, idx=idx):
                        K.wait(q, "ac", acv)
                        i = q.dma_start(
                            out=cache[e * 128:(e + 1) * 128,
                                      k * D + ko * 128:k * D + (ko + 1) * 128],
                            in_=EV[:])
                        st_done[idx] = K.inc(i, "lqs", 16)


def dev_run_l2ffn(x, inp):
    nc = _build_l2ffn_launch()
    in_maps = []
    for b in range(NCORES):
        in_maps.append({
            "xT": np.ascontiguousarray(x[b].T),
            "gw": np.asarray(inp["L2_ffn_gate_w"]), "gb": np.asarray(inp["L2_ffn_gate_b"]),
            "w0": np.asarray(inp["L2_ffn_w0"]), "b0": np.asarray(inp["L2_ffn_b0"]),
            "w1": np.asarray(inp["L2_ffn_w1"]), "b1": np.asarray(inp["L2_ffn_b1"]),
            "w2": np.asarray(inp["L2_ffn_w2"]), "b2": np.asarray(inp["L2_ffn_b2"]),
            "idenf": np.eye(128, dtype=np.float32),
            "idenb": np.eye(128).astype(ml_dtypes.bfloat16),
        })
    res = run_bass_kernel_spmd(nc, in_maps, core_ids=list(range(NCORES)))
    return np.stack([np.asarray(res.results[b]["yT"], np.float32).T for b in range(NCORES)])



def _build_l1ffn_launch():
    nc = bass.Bass("TRN2", target_bir_lowering=False, detect_race_conditions=False)
    xT = nc.dram_tensor("xT", [D, T], F32, kind="ExternalInput").ap()
    wr = nc.dram_tensor("wr", [D, 8], F32, kind="ExternalInput").ap()
    br = nc.dram_tensor("br", [8], F32, kind="ExternalInput").ap()
    w1 = nc.dram_tensor("w1", [8, D, 4096], F32, kind="ExternalInput").ap()
    b1 = nc.dram_tensor("b1", [8, 4096], F32, kind="ExternalInput").ap()
    w2 = nc.dram_tensor("w2", [8, 4096, D], F32, kind="ExternalInput").ap()
    b2 = nc.dram_tensor("b2", [8, D], F32, kind="ExternalInput").ap()
    idenf = nc.dram_tensor("idenf", [128, 128], F32, kind="ExternalInput").ap()
    yT = nc.dram_tensor("yT", [D, T], F32, kind="ExternalOutput").ap()
    c0 = nc.dram_tensor("c0", [8 * 8 * 128, NDT * 512], BF16, kind="Internal").ap()
    cdn = nc.dram_tensor("cdn", [8 * 128, 32 * 1024], BF16, kind="Internal").ap()

    with ExitStack() as stack:
        blk = stack.enter_context(nc.Block())
        K = Ctx(nc, blk, stack)
        for s_ in ("lx", "lw", "lu0", "ldn", "lqw", "lqs", "mm", "dv", "ac", "so"):
            K.sem(s_)
        P0 = dict(
            ONES_F=K.sb("p_ONESF", [128, 1], F32),
            ONES_B=K.sb("p_ONESB", [128, 1], BF16),
            ONES_R=K.sb("p_ONESR", [1, 128], F32),
            IDF=K.sb("p_IDF", [128, 128], F32),
        )
        B1S = K.sb("p_B1S", [128, 256], F32)
        B2S = K.sb("p_B2S", [8, D], BF16)
        GB1 = K.sb("p_GB1", [1, 8], F32)
        ONE8 = K.sb("p_ONE8", [8, 1], F32)
        EQT = K.sb("cap_EQT", [8, 2 * TC], F32)
        POS = K.sb("cap_POS", [8, 2 * TC], F32)
        CAR = K.sb("cap_CAR", [8, 2], F32)
        KV = K.sb("cap_KV", [1, 2 * TC], F32)
        KV2 = K.sb("cap_KV2", [1, 2 * TC], F32)
        K0T = K.sb("cap_K0T", [128, 2 * (TC // 128)], F32)

        @blk.vector
        def _(v):
            v.memset(P0["ONES_F"][:], 1.0)
            v.memset(P0["ONES_B"][:], 1.0)
            v.memset(P0["ONES_R"][:], 1.0)
            v.memset(ONE8[:], 1.0)
            i = v.memset(CAR[:], 0.0)
            K.inc(i, "dv")

        @blk.sync
        def _(q):
            i = q.dma_start(out=P0["IDF"][:], in_=idenf)
            K.inc(i, "lx", 16)

        @blk.gpsimd
        def _(g):
            i = g.dma_start(out=B2S[:], in_=b2)
            K.inc(i, "lw", 16)
            i = g.dma_start(out=GB1[:], in_=bass.AP(tensor=br.tensor, offset=0,
                                                    ap=[[8, 1], [1, 8]]))
            K.inc(i, "lw", 16)

        with ExitStack() as SS:
            En = SS.enter_context
            BST = En(nc.sbuf_tensor("p_BST", [128, 256], F32))
            PTB = En(nc.psum_tensor("p_PTB", [128, 256], F32))

            @blk.sync
            def _(q):
                src = b1.rearrange("e (o p) -> (e o) p", p=128)
                i = q.dma_start(out=BST[:, 0:128], in_=src[0:128, :])
                K.inc(i, "lx", 16)
                i = q.dma_start(out=BST[:, 128:256], in_=src[128:256, :])
                K.inc(i, "lx", 16)
            lxv = K.val("lx")

            @blk.tensor
            def _(tx):
                K.wait(tx, "lx", lxv)
                tx.transpose(PTB[:, 0:128], BST[:, 0:128], P0["IDF"][:])
                i = tx.transpose(PTB[:, 128:256], BST[:, 128:256], P0["IDF"][:])
                K.inc(i, "mm")
            mmv = K.val("mm")

            @blk.scalar
            def _(sc):
                K.wait(sc, "mm", mmv)
                sc.copy(B1S[:, 0:128], PTB[:, 0:128])
                i = sc.copy(B1S[:, 128:256], PTB[:, 128:256])
                K.inc(i, "ac")

            emit_repack_up(K, w1, c0, 8, 4096)
            emit_repack_dn(K, w2, cdn, 8, 4096, 1024)
        emit_fence(K)

        @blk.sync
        def _(q):
            K.wait(q, "lqs", K.val("lqs"))

        def setup(K2, RT):
            @blk.gpsimd
            def _(g):
                for k in range(NDT):
                    i = g.dma_start(out=RT[:, k * 8:(k + 1) * 8],
                                    in_=wr[k * 128:(k + 1) * 128, :])
                    K.inc(i, "lw", 16)
            lwv = K.val("lw")
            @blk.tensor
            def _(tx):
                K.wait(tx, "lw", lwv)
            @blk.vector
            def _(v):
                K.wait(v, "lw", lwv)
                K.wait(v, "lx", 16)
            return K.val("ac")

        def pre_gate(K2, LG, SCR, MU, dv_lg):
            NT_ = TC // 128
            NE = NT_ * 8
            mx = SCR[:, 3 * NE + 4 * NT_:3 * NE + 5 * NT_]
            sm = SCR[:, 3 * NE + 5 * NT_:3 * NE + 6 * NT_]

            @blk.vector
            def _(v):
                K.wait(v, "dv", dv_lg)
                for tt in range(NT_):
                    i = v.tensor_reduce(mx[:, tt:tt + 1], LG[:, tt * 8:(tt + 1) * 8],
                                        AX.X, ALU.max)
                K.inc(i, "dv")
                K.wait(v, "dv")
                for tt in range(NT_):
                    i = v.tensor_scalar(mx[:, tt:tt + 1], mx[:, tt:tt + 1], -1.0, None, ALU.mult)
                K.inc(i, "dv")
            dv1 = K.val("dv")

            @blk.scalar
            def _(sc):
                K.wait(sc, "dv", dv1)
                for tt in range(NT_):
                    i = sc.activation(LG[:, tt * 8:(tt + 1) * 8], LG[:, tt * 8:(tt + 1) * 8],
                                      AF.Exp, bias=mx[:, tt:tt + 1])
                K.inc(i, "ac")
            ac1 = K.val("ac")

            @blk.vector
            def _(v):
                K.wait(v, "ac", ac1)
                for tt in range(NT_):
                    i = v.tensor_reduce(sm[:, tt:tt + 1], LG[:, tt * 8:(tt + 1) * 8],
                                        AX.X, ALU.add)
                K.inc(i, "dv")
                K.wait(v, "dv")
                for tt in range(NT_):
                    i = v.reciprocal(sm[:, tt:tt + 1], sm[:, tt:tt + 1])
                K.inc(i, "dv")
                K.wait(v, "dv")
                for tt in range(NT_):
                    i = v.tensor_scalar(LG[:, tt * 8:(tt + 1) * 8], LG[:, tt * 8:(tt + 1) * 8],
                                        sm[:, tt:tt + 1], None, ALU.mult)
                K.inc(i, "dv")
            return K.val("dv")


        def keep_fn(K2, eqs, eq2s, dv2):
            NT_ = TC // 128
            for tt in range(NT_):
                for s_i, buf in ((0, eqs), (1, eq2s)):
                    @blk.tensor
                    def _(tx, tt=tt, s_i=s_i, buf=buf):
                        K.wait(tx, "dv", dv2)
                        K.wait(tx, "ac", K.val("ac"))
                        i = tx.transpose(K.moe_ps_scratch[0:8, 0:128], buf[:, tt * 8:(tt + 1) * 8],
                                         P0["IDF"][:])
                        K.inc(i, "mm")
                    mmv = K.val("mm")

                    @blk.scalar
                    def _(sc, tt=tt, s_i=s_i, mmv=mmv):
                        K.wait(sc, "mm", mmv)
                        dst = bass.AP(tensor=EQT.ap().tensor, offset=tt * 256 + s_i,
                                      ap=[[2 * TC, 8], [2, 128]])
                        i = sc.copy(dst, K.moe_ps_scratch[0:8, 0:128])
                        K.inc(i, "ac")
            ac_eqt = K.val("ac")

            @blk.vector
            def _(v):
                K.wait(v, "ac", ac_eqt)
                i = v.tensor_tensor_scan(POS[:], EQT[:].broadcast_to((8, 2 * TC)) if False
                                         else bass.AP(tensor=P0["ONES_F"].ap().tensor,
                                                      offset=0, ap=[[1, 8], [0, 2 * TC]]),
                                         EQT[:], CAR[:, 0:1], ALU.mult, ALU.add)
                K.inc(i, "dv")
                K.wait(v, "dv")
                v.tensor_copy(CAR[:, 0:1], POS[:, 2 * TC - 1:2 * TC])
                i = v.tensor_tensor(POS[:], POS[:], EQT[:], ALU.subtract)
                K.inc(i, "dv")
                K.wait(v, "dv")
                i = v.tensor_scalar(POS[:], POS[:], 640.0, None, ALU.is_lt)
                K.inc(i, "dv")
                K.wait(v, "dv")
                i = v.tensor_tensor(POS[:], POS[:], EQT[:], ALU.mult)
                K.inc(i, "dv")
            dv_k = K.val("dv")

            @blk.tensor
            def _(tx):
                K.wait(tx, "dv", dv_k)
                K.wait(tx, "dv", K.val("dv"))
                for cc in range(2 * TC // 512):
                    i = tx.matmul(K.moe_ps_scratch[0:1, 0:512] if False else K.moe_ps_scratch[0:1, 0:512],
                                  ONE8[:], POS[:, cc * 512:(cc + 1) * 512],
                                  start=True, stop=True)
                    K.inc(i, "mm")
            return None
        def keep_fn2(K2, eqs, eq2s, dv2):
            NT_ = TC // 128
            for tt in range(NT_):
                for s_i, buf in ((0, eqs), (1, eq2s)):
                    @blk.tensor
                    def _(tx, tt=tt, s_i=s_i, buf=buf):
                        K.wait(tx, "dv", dv2)
                        K.wait(tx, "ac", K.val("ac"))
                        i = tx.transpose(K.moe_ps_scratch[0:8, 0:128], buf[:, tt * 8:(tt + 1) * 8],
                                         P0["IDF"][:])
                        K.inc(i, "mm")
                    mmv = K.val("mm")

                    @blk.scalar
                    def _(sc, tt=tt, s_i=s_i, mmv=mmv):
                        K.wait(sc, "mm", mmv)
                        dst = bass.AP(tensor=EQT.ap().tensor, offset=tt * 256 + s_i,
                                      ap=[[2 * TC, 8], [2, 128]])
                        i = sc.copy(dst, K.moe_ps_scratch[0:8, 0:128])
                        K.inc(i, "ac")
            ac_eqt = K.val("ac")

            @blk.vector
            def _(v):
                K.wait(v, "ac", ac_eqt)
                ones_b = bass.AP(tensor=P0["ONES_F"].ap().tensor, offset=0,
                                 ap=[[1, 8], [0, 2 * TC]])
                i = v.tensor_tensor_scan(POS[:], ones_b, EQT[:], CAR[:, 0:1],
                                         ALU.mult, ALU.add)
                K.inc(i, "dv")
                K.wait(v, "dv")
                v.tensor_copy(CAR[:, 0:1], POS[:, 2 * TC - 1:2 * TC])
                i = v.tensor_tensor(POS[:], POS[:], EQT[:], ALU.subtract)
                K.inc(i, "dv")
                K.wait(v, "dv")
                i = v.tensor_scalar(POS[:], POS[:], 640.0, None, ALU.is_lt)
                K.inc(i, "dv")
                K.wait(v, "dv")
                i = v.tensor_tensor(POS[:], POS[:], EQT[:], ALU.mult)
                K.inc(i, "dv")
            dv_k = K.val("dv")

            kv_done = []
            for cc in range(2 * TC // 512):
                @blk.tensor
                def _(tx, cc=cc):
                    K.wait(tx, "dv", dv_k)
                    K.wait(tx, "dv", K.val("dv"))
                    i = tx.matmul(K.moe_ps_scratch[0:1, 0:512], ONE8[:], POS[:, cc * 512:(cc + 1) * 512],
                                  start=True, stop=True)
                    K.inc(i, "mm")
                mmv = K.val("mm")

                @blk.vector
                def _(v, cc=cc, mmv=mmv):
                    K.wait(v, "mm", mmv)
                    i = v.tensor_copy(KV[:, cc * 512:(cc + 1) * 512], K.moe_ps_scratch[0:1, 0:512])
                    K.inc(i, "dv")
            dv_kv = K.val("dv")

            NT2 = TC // 128
            @blk.vector
            def _(v):
                K.wait(v, "dv", dv_kv)
                v.tensor_copy(KV2[0:1, 0:TC],
                              bass.AP(tensor=KV.ap().tensor, offset=0, ap=[[2 * TC, 1], [2, TC]]))
                i = v.tensor_copy(KV2[0:1, TC:2 * TC],
                                  bass.AP(tensor=KV.ap().tensor, offset=1, ap=[[2 * TC, 1], [2, TC]]))
                K.inc(i, "dv")
            dv_kv2 = K.val("dv")
            for tt in range(NT2):
                for s_i in range(2):
                    @blk.tensor
                    def _(tx, tt=tt, s_i=s_i):
                        K.wait(tx, "dv", dv_kv2)
                        K.wait(tx, "ac", K.val("ac"))
                        lhs = KV2[0:1, s_i * TC + tt * 128:s_i * TC + (tt + 1) * 128]
                        i = tx.matmul(K.moe_ps_scratch[0:128, 128 + s_i * NT2 + tt:129 + s_i * NT2 + tt],
                                      lhs, P0["ONES_R"][0:1, 0:1], start=True, stop=True)
                        K.inc(i, "mm")
                    mmv = K.val("mm")

                    @blk.scalar
                    def _(sc, tt=tt, s_i=s_i, mmv=mmv):
                        K.wait(sc, "mm", mmv)
                        i = sc.copy(K0T[:, s_i * NT2 + tt:s_i * NT2 + tt + 1],
                                    K.moe_ps_scratch[0:128, 128 + s_i * NT2 + tt:129 + s_i * NT2 + tt])
                        K.inc(i, "ac")
            acv = K.val("ac")

            @blk.vector
            def _(v):
                K.wait(v, "ac", acv)
            return K.val("dv"), K0T[:, 0:NT2], K0T[:, NT2:2 * NT2]

        cfg = dict(E=8, H=4096, var="l1", w_up=[c0], w_dn=cdn, setup=setup, scale=0.5,
                   b_up=[B1S], b_dn=B2S, gate_bias1=GB1, pre_gate=pre_gate,
                   keep_fn=keep_fn2)
        emit_moe(K, xT, yT, P0, cfg)

        @blk.sync
        def _(q):
            K.wait(q, "so", K.val("so"))
    return nc


def dev_run_l1ffn(x, inp):
    nc = _build_l1ffn_launch()
    in_maps = []
    for b in range(NCORES):
        in_maps.append({
            "xT": np.ascontiguousarray(x[b].T),
            "wr": np.asarray(inp["L1_ffn_Wr"]), "br": np.asarray(inp["L1_ffn_br"]),
            "w1": np.asarray(inp["L1_ffn_W1"]), "b1": np.asarray(inp["L1_ffn_b1"]),
            "w2": np.asarray(inp["L1_ffn_W2"]), "b2": np.asarray(inp["L1_ffn_b2"]),
            "idenf": np.eye(128, dtype=np.float32),
        })
    res = run_bass_kernel_spmd(nc, in_maps, core_ids=list(range(NCORES)))
    return np.stack([np.asarray(res.results[b]["yT"], np.float32).T for b in range(NCORES)])



def _build_l0attn_launch():
    nc = bass.Bass("TRN2", target_bir_lowering=False, detect_race_conditions=False)
    xT = nc.dram_tensor("xT", [D, T], F32, kind="ExternalInput").ap()
    wq = nc.dram_tensor("wq", [D, D], F32, kind="ExternalInput").ap()
    bq = nc.dram_tensor("bq", [D], F32, kind="ExternalInput").ap()
    wk = nc.dram_tensor("wk", [D, D], F32, kind="ExternalInput").ap()
    bk = nc.dram_tensor("bk", [D], F32, kind="ExternalInput").ap()
    wv = nc.dram_tensor("wv", [D, D], F32, kind="ExternalInput").ap()
    bv = nc.dram_tensor("bv", [D], F32, kind="ExternalInput").ap()
    wo = nc.dram_tensor("wo", [D, D], F32, kind="ExternalInput").ap()
    bo = nc.dram_tensor("bo", [D], F32, kind="ExternalInput").ap()
    cmask = nc.dram_tensor("cmask", [4 * 128, 512], BF16, kind="ExternalInput").ap()
    onesd = nc.dram_tensor("onesd", [1, T], BF16, kind="ExternalInput").ap()
    yT = nc.dram_tensor("yT", [D, T], F32, kind="ExternalOutput").ap()
    aod = nc.dram_tensor("aod", [D, T], BF16, kind="Internal").ap()

    HG = 2
    NQB = T // 512
    NTT = T // 128

    with ExitStack() as stack:
        blk = stack.enter_context(nc.Block())
        K = Ctx(nc, blk, stack)
        for s_ in ("lx", "lw", "lwq", "gp", "mm", "dv", "ac", "so"):
            K.sem(s_)
        ONES_F = K.sb("a_ONESF", [128, 1], F32)
        ONES_B = K.sb("a_ONESB", [128, 1], BF16)
        ONES_R = K.sb("a_ONESR", [1, 128], F32)
        XHB = K.sb("a_XHB", [128, NDT * T], BF16)
        AOS = K.sb("a_AOS", [64, 512], BF16)
        AOC = K.sb("a_AOC", [128, NDT * TC], BF16)
        QH = [K.sb(f"a_QH{i}", [66, T], BF16) for i in range(HG)]
        KH = [K.sb(f"a_KH{i}", [66, T], BF16) for i in range(HG)]
        VT = K.sb("a_VT", [128, NTT * (65 * HG)], BF16)
        WB = K.sb("a_WB", [128, NDT * 128], BF16)
        PTS = K.sb("a_PTS", [128, 512], BF16)
        PVS = K.sb("a_PVS", [65, 512], BF16)
        RL = K.sb("a_RL", [1, 512], F32)
        NRM = K.sb("a_NRM", [1, T], BF16)
        NR2 = K.sb("a_NR2", [1, T], BF16)
        CM = K.sb("a_CM", [128, 4 * 512], BF16)
        XC = K.sb("a_XC", [128, TC], F32)
        MU = K.sb("a_MU", [1, TC], F32)
        G1 = K.sb("a_G1", [128, NDT * TC], F32)
        SQB = K.sb("a_SQB", [128, NDT * TC], BF16)
        BQK = K.sb("a_BQK", [128, 2], F32)
        KMX = K.sb("a_KMX", [1, 1], F32)
        BOS = K.sb("a_BOS", [128, 8], F32)
        BVB = K.sb("a_BVB", [128, 8], BF16)
        WOB = K.sb("a_WOB", [128, NDT * 1024], BF16)
        PS_SUM = stack.enter_context(nc.psum_tensor("a_ps_sum", [1, TC], F32))
        PS_SS = stack.enter_context(nc.psum_tensor("a_ps_ss", [1, TC], F32))
        PS_MUR = stack.enter_context(nc.psum_tensor("a_ps_mur", [128, TC], F32))
        PS_RSR = stack.enter_context(nc.psum_tensor("a_ps_rsr", [128, TC], F32))
        PS_A = [stack.enter_context(nc.psum_tensor(f"a_ps_a{i}", [128, 512], F32)) for i in range(2)]
        PS_B = [stack.enter_context(nc.psum_tensor(f"a_ps_b{i}", [65, 512], F32)) for i in range(2)]
        P0 = dict(ONES_F=ONES_F, ONES_B=ONES_B, ONES_R=ONES_R)
        slopes = [2.0 ** (-0.5 * (i + 1)) for i in range(16)]

        @blk.vector
        def _(v):
            v.memset(ONES_F[:], 1.0)
            v.memset(ONES_B[:], 1.0)
            i = v.memset(ONES_R[:], 1.0)
            K.inc(i, "dv")

        @blk.sync
        def _(q):
            i = q.dma_start(out=CM[:], in_=bass.AP(tensor=cmask.tensor, offset=0,
                                                   ap=[[512, 128], [128 * 512, 4], [1, 512]]))
            K.inc(i, "lx", 16)
            i = q.dma_start(out=BOS[:], in_=bass.AP(tensor=bo.tensor, offset=0,
                                                    ap=[[1, 128], [128, 8]]),
                            allow_slow_non_contiguous=True)
            K.inc(i, "lx", 16)

        @blk.gpsimd
        def _(g):
            i = g.dma_start(out=BVB[:], in_=bass.AP(tensor=bv.tensor, offset=0,
                                                    ap=[[1, 128], [128, 8]]),
                            allow_slow_non_contiguous=True)
            K.inc(i, "lw", 16)

        dv_prev = K.val("dv")
        for c in range(NCHUNK):
            @blk.sync
            def _(q, c=c):
                K.wait(q, "mm", K.val("mm"))
                K.wait(q, "ac", K.val("ac"))
                K.wait(q, "dv", K.val("dv"))
                for k in range(NDT):
                    i = q.dma_start(out=G1[:, k * TC:(k + 1) * TC],
                                    in_=xT[k * 128:(k + 1) * 128, c * TC:(c + 1) * TC])
                    K.inc(i, "lx", 16)
            lxv = K.val("lx")
            for eng in ("vector", "scalar", "tensor"):
                @getattr(blk, eng)
                def _(e):
                    K.wait(e, "lx", lxv)
            XCl = [G1[:, k * TC:(k + 1) * TC] for k in range(NDT)]
            mm_rep = emit_ln_stats(K, XCl, slice(0, TC), ONES_F, SQB, ONES_B,
                                   PS_SUM[:], PS_SS[:], MU, PS_SUM[:],
                                   PS_MUR[:], PS_RSR[:], ONES_R,
                                   dep_data=[("dv", dv_prev)])

            @blk.vector
            def _(v, c=c, mm_rep=mm_rep):
                K.wait(v, "mm", mm_rep)
                for k in range(NDT):
                    dst = XHB[:, k * T + c * TC:k * T + (c + 1) * TC]
                    v.tensor_tensor(XC[:], G1[:, k * TC:(k + 1) * TC], PS_MUR[:], ALU.subtract)
                    i = v.tensor_tensor(dst, XC[:], PS_RSR[:], ALU.mult)
                K.inc(i, "dv")
            dv_prev = K.val("dv")
        dv_ln = K.val("dv")

        psA_free = [None, None]
        psB_free = [None, None]

        for g in range(16 // HG):
            h0 = g * HG
            for j, (wsrc, bsrc, kind) in enumerate(((wq, bq, "q"), (wk, bk, "k"),
                                                    (wv, bv, "v"))):
                mm_cons = K.val("mm")

                @blk.gpsimd
                def _(gq, j=j, wsrc=wsrc, bsrc=bsrc, mm_cons=mm_cons):
                    K.wait(gq, "mm", mm_cons)
                    for k in range(NDT):
                        i = gq.dma_start(
                            out=WB[:, k * 128:k * 128 + 64 * HG],
                            in_=bass.AP(tensor=wsrc.tensor, offset=k * 128 * 1024 + h0 * 64,
                                        ap=[[1024, 128], [1, 64 * HG]]))
                        K.inc(i, "lwq", 16)
                    if kind != "v":
                        i = gq.dma_start(out=BQK[:, j:j + 1],
                                         in_=bass.AP(tensor=bsrc.tensor, offset=h0 * 64,
                                                     ap=[[1, 64 * HG], [1, 1]]))
                        K.inc(i, "lwq", 16)
                lw_v = K.val("lwq")

                if kind != "v":
                    dsts = QH if kind == "q" else KH
                    for tb in range(NQB):
                        par = tb % 2
                        @blk.tensor
                        def _(tx, tb=tb, par=par, lw_v=lw_v):
                            K.wait(tx, "lwq", lw_v)
                            K.wait(tx, "dv", dv_ln)
                            if psA_free[par]:
                                K.wait(tx, psA_free[par][0], psA_free[par][1])
                            for k in range(NDT):
                                i = tx.matmul(PS_A[par][0:64 * HG, :],
                                              WB[:, k * 128:k * 128 + 64 * HG],
                                              XHB[:, k * T + tb * 512:k * T + (tb + 1) * 512],
                                              start=(k == 0), stop=(k == NDT - 1))
                            K.inc(i, "mm")
                        mmv = K.val("mm")

                        @blk.vector
                        def _(v, tb=tb, par=par, mmv=mmv, dsts=dsts, j=j):
                            K.wait(v, "mm", mmv)
                            K.wait(v, "lwq", lw_v)
                            for hh in range(HG):
                                i = v.tensor_scalar(
                                    dsts[hh][0:64, tb * 512:(tb + 1) * 512],
                                    PS_A[par][hh * 64:(hh + 1) * 64, :],
                                    BQK[hh * 64:(hh + 1) * 64, j:j + 1], None, ALU.add)
                            psA_free[par] = ("dv", K.inc(i, "dv"))
                else:
                    for ti in range(NTT):
                        par = ti % 2
                        @blk.tensor
                        def _(tx, ti=ti, par=par, lw_v=lw_v):
                            K.wait(tx, "lwq", lw_v)
                            K.wait(tx, "dv", dv_ln)
                            if psA_free[par]:
                                K.wait(tx, psA_free[par][0], psA_free[par][1])
                            for k in range(NDT):
                                i = tx.matmul(PS_A[par][0:128, 0:64 * HG],
                                              XHB[:, k * T + ti * 128:k * T + (ti + 1) * 128],
                                              WB[:, k * 128:k * 128 + 64 * HG],
                                              start=(k == 0), stop=(k == NDT - 1))
                            K.inc(i, "mm")
                        mmv = K.val("mm")

                        @blk.vector
                        def _(v, ti=ti, par=par, mmv=mmv):
                            K.wait(v, "mm", mmv)
                            for hh in range(HG):
                                i = v.tensor_copy(
                                    VT[:, ti * 65 * HG + hh * 65:ti * 65 * HG + hh * 65 + 64],
                                    PS_A[par][0:128, hh * 64:(hh + 1) * 64])
                            for hh in range(HG):
                                i = v.memset(VT[:, ti * 65 * HG + hh * 65 + 64:
                                                ti * 65 * HG + hh * 65 + 65], 1.0)
                            psA_free[par] = ("dv", K.inc(i, "dv"))
            dv_proj = K.val("dv")

            @blk.gpsimd
            def _(gq, dv_proj=dv_proj):
                K.wait(gq, "dv", dv_proj)
                for hh in range(HG):
                    i = gq.iota(KH[hh][64:65, :], pattern=[[1, T]], base=0,
                                channel_multiplier=0,
                                allow_small_or_imprecise_dtypes=True)
                    K.inc(i, "gp")
            gp_iota = K.val("gp")

            for hh in range(HG):
                @blk.scalar
                def _(sc, hh=hh):
                    K.wait(sc, "dv", dv_proj)
                    K.wait(sc, "mm", K.val("mm"))
                    i = sc.activation(SQB[0:64, 0:T], QH[hh][0:64, :], AF.Square)
                    K.inc(i, "ac")
                ac_sq = K.val("ac")
                for tb in range(NQB):
                    @blk.tensor
                    def _(tx, tb=tb, ac_sq=ac_sq):
                        K.wait(tx, "ac", ac_sq)
                        K.wait(tx, "dv", K.val("dv"))
                        i = tx.matmul(PS_B[0][0:1, :], ONES_B[0:64, :],
                                      SQB[0:64, tb * 512:(tb + 1) * 512], start=True, stop=True)
                        K.inc(i, "mm")
                    mmv = K.val("mm")

                    @blk.vector
                    def _(v, tb=tb, mmv=mmv):
                        K.wait(v, "mm", mmv)
                        i = v.tensor_copy(NRM[:, tb * 512:(tb + 1) * 512], PS_B[0][0:1, :])
                        K.inc(i, "dv")
                dv_q2 = K.val("dv")

                @blk.scalar
                def _(sc, dv_q2=dv_q2):
                    K.wait(sc, "dv", dv_q2)
                    i = sc.activation(NRM[:, :], NRM[:, :], AF.Sqrt)
                    K.inc(i, "ac")
                ac_nq = K.val("ac")

                @blk.scalar
                def _(sc, hh=hh, ac_nq=ac_nq):
                    K.wait(sc, "ac", ac_nq)
                    K.wait(sc, "mm", K.val("mm"))
                    i = sc.activation(SQB[0:64, 0:T], KH[hh][0:64, :], AF.Square)
                    K.inc(i, "ac")
                ac_sk = K.val("ac")
                for tb in range(NQB):
                    @blk.tensor
                    def _(tx, tb=tb, ac_sk=ac_sk):
                        K.wait(tx, "ac", ac_sk)
                        K.wait(tx, "dv", K.val("dv"))
                        i = tx.matmul(PS_B[0][0:1, :], ONES_B[0:64, :],
                                      SQB[0:64, tb * 512:(tb + 1) * 512], start=True, stop=True)
                        K.inc(i, "mm")
                    mmv = K.val("mm")

                    @blk.vector
                    def _(v, tb=tb, mmv=mmv):
                        K.wait(v, "mm", mmv)
                        i = v.tensor_copy(NR2[:, tb * 512:(tb + 1) * 512], PS_B[0][0:1, :])
                        K.inc(i, "dv")
                dv_k2 = K.val("dv")

                @blk.vector
                def _(v, dv_k2=dv_k2):
                    K.wait(v, "dv", dv_k2)
                    i = v.tensor_reduce(NR2[:, 0:1], NR2[:, :], AX.X, ALU.max)
                    K.inc(i, "dv")
                dv_km = K.val("dv")

                @blk.scalar
                def _(sc, dv_km=dv_km):
                    K.wait(sc, "dv", dv_km)
                    i = sc.activation(KMX[:], NR2[:, 0:1], AF.Sqrt)
                    K.inc(i, "ac")
                ac_km = K.val("ac")

                @blk.vector
                def _(v, hh=hh, ac_km=ac_km):
                    K.wait(v, "ac", ac_km)
                    i = v.tensor_scalar(NRM[:, :], NRM[:, :], KMX[:], None, ALU.mult)
                    K.inc(i, "dv")
                    K.wait(v, "dv")
                    i = v.tensor_scalar(NRM[:, :], NRM[:, :], -1.0, -1.0, ALU.mult, ALU.add)
                    K.inc(i, "dv")
                    K.wait(v, "dv")
                    i = v.memset(NR2[:, :], -8.0 * slopes[h0 + hh])
                    K.inc(i, "dv")
                dv_rows = K.val("dv")

                @blk.sync
                def _(q, hh=hh, dv_rows=dv_rows):
                    K.wait(q, "dv", dv_rows)
                    i = q.dma_start(out=QH[hh][65:66, :], in_=NRM[0:1, :])
                    K.inc(i, "lx", 16)
                    i = q.dma_start(out=QH[hh][64:65, :], in_=NR2[0:1, :])
                    K.inc(i, "lx", 16)
                    i = q.dma_start(out=KH[hh][65:66, :], in_=onesd)
                    K.inc(i, "lx", 16)
            lx_rows = K.val("lx")

            for hh in range(HG):
                h = h0 + hh
                for qb in range(NQB):
                    nkt = (qb + 1) * 4
                    bq_ = qb % 2
                    @blk.tensor
                    def _(tx):
                        K.wait(tx, "lx", lx_rows)
                        K.wait(tx, "gp", gp_iota)
                        K.wait(tx, "dv", K.val("dv"))
                        K.wait(tx, "ac", K.val("ac"))
                    for kt in range(nkt):
                        par = kt % 2
                        @blk.tensor
                        def _(tx, hh=hh, qb=qb, kt=kt, par=par):
                            if psA_free[par]:
                                K.wait(tx, psA_free[par][0], psA_free[par][1])
                            i = tx.matmul(PS_A[par][:, :],
                                          KH[hh][:, kt * 128:(kt + 1) * 128],
                                          QH[hh][:, qb * 512:(qb + 1) * 512],
                                          start=True, stop=True)
                            K.inc(i, "mm")
                        mmv = K.val("mm")

                        diag = (kt * 128 >= qb * 512)
                        @blk.scalar
                        def _(sc, kt=kt, par=par, mmv=mmv):
                            K.wait(sc, "mm", mmv)
                            K.wait(sc, "mm", K.val("mm"))
                            i = sc.activation(PTS[:, :], PS_A[par][:, :], AF.Exp, scale=0.125)
                            psA_free[par] = ("ac", K.inc(i, "ac"))
                        acv = K.val("ac")

                        dvv = 0
                        if diag:
                            dk = (kt * 128 - qb * 512) // 128
                            @blk.vector
                            def _(v, dk=dk, acv=acv):
                                K.wait(v, "ac", acv)
                                i = v.tensor_tensor(PTS[:, :], PTS[:, :],
                                                    CM[:, dk * 512:(dk + 1) * 512], ALU.mult)
                                K.inc(i, "dv")
                            dvv = K.val("dv")

                        @blk.tensor
                        def _(tx, hh=hh, kt=kt, nkt=nkt, bq_=bq_, dvv=dvv, acv=acv):
                            if dvv > 0:
                                K.wait(tx, "dv", dvv)
                            K.wait(tx, "ac", acv)
                            if kt == 0 and psB_free[bq_]:
                                K.wait(tx, psB_free[bq_][0], psB_free[bq_][1])
                            i = tx.matmul(PS_B[bq_][:, :],
                                          VT[:, kt * 65 * HG + hh * 65:kt * 65 * HG + (hh + 1) * 65],
                                          PTS[:, :],
                                          start=(kt == 0), stop=(kt == nkt - 1))
                            K.inc(i, "mm")
                    mm_pv = K.val("mm")

                    @blk.scalar
                    def _(sc, bq_=bq_, mm_pv=mm_pv):
                        K.wait(sc, "mm", mm_pv)
                        i = sc.copy(PVS[:], PS_B[bq_][:, :])
                        psB_free[bq_] = ("ac", K.inc(i, "ac"))
                    ac_pv = K.val("ac")

                    @blk.vector
                    def _(v, ac_pv=ac_pv):
                        K.wait(v, "ac", ac_pv)
                        i = v.reciprocal(RL[:], PVS[64:65, :])
                        K.inc(i, "dv")
                    dv_rl = K.val("dv")

                    @blk.tensor
                    def _(tx, par=0, dv_rl=dv_rl):
                        K.wait(tx, "dv", dv_rl)
                        if psA_free[0]:
                            K.wait(tx, psA_free[0][0], psA_free[0][1])
                        i = tx.matmul(PS_A[0][0:64, :], ONES_R[0:1, 0:64], RL[:],
                                      start=True, stop=True)
                        K.inc(i, "mm")
                    mm_rl = K.val("mm")

                    @blk.vector
                    def _(v, mm_rl=mm_rl):
                        K.wait(v, "mm", mm_rl)
                        K.wait(v, "so", K.val("so"))
                        i = v.tensor_tensor(AOS[:], PVS[0:64, :], PS_A[0][0:64, :], ALU.mult)
                        psA_free[0] = ("dv", K.inc(i, "dv"))
                    dv_ao = K.val("dv")

                    @blk.sync
                    def _(q, h=h, qb=qb, dv_ao=dv_ao):
                        K.wait(q, "dv", dv_ao)
                        i = q.dma_start(out=aod[h * 64:(h + 1) * 64, qb * 512:(qb + 1) * 512],
                                        in_=AOS[:])
                        K.inc(i, "so", 16)
        dv_attn = K.val("dv")
        so_attn = K.val("so")

        @blk.gpsimd
        def _(gq):
            K.wait(gq, "mm", K.val("mm"))
            for k in range(NDT):
                i = gq.dma_start(out=WOB[:, k * 1024:(k + 1) * 1024],
                                 in_=wo[k * 128:(k + 1) * 128, :])
                K.inc(i, "lwq", 16)
        lw_wo = K.val("lwq")

        for c in range(NCHUNK):
            @blk.sync
            def _(q, c=c):
                K.wait(q, "so", so_attn)
                K.wait(q, "dv", K.val("dv"))
                K.wait(q, "mm", K.val("mm"))
                for k in range(NDT):
                    i = q.dma_start(out=AOC[:, k * TC:(k + 1) * TC],
                                    in_=aod[k * 128:(k + 1) * 128, c * TC:(c + 1) * TC])
                    K.inc(i, "lx", 16)
                for k in range(NDT):
                    i = q.dma_start(out=G1[:, k * TC:(k + 1) * TC],
                                    in_=xT[k * 128:(k + 1) * 128, c * TC:(c + 1) * TC])
                    K.inc(i, "lx", 16)
            lx_xc = K.val("lx")

            for ob in range(NDT):
                par = ob % 2
                @blk.tensor
                def _(tx, ob=ob, par=par, lx_xc=lx_xc):
                    K.wait(tx, "lwq", lw_wo)
                    K.wait(tx, "lx", lx_xc)
                    K.wait(tx, "lw", 16)
                    K.wait(tx, "dv", K.val("dv"))
                    for k in range(NDT):
                        tx.matmul(PS_A[par][:, :], WOB[:, k * 1024 + ob * 128:k * 1024 + (ob + 1) * 128],
                                  AOC[:, k * TC:(k + 1) * TC],
                                  start=(k == 0), stop=False)
                    for k in range(NDT):
                        i = tx.matmul(PS_A[par][:, :], WOB[:, k * 1024 + ob * 128:k * 1024 + (ob + 1) * 128],
                                      bass.AP(tensor=BVB.ap().tensor, offset=k,
                                              ap=[[8, 128], [0, TC]]),
                                      start=False, stop=(k == NDT - 1))
                    K.inc(i, "mm")
                mm_o = K.val("mm")

                @blk.vector
                def _(v, ob=ob, par=par, mm_o=mm_o):
                    K.wait(v, "mm", mm_o)
                    i = v.scalar_tensor_tensor(G1[:, ob * TC:(ob + 1) * TC], PS_A[par][:, :],
                                               BOS[:, ob:ob + 1],
                                               G1[:, ob * TC:(ob + 1) * TC], ALU.add, ALU.add)
                    K.inc(i, "dv")
            dv_o = K.val("dv")

            @blk.sync
            def _(q, c=c, dv_o=dv_o):
                K.wait(q, "dv", dv_o)
                for k in range(NDT):
                    i = q.dma_start(out=yT[k * 128:(k + 1) * 128, c * TC:(c + 1) * TC],
                                    in_=G1[:, k * TC:(k + 1) * TC])
                    K.inc(i, "so", 16)

        @blk.sync
        def _(q):
            K.wait(q, "so", K.val("so"))
    return nc


def dev_run_l0attn(x, inp):
    nc = _build_l0attn_launch()
    cm = np.zeros((4, 128, 512), np.float32)
    for a in range(4):
        for p in range(128):
            cm[a, p, a * 128 + p:] = 1.0
    cm = cm.reshape(4 * 128, 512).astype(ml_dtypes.bfloat16)
    in_maps = []
    for b in range(NCORES):
        in_maps.append({
            "xT": np.ascontiguousarray(x[b].T),
            "wq": np.asarray(inp["L0_attn_Wq"]), "bq": np.asarray(inp["L0_attn_bq"]),
            "wk": np.asarray(inp["L0_attn_Wk"]), "bk": np.asarray(inp["L0_attn_bk"]),
            "wv": np.asarray(inp["L0_attn_Wv"]), "bv": np.asarray(inp["L0_attn_bv"]),
            "wo": np.asarray(inp["L0_attn_Wo"]), "bo": np.asarray(inp["L0_attn_bo"]),
            "cmask": cm,
            "onesd": np.ones((1, T), ml_dtypes.bfloat16),
        })
    res = run_bass_kernel_spmd(nc, in_maps, core_ids=list(range(NCORES)))
    return np.stack([np.asarray(res.results[b]["yT"], np.float32).T for b in range(NCORES)])



def _build_l3attn_launch():
    import math
    nc = bass.Bass("TRN2", target_bir_lowering=False, detect_race_conditions=False)
    xT = nc.dram_tensor("xT", [D, T], F32, kind="ExternalInput").ap()
    wqk = nc.dram_tensor("wqk", [D, 128], F32, kind="ExternalInput").ap()
    bqk = nc.dram_tensor("bqk", [128], F32, kind="ExternalInput").ap()
    gam = nc.dram_tensor("gam", [2, 128], F32, kind="ExternalInput").ap()
    bet = nc.dram_tensor("bet", [2, 128], F32, kind="ExternalInput").ap()
    wv = nc.dram_tensor("wv", [D, D], F32, kind="ExternalInput").ap()
    bv = nc.dram_tensor("bv", [D], F32, kind="ExternalInput").ap()
    rbt = nc.dram_tensor("rbt", [32], F32, kind="ExternalInput").ap()
    ohb = nc.dram_tensor("ohb", [32, T], F32, kind="ExternalInput").ap()
    cmask = nc.dram_tensor("cmask", [4 * 128, 512], BF16, kind="ExternalInput").ap()
    yT = nc.dram_tensor("yT", [D, T], F32, kind="ExternalOutput").ap()
    dgp = nc.dram_tensor("dgp", [2 * T], F32, kind="Internal").ap()
    dgp2 = nc.dram_tensor("dgp2", [128, 2 * T + 128], F32, kind="Internal").ap()

    MU_L = math.sqrt(0.5)
    STD_L = math.sqrt(0.25 * math.pi)
    ERF_SC = 1.0 / (STD_L * math.sqrt(2.0))
    NTT = T // 128

    with ExitStack() as stack:
        blk = stack.enter_context(nc.Block())
        K = Ctx(nc, blk, stack)
        for s_ in ("lx", "lw", "lwq", "lbt", "mm", "dv", "ac", "so"):
            K.sem(s_)
        ONES_F = K.sb("c_ONESF", [128, 1], F32)
        ONES_B = K.sb("c_ONESB", [128, 1], BF16)
        ONES_R = K.sb("c_ONESR", [1, 128], F32)
        P0 = dict(ONES_F=ONES_F, ONES_B=ONES_B, ONES_R=ONES_R)
        XHB = K.sb("c_XHB", [128, NDT * T], BF16)
        QF = K.sb("c_QF", [128, T], BF16)
        KF = K.sb("c_KF", [128, T], BF16)
        VT = K.sb("c_VT", [128, NTT * 1024], BF16)
        AOF = K.sb("c_AOF", [128, NDT * TC], F32)
        G1 = K.sb("c_G1", [128, NDT * TC], F32)
        XC = K.sb("c_XC", [128, TC], F32)
        MU = K.sb("c_MU", [1, TC], F32)
        PTS = K.sb("c_PTS", [128, 512], BF16)
        SB1 = K.sb("c_SB1", [128, 512], F32)
        BT = [K.sb(f"c_BT{i}", [128, 512], BF16) for i in range(2)]
        CM = K.sb("c_CM", [128, 4 * 512], BF16)
        WB = K.sb("c_WB", [128, NDT * 128], BF16)
        WVB = K.sb("c_WVB", [128, NDT * 512], BF16)
        BQKC = K.sb("c_BQKC", [128, 1], F32)
        GB = K.sb("c_GB", [128, 4], F32)
        BVR = K.sb("c_BVR", [128, 1024], BF16)
        TBL = K.sb("c_TBL", [32, 1], F32)
        OHP = K.sb("c_OHP", [32, 512], F32)
        DRow = K.sb("c_DRow", [1, 512], F32)
        BIX = K.sb("c_BIX", [128, 1], U32)
        PS_SUM = stack.enter_context(nc.psum_tensor("c_ps_sum", [1, TC], F32))
        PS_SS = stack.enter_context(nc.psum_tensor("c_ps_ss", [1, TC], F32))
        PS_MUR = stack.enter_context(nc.psum_tensor("c_ps_mur", [128, TC], F32))
        PS_RSR = stack.enter_context(nc.psum_tensor("c_ps_rsr", [128, TC], F32))
        PS_A = [stack.enter_context(nc.psum_tensor(f"c_ps_a{i}", [128, 512], F32)) for i in range(2)]
        PS_B = [stack.enter_context(nc.psum_tensor(f"c_ps_b{i}", [128, 512], F32)) for i in range(2)]

        @blk.vector
        def _(v):
            v.memset(ONES_F[:], 1.0)
            v.memset(ONES_B[:], 1.0)
            i = v.memset(ONES_R[:], 1.0)
            K.inc(i, "dv")

        @blk.sync
        def _(q):
            i = q.dma_start(out=CM[:], in_=bass.AP(tensor=cmask.tensor, offset=0,
                                                   ap=[[512, 128], [128 * 512, 4], [1, 512]]))
            K.inc(i, "lx", 16)
            i = q.dma_start(out=BQKC[:], in_=bass.AP(tensor=bqk.tensor, offset=0,
                                                     ap=[[1, 128], [1, 1]]))
            K.inc(i, "lx", 16)
            i = q.dma_start(out=GB[:, 0:2], in_=bass.AP(tensor=gam.tensor, offset=0,
                                                        ap=[[1, 128], [128, 2]]),
                            allow_slow_non_contiguous=True)
            K.inc(i, "lx", 16)
            i = q.dma_start(out=GB[:, 2:4], in_=bass.AP(tensor=bet.tensor, offset=0,
                                                        ap=[[1, 128], [128, 2]]),
                            allow_slow_non_contiguous=True)
            K.inc(i, "lx", 16)
            i = q.dma_start(out=TBL[:], in_=bass.AP(tensor=rbt.tensor, offset=0,
                                                    ap=[[1, 32], [1, 1]]))
            K.inc(i, "lx", 16)
        lx_p = K.val("lx")

        @blk.gpsimd
        def _(g):
            for k in range(NDT):
                i = g.dma_start(out=WB[:, k * 128:(k + 1) * 128],
                                in_=wqk[k * 128:(k + 1) * 128, :])
                K.inc(i, "lw", 16)

        lw_w = K.val("lw")

        @blk.sync
        def _(q):
            i = q.dma_start(out=AOF[0:1, 0:1024], in_=bass.AP(tensor=bv.tensor, offset=0,
                                                              ap=[[1024, 1], [1, 1024]]))
            K.inc(i, "lx", 16)
        lx_bv = K.val("lx")
        for half in range(2):
            @blk.tensor
            def _(tx, half=half):
                K.wait(tx, "lx", lx_bv)
                K.wait(tx, "ac", K.val("ac"))
                i = tx.matmul(PS_A[half][:, :], ONES_R[:], AOF[0:1, half * 512:(half + 1) * 512],
                              start=True, stop=True)
                K.inc(i, "mm")
            mmv = K.val("mm")

            @blk.scalar
            def _(sc, half=half, mmv=mmv):
                K.wait(sc, "mm", mmv)
                i = sc.copy(BVR[:, half * 512:(half + 1) * 512], PS_A[half][:, :])
                K.inc(i, "ac")
        ac_bvr = K.val("ac")

        @blk.vector
        def _(v):
            i = v.memset(DRow[:], -MU_L)
            K.inc(i, "dv")
        dv_dr = K.val("dv")
        for i_ in range(T // 512):
            @blk.sync
            def _(q, i_=i_, dv_dr=dv_dr):
                K.wait(q, "dv", dv_dr)
                i = q.dma_start(out=dgp[i_ * 512:(i_ + 1) * 512], in_=DRow[0:1, :])
                K.inc(i, "lbt", 16)
        for i_ in range(T // 512):
            @blk.sync
            def _(q, i_=i_):
                K.wait(q, "dv", K.val("dv"))
                i = q.dma_start(out=OHP[:], in_=ohb[:, i_ * 512:(i_ + 1) * 512])
                K.inc(i, "lx", 16)
            lxv = K.val("lx")

            @blk.tensor
            def _(tx, i_=i_, lxv=lxv):
                K.wait(tx, "lx", lxv)
                K.wait(tx, "lx", lx_p)
                K.wait(tx, "dv", K.val("dv"))
                K.wait(tx, "ac", ac_bvr)
                i = tx.matmul(PS_A[0][0:1, :], TBL[:], OHP[:], start=True, stop=True)
                K.inc(i, "mm")
            mmv = K.val("mm")

            @blk.vector
            def _(v, i_=i_, mmv=mmv):
                K.wait(v, "mm", mmv)
                i = v.tensor_scalar(DRow[:], PS_A[0][0:1, :], float(np.sqrt(128.0)), -MU_L,
                                    ALU.mult, ALU.add)
                K.inc(i, "dv")
            dvv = K.val("dv")

            @blk.sync
            def _(q, i_=i_, dvv=dvv):
                K.wait(q, "dvv" if False else "dv", dvv)
                i = q.dma_start(out=dgp[T + i_ * 512:T + (i_ + 1) * 512], in_=DRow[0:1, :])
                K.inc(i, "lbt", 16)
        lbt_d0 = K.val("lbt")

        @blk.sync
        def _(q):
            K.wait(q, "lbt", lbt_d0)
            for p in range(128):
                i = q.dma_start(out=dgp2[p:p + 1, p:p + 2 * T],
                                in_=bass.AP(tensor=dgp.tensor, offset=0, ap=[[2 * T, 1], [1, 2 * T]]))
                K.inc(i, "lbt", 16)
        lbt_d = K.val("lbt")

        dv_prev = K.val("dv")
        for c in range(NCHUNK):
            @blk.sync
            def _(q, c=c):
                K.wait(q, "mm", K.val("mm"))
                K.wait(q, "ac", K.val("ac"))
                K.wait(q, "dv", K.val("dv"))
                for k in range(NDT):
                    i = q.dma_start(out=G1[:, k * TC:(k + 1) * TC],
                                    in_=xT[k * 128:(k + 1) * 128, c * TC:(c + 1) * TC])
                    K.inc(i, "lx", 16)
            lxv = K.val("lx")
            for eng in ("vector", "scalar", "tensor"):
                @getattr(blk, eng)
                def _(e):
                    K.wait(e, "lx", lxv)
            XCl = [G1[:, k * TC:(k + 1) * TC] for k in range(NDT)]
            mm_rep = emit_ln_stats(K, XCl, slice(0, TC), ONES_F, VT[:, 0:NDT * TC], ONES_B,
                                   PS_SUM[:], PS_SS[:], MU, PS_SUM[:],
                                   PS_MUR[:], PS_RSR[:], ONES_R,
                                   dep_data=[("dv", dv_prev)])

            @blk.vector
            def _(v, c=c, mm_rep=mm_rep):
                K.wait(v, "mm", mm_rep)
                for k in range(NDT):
                    dst = XHB[:, k * T + c * TC:k * T + (c + 1) * TC]
                    v.tensor_tensor(XC[:], G1[:, k * TC:(k + 1) * TC], PS_MUR[:], ALU.subtract)
                    i = v.tensor_tensor(dst, XC[:], PS_RSR[:], ALU.mult)
                K.inc(i, "dv")
            dv_prev = K.val("dv")
        dv_ln = K.val("dv")

        psA_free = [None, None]
        psB_free = [None, None]
        for tb in range(T // 512):
            par = tb % 2
            @blk.tensor
            def _(tx, tb=tb, par=par):
                K.wait(tx, "lw", lw_w)
                K.wait(tx, "dv", dv_ln)
                if psA_free[par]:
                    K.wait(tx, psA_free[par][0], psA_free[par][1])
                for k in range(NDT):
                    i = tx.matmul(PS_A[par][:, :], WB[:, k * 128:(k + 1) * 128],
                                  XHB[:, k * T + tb * 512:k * T + (tb + 1) * 512],
                                  start=(k == 0), stop=(k == NDT - 1))
                K.inc(i, "mm")
            mmv = K.val("mm")

            @blk.scalar
            def _(sc, tb=tb, par=par, mmv=mmv):
                K.wait(sc, "mm", mmv)
                K.wait(sc, "lx", lx_p)
                K.wait(sc, "dv", K.val("dv"))
                i = sc.activation(SB1[:, 0:512], PS_A[par][:, :], AF.Silu, bias=BQKC[:])
                K.inc(i, "ac")
            acv = K.val("ac")

            @blk.vector
            def _(v, tb=tb, par=par, acv=acv):
                K.wait(v, "ac", acv)
                v.tensor_scalar(QF[:, tb * 512:(tb + 1) * 512], SB1[:, 0:512],
                                GB[:, 0:1], GB[:, 2:3], ALU.mult, ALU.add)
                i = v.tensor_scalar(KF[:, tb * 512:(tb + 1) * 512], SB1[:, 0:512],
                                    GB[:, 1:2], GB[:, 3:4], ALU.mult, ALU.add)
                psA_free[par] = ("dv", K.inc(i, "dv"))
        dv_qk = K.val("dv")

        for half in range(2):
            mm_cons = K.val("mm")
            @blk.gpsimd
            def _(g, half=half, mm_cons=mm_cons):
                K.wait(g, "mm", mm_cons)
                for k in range(NDT):
                    i = g.dma_start(out=WVB[:, k * 512:(k + 1) * 512],
                                    in_=bass.AP(tensor=wv.tensor,
                                                offset=k * 128 * 1024 + half * 512,
                                                ap=[[1024, 128], [1, 512]]))
                    K.inc(i, "lw", 16)
            lw_wv = K.val("lw")
            for ti in range(NTT):
                par = ti % 2
                @blk.tensor
                def _(tx, ti=ti, half=half, par=par, lw_wv=lw_wv):
                    K.wait(tx, "lw", lw_wv)
                    K.wait(tx, "dv", dv_ln)
                    if psB_free[par]:
                        K.wait(tx, psB_free[par][0], psB_free[par][1])
                    for k in range(NDT):
                        i = tx.matmul(PS_B[par][:, :],
                                      XHB[:, k * T + ti * 128:k * T + (ti + 1) * 128],
                                      WVB[:, k * 512:(k + 1) * 512],
                                      start=(k == 0), stop=(k == NDT - 1))
                    K.inc(i, "mm")
                mmv = K.val("mm")

                @blk.vector
                def _(v, ti=ti, half=half, par=par, mmv=mmv):
                    K.wait(v, "mm", mmv)
                    K.wait(v, "ac", K.val("ac"))
                    i = v.tensor_tensor(SB1[:, :], PS_B[par][:, :],
                                        BVR[:, half * 512:(half + 1) * 512], ALU.add)
                    K.inc(i, "dv")
                dvv = K.val("dv")

                @blk.scalar
                def _(sc, ti=ti, half=half, par=par, dvv=dvv):
                    K.wait(sc, "dv", dvv)
                    i = sc.activation(VT[:, ti * 1024 + half * 512:ti * 1024 + (half + 1) * 512],
                                      SB1[:, :], AF.Silu)
                    psB_free[par] = ("ac", K.inc(i, "ac"))
        dv_v = K.val("ac")

        for ib in range(T // 512):
            njt = (ib * 512 + 512) // 128
            for jt in range(njt):
                par = jt % 2
                dk = jt * 128 - ib * 512
                @blk.gpsimd
                def _(g, ib=ib, jt=jt, par=par):
                    K.wait(g, "lbt", lbt_d)
                    K.wait(g, "dv", K.val("dv"))
                    i = g.dma_start(out=BT[par][:],
                                    in_=bass.AP(tensor=dgp2.tensor,
                                                offset=T + ib * 512 - jt * 128,
                                                ap=[[2 * T + 128, 128], [1, 512]]))
                    K.inc(i, "lbt", 16)
                lbt_v = K.val("lbt")

                @blk.tensor
                def _(tx, ib=ib, jt=jt, par=par):
                    K.wait(tx, "dv", dv_qk)
                    if psA_free[par]:
                        K.wait(tx, psA_free[par][0], psA_free[par][1])
                    i = tx.matmul(PS_A[par][:, :], KF[:, jt * 128:(jt + 1) * 128],
                                  QF[:, ib * 512:(ib + 1) * 512], start=True, stop=True)
                    K.inc(i, "mm")
                mmv = K.val("mm")

                @blk.vector
                def _(v, par=par, mmv=mmv, lbt_v=lbt_v):
                    K.wait(v, "mm", mmv)
                    K.wait(v, "lbt", lbt_v)
                    K.wait(v, "ac", K.val("ac"))
                    i = v.scalar_tensor_tensor(SB1[:, :], PS_A[par][:, :], 1.0 / T,
                                               BT[par][:], ALU.mult, ALU.add)
                    psA_free[par] = ("dv", K.inc(i, "dv"))
                dvv = K.val("dv")

                @blk.scalar
                def _(sc, dvv=dvv):
                    K.wait(sc, "dv", dvv)
                    K.wait(sc, "dv", K.val("dv"))
                    i = sc.activation(PTS[:, :], SB1[:, :], AF.Erf, scale=ERF_SC)
                    K.inc(i, "ac")
                ac_e = K.val("ac")

                @blk.vector
                def _(v, dk=dk, ac_e=ac_e):
                    K.wait(v, "ac", ac_e)
                    i = v.tensor_scalar(PTS[:, :], PTS[:, :], 0.5, 0.5, ALU.mult, ALU.add)
                    K.inc(i, "dv")
                    if 0 <= dk:
                        K.wait(v, "dv")
                        i = v.tensor_tensor(PTS[:, :], PTS[:, :],
                                            CM[:, (dk // 128) * 512:(dk // 128 + 1) * 512],
                                            ALU.mult)
                        K.inc(i, "dv")
                dv_p = K.val("dv")

                for ds in range(NDT):
                    bpar = ds % 2
                    @blk.tensor
                    def _(tx, jt=jt, ds=ds, bpar=bpar, dv_p=dv_p):
                        K.wait(tx, "dv", dv_p)
                        K.wait(tx, "dv", dv_v)
                        if psB_free[bpar]:
                            K.wait(tx, psB_free[bpar][0], psB_free[bpar][1])
                        i = tx.matmul(PS_B[bpar][:, :],
                                      VT[:, jt * 1024 + ds * 128:jt * 1024 + (ds + 1) * 128],
                                      PTS[:, :], start=True, stop=True)
                        K.inc(i, "mm")
                    mm_pv = K.val("mm")

                    @blk.vector
                    def _(v, jt=jt, ds=ds, bpar=bpar, mm_pv=mm_pv):
                        K.wait(v, "mm", mm_pv)
                        dst = AOF[:, ds * TC:(ds + 1) * TC]
                        if jt == 0:
                            i = v.tensor_copy(dst, PS_B[bpar][:, :])
                        else:
                            i = v.tensor_tensor(dst, dst, PS_B[bpar][:, :], ALU.add)
                        psB_free[bpar] = ("dv", K.inc(i, "dv"))
            dv_pv = K.val("dv")

            @blk.sync
            def _(q, ib=ib):
                K.wait(q, "so", K.val("so"))
                K.wait(q, "dv", K.val("dv"))
                for k in range(NDT):
                    i = q.dma_start(out=G1[:, k * TC:(k + 1) * TC],
                                    in_=xT[k * 128:(k + 1) * 128, ib * 512:(ib + 1) * 512])
                    K.inc(i, "lx", 16)
            lx_xc = K.val("lx")

            @blk.vector
            def _(v, ib=ib, dv_pv=dv_pv, lx_xc=lx_xc):
                K.wait(v, "dv", dv_pv)
                K.wait(v, "lx", lx_xc)
                for k in range(NDT):
                    i = v.tensor_tensor(G1[:, k * TC:(k + 1) * TC], G1[:, k * TC:(k + 1) * TC],
                                        AOF[:, k * TC:(k + 1) * TC], ALU.add)
                K.inc(i, "dv")
            dv_o = K.val("dv")

            @blk.sync
            def _(q, ib=ib, dv_o=dv_o):
                K.wait(q, "dv", dv_o)
                for k in range(NDT):
                    i = q.dma_start(out=yT[k * 128:(k + 1) * 128, ib * 512:(ib + 1) * 512],
                                    in_=G1[:, k * TC:(k + 1) * TC])
                    K.inc(i, "so", 16)

        @blk.sync
        def _(q):
            K.wait(q, "so", K.val("so"))
    return nc


def dev_run_l3attn(x, inp):
    import math
    nc = _build_l3attn_launch()
    cm = np.zeros((4, 128, 512), np.float32)
    for a in range(4):
        for p in range(128):
            cm[a, p, a * 128 + p:] = 1.0
    cm = cm.reshape(4 * 128, 512).astype(ml_dtypes.bfloat16)
    NB, MD = 32, 128
    ohb = np.zeros((32, T), np.float32)
    for n in range(T):
        me = NB // 2
        if n < me:
            b = n
        else:
            ns = max(n, 1)
            b = me + int(math.log(ns / me) / math.log(MD / me) * (NB - me))
            b = min(b, NB - 1)
        ohb[b, n] = 1.0
    in_maps = []
    for b in range(NCORES):
        in_maps.append({
            "xT": np.ascontiguousarray(x[b].T),
            "wqk": np.asarray(inp["L3_attn_wqk"]), "bqk": np.asarray(inp["L3_attn_bqk"]),
            "gam": np.asarray(inp["L3_attn_gamma"]), "bet": np.asarray(inp["L3_attn_beta"]),
            "wv": np.asarray(inp["L3_attn_wv"]), "bv": np.asarray(inp["L3_attn_bv"]),
            "rbt": np.asarray(inp["L3_attn_rel_bias"]),
            "ohb": ohb, "cmask": cm,
        })
    res = run_bass_kernel_spmd(nc, in_maps, core_ids=list(range(NCORES)))
    return np.stack([np.asarray(res.results[b]["yT"], np.float32).T for b in range(NCORES)])



def _build_l1attn_launch():
    nc = bass.Bass("TRN2", target_bir_lowering=False, detect_race_conditions=False)
    xT = nc.dram_tensor("xT", [D, T], F32, kind="ExternalInput").ap()
    wq = nc.dram_tensor("wq", [D, D], F32, kind="ExternalInput").ap()
    wk = nc.dram_tensor("wk", [D, D], F32, kind="ExternalInput").ap()
    wv = nc.dram_tensor("wv", [D, D], F32, kind="ExternalInput").ap()
    wa = nc.dram_tensor("wa", [D, 16], F32, kind="ExternalInput").ap()
    ba = nc.dram_tensor("ba", [16], F32, kind="ExternalInput").ap()
    wb = nc.dram_tensor("wb", [D, 16], F32, kind="ExternalInput").ap()
    bb = nc.dram_tensor("bb", [16], F32, kind="ExternalInput").ap()
    wg = nc.dram_tensor("wg", [D, D], F32, kind="ExternalInput").ap()
    wo = nc.dram_tensor("wo", [D, D], F32, kind="ExternalInput").ap()
    gn = nc.dram_tensor("gn", [D], F32, kind="ExternalInput").ap()
    triui = nc.dram_tensor("triui", [128, 128], BF16, kind="ExternalInput").ap()
    trils = nc.dram_tensor("trils", [128, 128], BF16, kind="ExternalInput").ap()
    idenb = nc.dram_tensor("idenb", [128, 128], BF16, kind="ExternalInput").ap()
    idenf = nc.dram_tensor("idenf", [128, 128], F32, kind="ExternalInput").ap()
    yT = nc.dram_tensor("yT", [D, T], F32, kind="ExternalOutput").ap()
    yd = nc.dram_tensor("yd", [D, T], BF16, kind="Internal").ap()

    CK = 128
    NCH = T // CK

    with ExitStack() as stack:
        blk = stack.enter_context(nc.Block())
        K = Ctx(nc, blk, stack)
        for s_ in ("lx", "lw", "lwq", "mm", "dv", "ac", "so"):
            K.sem(s_)
        ONES_F = K.sb("d_ONESF", [128, 1], F32)
        ONES_B = K.sb("d_ONESB", [128, 1], BF16)
        ONES_R = K.sb("d_ONESR", [1, 128], F32)
        ONES_RB = K.sb("d_ONESRB", [1, 128], BF16)
        P0 = dict(ONES_F=ONES_F, ONES_B=ONES_B, ONES_R=ONES_R)
        XHB = K.sb("d_XHB", [128, NDT * T], BF16)
        QF = K.sb("d_QF", [64, T], BF16)
        KF = K.sb("d_KF", [64, T], BF16)
        VF = K.sb("d_VF", [64, T], BF16)
        ABT = K.sb("d_ABT", [128, NCH * 16], F32)
        BBT = K.sb("d_BBT", [128, NCH * 16], F32)
        LAT_ = K.sb("d_LAT", [128, NCH * 16], BF16)
        WBUF = K.sb("d_WBUF", [128, NDT * 192], BF16)
        G1 = K.sb("d_G1", [128, NDT * TC], F32)
        SQB = K.sb("d_SQB", [128, NDT * TC], BF16)
        XC = K.sb("d_XC", [128, TC], F32)
        MU = K.sb("d_MU", [1, TC], F32)
        SBX = K.sb("d_SBX", [128, 512], F32)
        ROWS = K.sb("d_ROWS", [1, 512], F32)
        COLS = K.sb("d_COLS", [128, 12], F32)
        CSC = K.sb("d_CSC", [128, 1], F32)
        GCC = K.sb("d_GCC", [64, 1], F32)
        MT = [K.sb(f"d_MT{i}", [128, 128], BF16) for i in range(7)]
        MTT = [K.sb(f"d_MTT{i}", [128, 128], BF16) for i in range(7)]
        NB_ = K.sb("d_NB", [128, 128], BF16)
        WTB = K.sb("d_WTB", [128, 128], BF16)
        EXD = K.sb("d_EXD", [128, 128], F32)
        VTOK = K.sb("d_VTOK", [128, 64], BF16)
        KDC = K.sb("d_KDC", [64, CK], BF16)
        KDT = K.sb("d_KDT", [128, 64], BF16)
        RB = K.sb("d_RB", [128, 64], F32)
        UB = K.sb("d_UB", [128, 64], BF16)
        UB2 = K.sb("d_UB2", [128, 64], BF16)
        FSC = K.sb("d_FSC", [128, 512], F32)
        YGB = K.sb("d_YGB", [128, NDT * 512], BF16)
        GSB = K.sb("d_GSB", [128, NDT * 512], BF16)
        WGB = K.sb("d_WGB", [128, 64 * 128], BF16)
        WOB = K.sb("d_WOB", [128, 64 * 128], BF16)
        S0 = K.sb("d_S0", [64, 64], F32)
        S0B = K.sb("d_S0B", [64, 64], BF16)
        YCH = K.sb("d_YCH", [128, 64], F32)
        YCB = K.sb("d_YCB", [64, 128], BF16)
        TRL = K.sb("d_TRL", [128, 3 * 128], BF16)
        IDF32 = K.sb("d_IDF32", [128, 128], F32)
        ZAB = K.sb("d_ZAB", [16, 1024], BF16)
        ZT1 = K.sb("d_ZT1", [128, NCH * 16], F32)
        ZT2 = K.sb("d_ZT2", [128, NCH * 16], F32)
        GNC = K.sb("d_GNC", [128, 8], F32)
        PS1 = stack.enter_context(nc.psum_tensor("d_ps1", [128, 512], F32))
        PS2 = stack.enter_context(nc.psum_tensor("d_ps2", [128, 512], F32))
        PS3 = stack.enter_context(nc.psum_tensor("d_ps3", [128, 512], F32))
        PS4 = stack.enter_context(nc.psum_tensor("d_ps4", [128, 128], F32))
        PSR = stack.enter_context(nc.psum_tensor("d_psr", [128, 512], F32))
        PS6 = stack.enter_context(nc.psum_tensor("d_ps6", [128, 128], BF16))
        PS_SUM = stack.enter_context(nc.psum_tensor("d_ps_sum", [1, TC], F32))
        PS_SS = stack.enter_context(nc.psum_tensor("d_ps_ss", [1, TC], F32))

        @blk.vector
        def _(v):
            v.memset(ONES_F[:], 1.0)
            v.memset(ONES_B[:], 1.0)
            v.memset(ONES_RB[:], 1.0)
            i = v.memset(ONES_R[:], 1.0)
            K.inc(i, "dv")

        @blk.sync
        def _(q):
            i = q.dma_start(out=TRL[:, 0:128], in_=triui)
            K.inc(i, "lx", 16)
            i = q.dma_start(out=TRL[:, 128:256], in_=trils)
            K.inc(i, "lx", 16)
            i = q.dma_start(out=TRL[:, 256:384], in_=idenb)
            K.inc(i, "lx", 16)
            i = q.dma_start(out=IDF32[:], in_=idenf)
            K.inc(i, "lx", 16)
            i = q.dma_start(out=GNC[:], in_=bass.AP(tensor=gn.tensor, offset=0,
                                                    ap=[[1, 128], [128, 8]]),
                            allow_slow_non_contiguous=True)
            K.inc(i, "lx", 16)
        lx_tr = K.val("lx")
        TRIUI = TRL[:, 0:128]
        TRILS = TRL[:, 128:256]
        IDB = TRL[:, 256:384]

        dv_prev = K.val("dv")
        for c in range(NCHUNK):
            @blk.sync
            def _(q, c=c):
                K.wait(q, "mm", K.val("mm"))
                K.wait(q, "ac", K.val("ac"))
                K.wait(q, "dv", K.val("dv"))
                for k in range(NDT):
                    i = q.dma_start(out=G1[:, k * TC:(k + 1) * TC],
                                    in_=xT[k * 128:(k + 1) * 128, c * TC:(c + 1) * TC])
                    K.inc(i, "lx", 16)
            lxv = K.val("lx")
            for eng in ("vector", "scalar", "tensor"):
                @getattr(blk, eng)
                def _(e):
                    K.wait(e, "lx", lxv)
            XCl = [G1[:, k * TC:(k + 1) * TC] for k in range(NDT)]
            mm_rep = emit_ln_stats(K, XCl, slice(0, TC), ONES_F, SQB, ONES_B,
                                   PS_SUM[:], PS_SS[:], MU, PS_SUM[:],
                                   PS1[:], PS2[:], ONES_R, dep_data=[("dv", dv_prev)])

            @blk.vector
            def _(v, c=c, mm_rep=mm_rep):
                K.wait(v, "mm", mm_rep)
                for k in range(NDT):
                    dst = XHB[:, k * T + c * TC:k * T + (c + 1) * TC]
                    v.tensor_tensor(XC[:], G1[:, k * TC:(k + 1) * TC], PS1[:, 0:TC], ALU.subtract)
                    i = v.tensor_tensor(dst, XC[:], PS2[:, 0:TC], ALU.mult)
                K.inc(i, "dv")
            dv_prev = K.val("dv")
        dv_ln = K.val("dv")

        @blk.gpsimd
        def _(g):
            for k in range(NDT):
                i = g.dma_start(out=WBUF[:, k * 128:k * 128 + 16], in_=wa[k * 128:(k + 1) * 128, :])
                K.inc(i, "lw", 16)
                i = g.dma_start(out=WBUF[:, k * 128 + 16:k * 128 + 32], in_=wb[k * 128:(k + 1) * 128, :])
                K.inc(i, "lw", 16)
            i = g.dma_start(out=COLS[0:16, 0:1], in_=bass.AP(tensor=ba.tensor, offset=0,
                                                             ap=[[1, 16], [1, 1]]))
            K.inc(i, "lw", 16)
            i = g.dma_start(out=COLS[0:16, 1:2], in_=bass.AP(tensor=bb.tensor, offset=0,
                                                             ap=[[1, 16], [1, 1]]))
            K.inc(i, "lw", 16)
        lw_ab = K.val("lw")

        for tb in range(T // 512):
            @blk.tensor
            def _(tx, tb=tb):
                K.wait(tx, "lw", lw_ab)
                K.wait(tx, "lx", lx_tr)
                K.wait(tx, "dv", dv_ln)
                K.wait(tx, "dv", K.val("dv"))
                K.wait(tx, "ac", K.val("ac"))
                for k in range(NDT):
                    tx.matmul(PS1[0:16, :], WBUF[:, k * 128:k * 128 + 16],
                              XHB[:, k * T + tb * 512:k * T + (tb + 1) * 512],
                              start=(k == 0), stop=(k == NDT - 1))
                for k in range(NDT):
                    i = tx.matmul(PS2[0:16, :], WBUF[:, k * 128 + 16:k * 128 + 32],
                                  XHB[:, k * T + tb * 512:k * T + (tb + 1) * 512],
                                  start=(k == 0), stop=(k == NDT - 1))
                K.inc(i, "mm")
            mmv = K.val("mm")

            @blk.vector
            def _(v, tb=tb, mmv=mmv):
                K.wait(v, "mm", mmv)
                v.tensor_scalar(ZAB[0:16, 0:512], PS1[0:16, :], COLS[0:16, 0:1], None, ALU.add)
                i = v.tensor_scalar(ZAB[0:16, 512:1024], PS2[0:16, :], COLS[0:16, 1:2], None, ALU.add)
                K.inc(i, "dv")
            dvv = K.val("dv")

            for sub in range(4):
                for which in range(2):
                    @blk.tensor
                    def _(tx, tb=tb, sub=sub, which=which, dvv=dvv):
                        K.wait(tx, "dv", dvv)
                        K.wait(tx, "ac", K.val("ac"))
                        i = tx.transpose(PS6[0:128, 0:16],
                                         ZAB[0:16, which * 512 + sub * 128:which * 512 + (sub + 1) * 128],
                                         IDB[0:16, 0:16])
                        K.inc(i, "mm")
                    mmt = K.val("mm")

                    @blk.scalar
                    def _(sc, tb=tb, sub=sub, which=which, mmt=mmt):
                        K.wait(sc, "mm", mmt)
                        c_ = tb * 4 + sub
                        dst = (ZT1 if which == 0 else ZT2)[:, c_ * 16:(c_ + 1) * 16]
                        i = sc.copy(dst, PS6[0:128, 0:16])
                        K.inc(i, "ac")
        ac_z = K.val("ac")

        @blk.scalar
        def _(sc):
            K.wait(sc, "ac", ac_z)
            i = sc.activation(BBT[:], ZT2[:], AF.Sigmoid)
            K.inc(i, "ac")
            K.wait(sc, "ac")
            i = sc.activation(ZT2[:], ZT1[:], AF.Exp, scale=-1.0)
            K.inc(i, "ac")
        ac_e = K.val("ac")

        @blk.vector
        def _(v):
            K.wait(v, "ac", ac_e)
            i = v.tensor_scalar(ZT2[:], ZT2[:], 1.0, None, ALU.add)
            K.inc(i, "dv")
        dv_e = K.val("dv")

        @blk.scalar
        def _(sc):
            K.wait(sc, "dv", dv_e)
            i = sc.activation(ZT2[:], ZT2[:], AF.Ln)
            K.inc(i, "ac")
            K.wait(sc, "ac")
            i = sc.activation(ABT[:], ZT1[:], AF.Sigmoid)
            K.inc(i, "ac")
        ac_l = K.val("ac")

        @blk.vector
        def _(v):
            K.wait(v, "ac", ac_l)
            i = v.tensor_scalar(LAT_[:], ZT2[:], -1.0, None, ALU.mult)
            K.inc(i, "dv")
        dv_gates = K.val("dv")

        psq = [None, None, None]
        for h in range(16):
            mm_cons = K.val("mm")
            @blk.gpsimd
            def _(g, h=h, mm_cons=mm_cons):
                K.wait(g, "mm", mm_cons)
                for k in range(NDT):
                    for j, wsrc in enumerate((wq, wk, wv)):
                        i = g.dma_start(out=WBUF[:, (k * 3 + j) * 64:(k * 3 + j + 1) * 64],
                                        in_=bass.AP(tensor=wsrc.tensor, offset=k * 128 * 1024 + h * 64,
                                                    ap=[[1024, 128], [1, 64]]))
                        K.inc(i, "lwq", 16)
            lw_qkv = K.val("lwq")

            for tb in range(T // 512):
                @blk.tensor
                def _(tx, tb=tb, h=h, lw_qkv=lw_qkv):
                    K.wait(tx, "lwq", lw_qkv)
                    K.wait(tx, "dv", K.val("dv"))
                    K.wait(tx, "ac", K.val("ac"))
                    for k in range(NDT):
                        tx.matmul(PS1[0:64, :], WBUF[:, (k * 3 + 0) * 64:(k * 3 + 1) * 64],
                                  XHB[:, k * T + tb * 512:k * T + (tb + 1) * 512],
                                  start=(k == 0), stop=(k == NDT - 1))
                    for k in range(NDT):
                        tx.matmul(PS2[0:64, :], WBUF[:, (k * 3 + 1) * 64:(k * 3 + 2) * 64],
                                  XHB[:, k * T + tb * 512:k * T + (tb + 1) * 512],
                                  start=(k == 0), stop=(k == NDT - 1))
                    for k in range(NDT):
                        i = tx.matmul(PS3[0:64, :], WBUF[:, (k * 3 + 2) * 64:(k * 3 + 3) * 64],
                                      XHB[:, k * T + tb * 512:k * T + (tb + 1) * 512],
                                      start=(k == 0), stop=(k == NDT - 1))
                    K.inc(i, "mm")
                mmv = K.val("mm")

                @blk.scalar
                def _(sc, mmv=mmv):
                    K.wait(sc, "mm", mmv)
                    K.wait(sc, "dv", K.val("dv"))
                    sc.activation(SQB[0:64, 0:512], PS1[0:64, :], AF.Square)
                    i = sc.activation(SQB[0:64, 512:1024], PS2[0:64, :], AF.Square)
                    K.inc(i, "ac")
                ac_s = K.val("ac")

                @blk.tensor
                def _(tx, ac_s=ac_s):
                    K.wait(tx, "ac", ac_s)
                    K.wait(tx, "dv", K.val("dv"))
                    tx.matmul(PS_SUM[0:1, 0:512], ONES_B[0:64, :], SQB[0:64, 0:512],
                              start=True, stop=True)
                    i = tx.matmul(PS_SS[0:1, 0:512], ONES_B[0:64, :], SQB[0:64, 512:1024],
                                  start=True, stop=True)
                    K.inc(i, "mm")
                mm_n = K.val("mm")

                @blk.scalar
                def _(sc, mm_n=mm_n):
                    K.wait(sc, "mm", mm_n)
                    sc.activation(MU[0:1, 0:512], PS_SUM[0:1, 0:512], AF.Sqrt)
                    i = sc.activation(XC[0:1, 0:512], PS_SS[0:1, 0:512], AF.Sqrt)
                    K.inc(i, "ac")
                ac_n = K.val("ac")

                @blk.vector
                def _(v, ac_n=ac_n):
                    K.wait(v, "ac", ac_n)
                    v.tensor_scalar(MU[0:1, 0:512], MU[0:1, 0:512], 1e-12, None, ALU.max)
                    i = v.tensor_scalar(XC[0:1, 0:512], XC[0:1, 0:512], 1e-12, None, ALU.max)
                    K.inc(i, "dv")
                    K.wait(v, "dv")
                    v.reciprocal(MU[0:1, 0:512], MU[0:1, 0:512])
                    i = v.reciprocal(XC[0:1, 0:512], XC[0:1, 0:512])
                    K.inc(i, "dv")
                dv_r = K.val("dv")

                @blk.tensor
                def _(tx, dv_r=dv_r):
                    K.wait(tx, "dv", dv_r)
                    K.wait(tx, "ac", K.val("ac"))
                    i = tx.matmul(PSR[0:64, :], ONES_R[0:1, 0:64], MU[0:1, 0:512],
                                  start=True, stop=True)
                    K.inc(i, "mm")
                mq = K.val("mm")

                @blk.scalar
                def _(sc, mq=mq):
                    K.wait(sc, "mm", mq)
                    i = sc.copy(SBX[0:64, :], PSR[0:64, :])
                    K.inc(i, "ac")
                aq = K.val("ac")

                @blk.tensor
                def _(tx, aq=aq):
                    K.wait(tx, "ac", aq)
                    i = tx.matmul(PSR[0:64, :], ONES_R[0:1, 0:64], XC[0:1, 0:512],
                                  start=True, stop=True)
                    K.inc(i, "mm")
                mk = K.val("mm")

                @blk.scalar
                def _(sc, mk=mk):
                    K.wait(sc, "mm", mk)
                    i = sc.copy(SBX[64:128, :], PSR[0:64, :])
                    K.inc(i, "ac")
                ak = K.val("ac")

                @blk.vector
                def _(v, tb=tb, ak=ak, mmv=mmv):
                    K.wait(v, "ac", ak)
                    v.tensor_tensor(QF[:, tb * 512:(tb + 1) * 512], PS1[0:64, :],
                                    SBX[0:64, :], ALU.mult)
                    v.tensor_tensor(KF[:, tb * 512:(tb + 1) * 512], PS2[0:64, :],
                                    SBX[64:128, :], ALU.mult)
                    i = v.tensor_copy(VF[:, tb * 512:(tb + 1) * 512], PS3[0:64, :])
                    K.inc(i, "dv")
            dv_qkv = K.val("dv")

            @blk.vector
            def _(v):
                K.wait(v, "dv", dv_qkv)
                v.memset(S0[:], 0.0)
                i = v.memset(S0B[:], 0.0)
                K.inc(i, "dv")

            for c in range(NCH):
                h_ = h
                sl = slice(c * CK, (c + 1) * CK)
                col = c * 16 + h_
                Qc = QF[:, sl]
                Kc = KF[:, sl]
                Vc = VF[:, sl]

                @blk.tensor
                def _(tx, col=col):
                    K.wait(tx, "dv", K.val("dv"))
                    K.wait(tx, "ac", K.val("ac"))
                    tx.matmul(PS4[0:128, 0:1], TRIUI, LAT_[:, col:col + 1],
                              start=True, stop=True)
                    i = tx.matmul(PS_SUM[0:1, 0:128], LAT_[:, col:col + 1], TRIUI,
                                  start=True, stop=True)
                    K.inc(i, "mm")
                mm_cs = K.val("mm")

                @blk.scalar
                def _(sc, mm_cs=mm_cs):
                    K.wait(sc, "mm", mm_cs)
                    sc.copy(CSC[:], PS4[0:128, 0:1])
                    i = sc.copy(ROWS[0:1, 0:128], PS_SUM[0:1, 0:128])
                    K.inc(i, "ac")
                ac_row = K.val("ac")

                @blk.vector
                def _(v, col=col, ac_row=ac_row):
                    K.wait(v, "ac", ac_row)
                    i = v.tensor_scalar(ROWS[0:1, 128:256], ROWS[0:1, 0:128], -1.0, None, ALU.mult)
                    K.inc(i, "dv")
                    K.wait(v, "dv")
                    i = v.reciprocal(COLS[:, 2:3], ABT[:, col:col + 1])
                    K.inc(i, "dv")
                    K.wait(v, "dv")
                    i = v.tensor_tensor(COLS[:, 3:4], BBT[:, col:col + 1], COLS[:, 2:3], ALU.mult)
                    K.inc(i, "dv")
                dv_c1 = K.val("dv")

                @blk.scalar
                def _(sc, dv_c1=dv_c1):
                    K.wait(sc, "dv", dv_c1)
                    sc.activation(COLS[:, 6:7], CSC[:], AF.Exp)
                    i = sc.activation(ROWS[0:1, 256:384], ROWS[0:1, 128:256], AF.Exp,
                                      bias=ROWS[0:1, 127:128])
                    K.inc(i, "ac")
                ac_c2 = K.val("ac")

                @blk.vector
                def _(v, col=col, ac_c2=ac_c2):
                    K.wait(v, "ac", ac_c2)
                    i = v.tensor_tensor(COLS[:, 7:8], COLS[:, 3:4], COLS[:, 6:7], ALU.mult)
                    K.inc(i, "dv")
                    K.wait(v, "dv")
                    i = v.tensor_scalar(COLS[:, 8:9], COLS[:, 7:8], -1.0, None, ALU.mult)
                    K.inc(i, "dv")
                dv_cols = K.val("dv")

                @blk.tensor
                def _(tx, dv_cols=dv_cols):
                    K.wait(tx, "dv", dv_cols)
                    i = tx.matmul(PS4[0:64, 0:1], ONES_R[0:1, 0:64], ROWS[0:1, 127:128],
                                  start=True, stop=True)
                    K.inc(i, "mm")
                mmg = K.val("mm")

                @blk.scalar
                def _(sc, mmg=mmg):
                    K.wait(sc, "mm", mmg)
                    i = sc.activation(GCC[:], PS4[0:64, 0:1], AF.Exp)
                    K.inc(i, "ac")
                ac_gcc = K.val("ac")

                @blk.tensor
                def _(tx, ac_gcc=ac_gcc):
                    K.wait(tx, "ac", ac_gcc)
                    K.wait(tx, "dv", K.val("dv"))
                    i = tx.matmul(PS4[:, :], Kc, Kc, start=True, stop=True)
                    K.inc(i, "mm")
                    tx.matmul(PSR[:, 0:128], ROWS[0:1, 0:128].bitcast(F32) if False else ROWS[0:1, 0:128],
                              ONES_R[0:1, 0:128], start=True, stop=False) if False else None
                    tx.matmul(PSR[:, 0:128], ROWS[0:1, 0:128], ONES_R[:], start=True, stop=False)
                    i = tx.matmul(PSR[:, 0:128], ONES_R[:], ROWS[0:1, 128:256], start=False, stop=True)
                    K.inc(i, "mm")
                mm_nd = K.val("mm")

                @blk.vector
                def _(v, mm_nd=mm_nd):
                    K.wait(v, "mm", mm_nd)
                    i = v.tensor_tensor(EXD[:], PSR[:, 0:128], TRILS, ALU.mult)
                    K.inc(i, "dv")
                dv_e1 = K.val("dv")

                @blk.scalar
                def _(sc, dv_e1=dv_e1):
                    K.wait(sc, "dv", dv_e1)
                    i = sc.activation(NB_[:], EXD[:], AF.Exp)
                    K.inc(i, "ac")
                ac_e1 = K.val("ac")

                @blk.vector
                def _(v, col=col, ac_e1=ac_e1):
                    K.wait(v, "ac", ac_e1)
                    i = v.tensor_tensor(WTB[:], NB_[:], TRILS, ALU.mult)
                    K.inc(i, "dv")
                    K.wait(v, "dv")
                    i = v.tensor_tensor(NB_[:], WTB[:], PS4[:, :], ALU.mult)
                    K.inc(i, "dv")
                    K.wait(v, "dv")
                    i = v.tensor_scalar(MT[0][:], NB_[:], COLS[:, 3:4], -1.0, ALU.mult, ALU.mult)
                    K.inc(i, "dv")
                dv_n0 = K.val("dv")

                @blk.tensor
                def _(tx, dv_n0=dv_n0):
                    K.wait(tx, "dv", dv_n0)
                    K.wait(tx, "ac", K.val("ac"))
                    i = tx.transpose(PS6[:, :], MT[0][:], IDB)
                    K.inc(i, "mm")
                mmt0 = K.val("mm")

                @blk.scalar
                def _(sc, mmt0=mmt0):
                    K.wait(sc, "mm", mmt0)
                    i = sc.copy(MTT[0][:], PS6[:, :])
                    K.inc(i, "ac")

                for j in range(1, 7):
                    acp = K.val("ac")
                    @blk.tensor
                    def _(tx, j=j, acp=acp):
                        K.wait(tx, "ac", acp)
                        i = tx.matmul(PS4[:, :], MTT[j - 1][:], MT[j - 1][:],
                                      start=True, stop=True)
                        K.inc(i, "mm")
                    mmq = K.val("mm")

                    @blk.scalar
                    def _(sc, j=j, mmq=mmq):
                        K.wait(sc, "mm", mmq)
                        i = sc.copy(MT[j][:], PS4[:, :])
                        K.inc(i, "ac")
                    acq = K.val("ac")

                    @blk.tensor
                    def _(tx, j=j, acq=acq):
                        K.wait(tx, "ac", acq)
                        i = tx.transpose(PS6[:, :], MT[j][:], IDB)
                        K.inc(i, "mm")
                    mmq2 = K.val("mm")

                    @blk.scalar
                    def _(sc, j=j, mmq2=mmq2):
                        K.wait(sc, "mm", mmq2)
                        i = sc.copy(MTT[j][:], PS6[:, :])
                        K.inc(i, "ac")

                acr = K.val("ac")
                @blk.tensor
                def _(tx, acr=acr):
                    K.wait(tx, "ac", acr)
                    K.wait(tx, "dv", K.val("dv"))
                    tx.matmul(PS4[:, 0:64], Kc, S0B[:], start=True, stop=True)
                    i = tx.transpose(PS6[:, 0:64], Vc, IDB[0:64, 0:64])
                    K.inc(i, "mm")
                mmr = K.val("mm")

                @blk.scalar
                def _(sc, mmr=mmr):
                    K.wait(sc, "mm", mmr)
                    i = sc.copy(VTOK[:, :], PS6[:, 0:64])
                    K.inc(i, "ac")
                acv = K.val("ac")

                @blk.vector
                def _(v, col=col, acv=acv):
                    K.wait(v, "ac", acv)
                    v.tensor_scalar(SBX[:, 0:64], VTOK[:, :], BBT[:, col:col + 1], None, ALU.mult)
                    i = v.tensor_scalar(SBX[:, 64:128], PS4[:, 0:64], COLS[:, 8:9], None, ALU.mult)
                    K.inc(i, "dv")
                    K.wait(v, "dv")
                    i = v.tensor_tensor(RB[:, :], SBX[:, 0:64], SBX[:, 64:128], ALU.add)
                    K.inc(i, "dv")
                dv_rb = K.val("dv")

                @blk.scalar
                def _(sc, dv_rb=dv_rb):
                    K.wait(sc, "dv", dv_rb)
                    i = sc.copy(UB[:, :], RB[:, :])
                    K.inc(i, "ac")

                cur, nxt = UB, UB2
                for j in range(6, -1, -1):
                    acu = K.val("ac")
                    dvu = K.val("dv")
                    @blk.tensor
                    def _(tx, j=j, acu=acu, dvu=dvu, cur=cur):
                        K.wait(tx, "ac", acu)
                        K.wait(tx, "dv", dvu)
                        i = tx.matmul(PS4[:, 0:64], MTT[j][:], cur[:, :], start=True, stop=True)
                        K.inc(i, "mm")
                    mmu = K.val("mm")

                    @blk.vector
                    def _(v, j=j, mmu=mmu, cur=cur, nxt=nxt):
                        K.wait(v, "mm", mmu)
                        i = v.tensor_tensor(nxt[:, :], cur[:, :], PS4[:, 0:64], ALU.add)
                        K.inc(i, "dv")
                    cur, nxt = nxt, cur
                UF = cur

                dvw = K.val("dv")
                @blk.tensor
                def _(tx, dvw=dvw):
                    K.wait(tx, "dv", dvw)
                    K.wait(tx, "ac", K.val("ac"))
                    i = tx.matmul(PS4[:, :], Kc, Qc, start=True, stop=True)
                    K.inc(i, "mm")
                    tx.matmul(PSR[:, 128:256], ONES_R[:], ROWS[0:1, 0:128], start=True, stop=False)
                    i = tx.matmul(PSR[:, 128:256], ROWS[0:1, 128:256], ONES_R[:], start=False, stop=True)
                    K.inc(i, "mm")
                mmw = K.val("mm")

                @blk.vector
                def _(v, mmw=mmw):
                    K.wait(v, "mm", mmw)
                    i = v.tensor_tensor(EXD[:], PSR[:, 128:256], TRIUI, ALU.mult)
                    K.inc(i, "dv")
                dvwe = K.val("dv")

                @blk.scalar
                def _(sc, dvwe=dvwe):
                    K.wait(sc, "dv", dvwe)
                    i = sc.activation(NB_[:], EXD[:], AF.Exp)
                    K.inc(i, "ac")
                acwe = K.val("ac")

                @blk.vector
                def _(v, acwe=acwe):
                    K.wait(v, "ac", acwe)
                    i = v.tensor_tensor(MT[0][:], NB_[:], TRIUI, ALU.mult)
                    K.inc(i, "dv")
                    K.wait(v, "dv")
                    i = v.tensor_tensor(WTB[:], MT[0][:], PS4[:, :], ALU.mult)
                    K.inc(i, "dv")
                dvwm = K.val("dv")

                @blk.tensor
                def _(tx, dvwm=dvwm):
                    K.wait(tx, "dv", dvwm)
                    i = tx.matmul(PS4[:, 0:64], Qc, S0B[:], start=True, stop=True)
                    K.inc(i, "mm")
                mmq0 = K.val("mm")

                @blk.scalar
                def _(sc, mmq0=mmq0):
                    K.wait(sc, "mm", mmq0)
                    i = sc.copy(SBX[:, 0:64], PS4[:, 0:64])
                    K.inc(i, "ac")
                acq0 = K.val("ac")

                @blk.tensor
                def _(tx, acq0=acq0, UF=UF):
                    K.wait(tx, "ac", acq0)
                    i = tx.matmul(PS4[:, 0:64], WTB[:], UF[:, :], start=True, stop=True)
                    K.inc(i, "mm")
                mmo2 = K.val("mm")

                @blk.vector
                def _(v, mmo2=mmo2):
                    K.wait(v, "mm", mmo2)
                    i = v.tensor_scalar(SBX[:, 64:128], SBX[:, 0:64], COLS[:, 6:7], None, ALU.mult)
                    K.inc(i, "dv")
                    K.wait(v, "dv")
                    i = v.tensor_tensor(YCH[:, :], SBX[:, 64:128], PS4[:, 0:64], ALU.add)
                    K.inc(i, "dv")
                dvy = K.val("dv")

                @blk.scalar
                def _(sc, dvy=dvy):
                    K.wait(sc, "dv", dvy)
                    i = sc.copy(VTOK[:, :], YCH[:, :])
                    K.inc(i, "ac")
                acy = K.val("ac")

                @blk.tensor
                def _(tx, acy=acy):
                    K.wait(tx, "ac", acy)
                    i = tx.transpose(PS6[0:64, 0:128], VTOK[:, :], IDB)
                    K.inc(i, "mm")
                    i = tx.matmul(PSR[0:64, 256:384], ONES_R[0:1, 0:64], ROWS[0:1, 256:384],
                                  start=True, stop=True)
                    K.inc(i, "mm")
                mmyt = K.val("mm")

                @blk.scalar
                def _(sc, mmyt=mmyt):
                    K.wait(sc, "mm", mmyt)
                    i = sc.copy(YCB[:, :], PS6[0:64, 0:128])
                    K.inc(i, "ac")
                acyt = K.val("ac")

                @blk.sync
                def _(q, h=h, c=c, acyt=acyt):
                    K.wait(q, "ac", acyt)
                    i = q.dma_start(out=yd[h * 64:(h + 1) * 64, c * CK:(c + 1) * CK],
                                    in_=YCB[:, :])
                    K.inc(i, "so", 16)

                @blk.vector
                def _(v, acyt=acyt):
                    K.wait(v, "ac", acyt)
                    i = v.tensor_tensor(KDC[:, :], Kc, PSR[0:64, 256:384], ALU.mult)
                    K.inc(i, "dv")
                dvkd = K.val("dv")

                @blk.tensor
                def _(tx, dvkd=dvkd):
                    K.wait(tx, "dv", dvkd)
                    i = tx.transpose(PS6[:, 0:64], KDC[:, :], IDB[0:64, 0:64])
                    K.inc(i, "mm")
                mmkd = K.val("mm")

                @blk.scalar
                def _(sc, mmkd=mmkd):
                    K.wait(sc, "mm", mmkd)
                    i = sc.copy(KDT[:, :], PS6[:, 0:64])
                    K.inc(i, "ac")
                ackd = K.val("ac")

                @blk.tensor
                def _(tx, ackd=ackd, UF=UF):
                    K.wait(tx, "ac", ackd)
                    i = tx.matmul(PS4[0:64, 0:64], KDT[:, :], UF[:, :], start=True, stop=True)
                    K.inc(i, "mm")
                mms = K.val("mm")

                @blk.vector
                def _(v, mms=mms):
                    K.wait(v, "mm", mms)
                    i = v.tensor_scalar(SBX[0:64, 0:64], S0[:, :], GCC[:, :], None, ALU.mult)
                    K.inc(i, "dv")
                    K.wait(v, "dv")
                    i = v.tensor_tensor(S0[:, :], SBX[0:64, 0:64], PS4[0:64, 0:64], ALU.add)
                    K.inc(i, "dv")
                dvs = K.val("dv")

                @blk.scalar
                def _(sc, dvs=dvs):
                    K.wait(sc, "dv", dvs)
                    i = sc.copy(S0B[:, :], S0[:, :])
                    K.inc(i, "ac")

        so_heads = K.val("so")

        @blk.gpsimd
        def _(g):
            K.wait(g, "so", so_heads)
            for k in range(NDT):
                for o in range(NDT):
                    i = g.dma_start(out=WGB[:, (k * 8 + o) * 128:(k * 8 + o + 1) * 128],
                                    in_=wg[k * 128:(k + 1) * 128, o * 128:(o + 1) * 128])
                    K.inc(i, "lw", 16)
                    i = g.dma_start(out=WOB[:, (k * 8 + o) * 128:(k * 8 + o + 1) * 128],
                                    in_=wo[k * 128:(k + 1) * 128, o * 128:(o + 1) * 128])
                    K.inc(i, "lw", 16)
        lw_fin = K.val("lw")

        for tb in range(T // 512):
            dv0 = K.val("dv"); ac0 = K.val("ac"); mm0 = K.val("mm"); so0 = K.val("so")
            @blk.sync
            def _(q, tb=tb, dv0=dv0, ac0=ac0, mm0=mm0, so0=so0):
                K.wait(q, "dv", dv0)
                K.wait(q, "ac", ac0)
                K.wait(q, "mm", mm0)
                K.wait(q, "so", so0)
                for k in range(NDT):
                    i = q.dma_start(out=YGB[:, k * 512:(k + 1) * 512],
                                    in_=yd[k * 128:(k + 1) * 128, tb * 512:(tb + 1) * 512])
                    K.inc(i, "lx", 16)
                    i = q.dma_start(out=G1[:, k * TC:(k + 1) * TC],
                                    in_=xT[k * 128:(k + 1) * 128, tb * 512:(tb + 1) * 512])
                    K.inc(i, "lx", 16)
            lx_f = K.val("lx")

            @blk.scalar
            def _(sc, lx_f=lx_f):
                K.wait(sc, "lx", lx_f)
                for k in range(NDT):
                    i = sc.activation(SQB[:, k * 512:(k + 1) * 512],
                                      YGB[:, k * 512:(k + 1) * 512], AF.Square)
                K.inc(i, "ac")
            ac_sq = K.val("ac")

            @blk.tensor
            def _(tx, ac_sq=ac_sq):
                K.wait(tx, "ac", ac_sq)
                K.wait(tx, "dv", K.val("dv"))
                for k in range(NDT):
                    i = tx.matmul(PS_SUM[0:1, :], ONES_B[:], SQB[:, k * 512:(k + 1) * 512],
                                  start=(k == 0), stop=(k == NDT - 1))
                K.inc(i, "mm")
            mm_ss = K.val("mm")

            @blk.vector
            def _(v, mm_ss=mm_ss):
                K.wait(v, "mm", mm_ss)
                i = v.tensor_scalar(XC[0:1, :], PS_SUM[0:1, :], 1.0 / 1024.0, 1e-5,
                                    ALU.mult, ALU.add)
                K.inc(i, "dv")
            dv_ms = K.val("dv")

            @blk.scalar
            def _(sc, dv_ms=dv_ms):
                K.wait(sc, "dv", dv_ms)
                i = sc.activation(MU[0:1, :], XC[0:1, :], AF.Sqrt)
                K.inc(i, "ac")
            ac_rt = K.val("ac")

            @blk.vector
            def _(v, ac_rt=ac_rt):
                K.wait(v, "ac", ac_rt)
                i = v.reciprocal(ROWS[0:1, 0:512], MU[0:1, 0:512])
                K.inc(i, "dv")
            dv_rc = K.val("dv")

            @blk.tensor
            def _(tx, dv_rc=dv_rc):
                K.wait(tx, "dv", dv_rc)
                i = tx.matmul(PSR[:, :], ONES_R[:], ROWS[0:1, 0:512], start=True, stop=True)
                K.inc(i, "mm")
            mm_rr = K.val("mm")

            @blk.scalar
            def _(sc, mm_rr=mm_rr):
                K.wait(sc, "mm", mm_rr)
                i = sc.copy(SBX[:, :], PSR[:, :])
                K.inc(i, "ac")
            ac_rr = K.val("ac")

            for o in range(NDT):
                acg = K.val("ac")
                @blk.tensor
                def _(tx, tb=tb, o=o, acg=acg, lw_fin=lw_fin):
                    K.wait(tx, "lw", lw_fin)
                    K.wait(tx, "ac", acg)
                    for k in range(NDT):
                        i = tx.matmul(PS1[:, :], WGB[:, (k * 8 + o) * 128:(k * 8 + o + 1) * 128],
                                      XHB[:, k * T + tb * 512:k * T + (tb + 1) * 512],
                                      start=(k == 0), stop=(k == NDT - 1))
                    K.inc(i, "mm")
                mmg2 = K.val("mm")

                @blk.scalar
                def _(sc, o=o, mmg2=mmg2):
                    K.wait(sc, "mm", mmg2)
                    i = sc.activation(GSB[:, o * 512:(o + 1) * 512], PS1[:, :], AF.Silu)
                    K.inc(i, "ac")
            ac_gate = K.val("ac")

            for k in range(NDT):
                acn = K.val("ac"); dvn = K.val("dv")
                @blk.vector
                def _(v, k=k, acn=acn, dvn=dvn):
                    K.wait(v, "ac", acn)
                    K.wait(v, "dv", dvn)
                    i = v.tensor_tensor(XC[:, :], YGB[:, k * 512:(k + 1) * 512], SBX[:, :], ALU.mult)
                    K.inc(i, "dv")
                    K.wait(v, "dv")
                    i = v.tensor_scalar(FSC[:, :], XC[:, :], GNC[:, k:k + 1], None, ALU.mult)
                    K.inc(i, "dv")
                    K.wait(v, "dv")
                    i = v.tensor_tensor(SQB[:, k * 512:(k + 1) * 512], FSC[:, :],
                                        GSB[:, k * 512:(k + 1) * 512], ALU.mult)
                    K.inc(i, "dv")
            dv_yn = K.val("dv")

            obuf_so = {}
            for o in range(NDT):
                dvo = K.val("dv"); aco = K.val("ac")
                @blk.tensor
                def _(tx, tb=tb, o=o, dvo=dvo, aco=aco):
                    K.wait(tx, "dv", dvo)
                    K.wait(tx, "ac", aco)
                    for k in range(NDT):
                        i = tx.matmul(PS2[:, :], WOB[:, (k * 8 + o) * 128:(k * 8 + o + 1) * 128],
                                      SQB[:, k * 512:(k + 1) * 512],
                                      start=(k == 0), stop=(k == NDT - 1))
                    K.inc(i, "mm")
                mmo = K.val("mm")

                buf = XC if (o % 2 == 0) else FSC
                @blk.vector
                def _(v, o=o, mmo=mmo, buf=buf):
                    K.wait(v, "mm", mmo)
                    if (o, ) and obuf_so.get(o % 2) is not None:
                        K.wait(v, "so", obuf_so[o % 2])
                    i = v.tensor_tensor(buf[:, :], PS2[:, :], G1[:, o * TC:(o + 1) * TC], ALU.add)
                    K.inc(i, "dv")
                dvob = K.val("dv")

                @blk.sync
                def _(q, tb=tb, o=o, dvob=dvob, buf=buf):
                    K.wait(q, "dv", dvob)
                    i = q.dma_start(out=yT[o * 128:(o + 1) * 128, tb * 512:(tb + 1) * 512],
                                    in_=buf[:, :])
                    K.inc(i, "so", 16)
                obuf_so[o % 2] = K.val("so")

        so_fin = K.val("so")
        @blk.gpsimd
        def _(g):
            g.wait_ge(K.sems["so"], so_fin)
    return nc


def dev_run_l1attn(x, inp, pre="L1_attn_"):
    import ml_dtypes
    nc = _build_l1attn_launch()
    B_ = x.shape[0]
    tri_i = np.triu(np.ones((128, 128), np.float32)).astype(ml_dtypes.bfloat16)
    tri_ls = np.tril(np.ones((128, 128), np.float32), -1).astype(ml_dtypes.bfloat16)
    ide_b = np.eye(128).astype(ml_dtypes.bfloat16)
    ide_f = np.eye(128).astype(np.float32)
    feeds = []
    for b in range(B_):
        feeds.append({
            "xT": np.ascontiguousarray(x[b].T),
            "wq": np.asarray(inp[pre + "Wq"], np.float32),
            "wk": np.asarray(inp[pre + "Wk"], np.float32),
            "wv": np.asarray(inp[pre + "Wv"], np.float32),
            "wa": np.asarray(inp[pre + "Wa"], np.float32),
            "ba": np.asarray(inp[pre + "ba"], np.float32),
            "wb": np.asarray(inp[pre + "Wb"], np.float32),
            "bb": np.asarray(inp[pre + "bb"], np.float32),
            "wg": np.asarray(inp[pre + "Wg"], np.float32),
            "wo": np.asarray(inp[pre + "Wo"], np.float32),
            "gn": np.asarray(inp[pre + "g_norm"], np.float32),
            "triui": tri_i, "trils": tri_ls, "idenb": ide_b, "idenf": ide_f,
        })
    res = run_bass_kernel_spmd(nc, feeds, core_ids=list(range(B_)))
    out = np.stack([np.asarray(res.results[b]["yT"], np.float32).T for b in range(B_)])
    return out


def _build_l2attn_launch():
    nc = bass.Bass("TRN2", target_bir_lowering=False, detect_race_conditions=False)
    TT_ = 512
    NB = 8
    xs = nc.dram_tensor("xs", [NB, D, TT_], F32, kind="ExternalInput").ap()
    wqkv = nc.dram_tensor("wqkv", [3 * D, D], F32, kind="ExternalInput").ap()
    bqkv = nc.dram_tensor("bqkv", [3 * D], F32, kind="ExternalInput").ap()
    wproj = nc.dram_tensor("wproj", [D, D], F32, kind="ExternalInput").ap()
    bproj = nc.dram_tensor("bproj", [D], F32, kind="ExternalInput").ap()
    mskd = nc.dram_tensor("mskd", [128, 1024], BF16, kind="ExternalInput").ap()
    idenb = nc.dram_tensor("idenb", [128, 128], BF16, kind="ExternalInput").ap()
    ys = nc.dram_tensor("ys", [NB, D, TT_], F32, kind="ExternalOutput").ap()

    with ExitStack() as stack:
        blk = stack.enter_context(nc.Block())
        K = Ctx(nc, blk, stack)
        for s_ in ("lx", "lw", "mm", "dv", "ac", "so"):
            K.sem(s_)
        ONES_F = K.sb("e_ONESF", [128, 1], F32)
        ONES_B = K.sb("e_ONESB", [128, 1], BF16)
        ONES_R = K.sb("e_ONESR", [1, 128], F32)
        ONES_RB = K.sb("e_ONESRB", [1, 128], BF16)
        WQB = K.sb("e_WQB", [128, 8 * 3072], BF16)
        WPB = K.sb("e_WPB", [128, 8 * 1024], BF16)
        WQS = K.sb("e_WQS", [128, 1024], BF16)
        BQR = K.sb("e_BQR", [1, 3072], BF16)
        BPC = K.sb("e_BPC", [128, 8], F32)
        MSK = K.sb("e_MSK", [128, 1024], BF16)
        IDB2 = K.sb("e_IDB", [128, 128], BF16)
        QTM = K.sb("e_QTM", [128, NB * 1024], BF16)
        KTM = K.sb("e_KTM", [128, NB * 1024], BF16)
        VTM = K.sb("e_VTM", [128, NB * 1024], BF16)
        G1 = K.sb("e_G1", [128, 1024], F32)
        SQB2 = K.sb("e_SQB2", [128, 1024], BF16)
        XT2 = K.sb("e_XT2", [128, 128], F32)
        MU2 = K.sb("e_MU2", [1, 128], F32)
        XLN = K.sb("e_XLN", [128, 1024], BF16)
        PRD = K.sb("e_PRD", [128, 1024], F32)
        FLD = K.sb("e_FLD", [128, 512], F32)
        EXC = K.sb("e_EXC", [128, 1024], F32)
        SCB = K.sb("e_SCB", [128, 1024], F32)
        DEN = K.sb("e_DEN", [128, 128], F32)
        DEN2 = K.sb("e_DEN2", [128, 128], F32)
        RC = K.sb("e_RC", [128, 128], F32)
        ACCA = K.sb("e_ACCA", [128, 1024], F32)
        ACCB = K.sb("e_ACCB", [128, 1024], F32)
        OUTB = K.sb("e_OUTB", [128, 1024], BF16)
        OTT = K.sb("e_OTT", [128, 1024], BF16)
        OB2A = K.sb("e_OB2A", [128, 128], F32)
        OB2B = K.sb("e_OB2B", [128, 128], F32)
        PS1 = stack.enter_context(nc.psum_tensor("e_ps1", [128, 512], F32))
        PS2 = stack.enter_context(nc.psum_tensor("e_ps2", [128, 512], F32))
        PS4 = stack.enter_context(nc.psum_tensor("e_ps4", [128, 128], F32))
        PS6 = stack.enter_context(nc.psum_tensor("e_ps6", [128, 128], BF16))
        PSMU = stack.enter_context(nc.psum_tensor("e_psmu", [128, 128], F32))
        PSRS = stack.enter_context(nc.psum_tensor("e_psrs", [128, 128], F32))
        PS_SUM = stack.enter_context(nc.psum_tensor("e_ps_sum", [1, 128], F32))
        PS_SS = stack.enter_context(nc.psum_tensor("e_ps_ss", [1, 128], F32))

        def sap(buf, offset, ap):
            return bass.AP(tensor=buf.ap().tensor, offset=offset, ap=ap)

        @blk.vector
        def _(v):
            v.memset(ONES_F[:], 1.0)
            v.memset(ONES_B[:], 1.0)
            v.memset(ONES_RB[:], 1.0)
            v.memset(ONES_R[:], 1.0)
            i = v.memset(SCB[:], 0.0)
            K.inc(i, "dv")
        dv_init = K.val("dv")

        @blk.sync
        def _(q):
            i = q.dma_start(out=MSK[:], in_=mskd)
            K.inc(i, "lw", 16)
            i = q.dma_start(out=IDB2[:], in_=idenb)
            K.inc(i, "lw", 16)
            i = q.dma_start(out=BPC[:], in_=bass.AP(tensor=bproj.tensor, offset=0,
                                                    ap=[[1, 128], [128, 8]]),
                            allow_slow_non_contiguous=True)
            K.inc(i, "lw", 16)

        @blk.gpsimd
        def _(g):
            i = g.dma_start(out=BQR[:], in_=bass.AP(tensor=bqkv.tensor, offset=0,
                                                    ap=[[3072, 1], [1, 3072]]))
            K.inc(i, "lw", 16)
        lw_c = K.val("lw")

        for phase, (wsrc, nstrip, wdst, wcols) in enumerate(
                ((wqkv, 24, WQB, 3072), (wproj, 8, WPB, 1024))):
            for st in range(nstrip):
                acp = K.val("ac")
                @blk.gpsimd
                def _(g, wsrc=wsrc, st=st, acp=acp):
                    K.wait(g, "ac", acp)
                    i = g.dma_start(out=WQS[:], in_=wsrc[st * 128:(st + 1) * 128, :])
                    K.inc(i, "lw", 16)
                lws = K.val("lw")

                for k in range(8):
                    acpk = K.val("ac")
                    @blk.tensor
                    def _(tx, k=k, lws=lws, lw_c=lw_c, acpk=acpk):
                        K.wait(tx, "lw", lws)
                        K.wait(tx, "lw", lw_c)
                        K.wait(tx, "ac", acpk)
                        i = tx.transpose(PS6[:, :], WQS[:, k * 128:(k + 1) * 128], IDB2[:])
                        K.inc(i, "mm")
                    mmk = K.val("mm")

                    @blk.scalar
                    def _(sc, st=st, k=k, mmk=mmk, wdst=wdst, wcols=wcols):
                        K.wait(sc, "mm", mmk)
                        i = sc.copy(wdst[:, k * wcols + st * 128:k * wcols + (st + 1) * 128],
                                    PS6[:, :])
                        K.inc(i, "ac")
        lw_w = K.val("lw")
        ac_w = K.val("ac")

        for tt in range(TT_ // 128):
            for n in range(NB):
                dvp = K.val("dv"); acp = K.val("ac"); mmp = K.val("mm")
                @blk.sync
                def _(q, n=n, tt=tt, dvp=dvp, acp=acp, mmp=mmp):
                    K.wait(q, "dv", dvp)
                    K.wait(q, "ac", acp)
                    K.wait(q, "mm", mmp)
                    for k in range(8):
                        i = q.dma_start(out=G1[:, k * 128:(k + 1) * 128],
                                        in_=bass.AP(tensor=xs.tensor,
                                                    offset=n * D * TT_ + k * 128 * TT_ + tt * 128,
                                                    ap=[[TT_, 128], [1, 128]]))
                        K.inc(i, "lx", 16)
                lxg = K.val("lx")

                @blk.scalar
                def _(sc, lxg=lxg):
                    K.wait(sc, "lx", lxg)
                    for k in range(8):
                        i = sc.activation(SQB2[:, k * 128:(k + 1) * 128],
                                          G1[:, k * 128:(k + 1) * 128], AF.Square)
                    K.inc(i, "ac")
                ac_sq = K.val("ac")

                @blk.tensor
                def _(tx, ac_sq=ac_sq, lxg=lxg):
                    K.wait(tx, "lx", lxg)
                    K.wait(tx, "dv", K.val("dv"))
                    for k in range(8):
                        i = tx.matmul(PS_SUM[:, :], ONES_F[:], G1[:, k * 128:(k + 1) * 128],
                                      start=(k == 0), stop=(k == 7))
                    K.inc(i, "mm")
                    K.wait(tx, "ac", ac_sq)
                    for k in range(8):
                        i = tx.matmul(PS_SS[:, :], ONES_B[:], SQB2[:, k * 128:(k + 1) * 128],
                                      start=(k == 0), stop=(k == 7))
                    K.inc(i, "mm")
                mm_st = K.val("mm")

                @blk.vector
                def _(v, mm_st=mm_st):
                    K.wait(v, "mm", mm_st)
                    i = v.tensor_scalar(MU2[:], PS_SUM[:, :], 1.0 / 1024.0, None, ALU.mult)
                    K.inc(i, "dv")
                dv_mu = K.val("dv")

                @blk.tensor
                def _(tx, dv_mu=dv_mu):
                    K.wait(tx, "dv", dv_mu)
                    i = tx.matmul(PSMU[:, :], ONES_R[:], MU2[:], start=True, stop=True)
                    K.inc(i, "mm")
                mm_mur = K.val("mm")

                @blk.vector
                def _(v, mm_mur=mm_mur):
                    v.tensor_tensor(PS2[0:1, 0:128], MU2[:], MU2[:], ALU.mult)
                    K.wait(v, "mm", mm_mur)
                    v.tensor_scalar(MU2[:], PS_SS[:, :], 1.0 / 1024.0, 1e-5, ALU.mult, ALU.add)
                    i = v.tensor_tensor(MU2[:], MU2[:], PS2[0:1, 0:128], ALU.subtract)
                    K.inc(i, "dv")
                dv_var = K.val("dv")

                @blk.scalar
                def _(sc, dv_var=dv_var):
                    K.wait(sc, "dv", dv_var)
                    i = sc.activation(MU2[:], MU2[:], AF.Sqrt)
                    K.inc(i, "ac")
                ac_sd = K.val("ac")

                @blk.vector
                def _(v, ac_sd=ac_sd):
                    K.wait(v, "ac", ac_sd)
                    i = v.reciprocal(MU2[:], MU2[:])
                    K.inc(i, "dv")
                dv_rs = K.val("dv")

                @blk.tensor
                def _(tx, dv_rs=dv_rs):
                    K.wait(tx, "dv", dv_rs)
                    i = tx.matmul(PSRS[:, :], ONES_R[:], MU2[:], start=True, stop=True)
                    K.inc(i, "mm")
                mm_rsr = K.val("mm")

                @blk.vector
                def _(v, mm_rsr=mm_rsr):
                    K.wait(v, "mm", mm_rsr)
                    for k in range(8):
                        v.tensor_tensor(XT2[:], G1[:, k * 128:(k + 1) * 128], PSMU[:, :],
                                        ALU.subtract)
                        i = v.tensor_tensor(XLN[:, k * 128:(k + 1) * 128], XT2[:], PSRS[:, :],
                                            ALU.mult)
                    K.inc(i, "dv")
                dv_ln = K.val("dv")

                for og in range(6):
                    acq = K.val("ac")
                    @blk.tensor
                    def _(tx, n=n, og=og, dv_ln=dv_ln, acq=acq, lw_w=lw_w, ac_w=ac_w):
                        K.wait(tx, "dv", dv_ln)
                        K.wait(tx, "lw", lw_w)
                        K.wait(tx, "ac", ac_w)
                        K.wait(tx, "ac", acq)
                        for k in range(8):
                            tx.matmul(PS1[:, :], XLN[:, k * 128:(k + 1) * 128],
                                      WQB[:, k * 3072 + og * 512:k * 3072 + (og + 1) * 512],
                                      start=(k == 0), stop=False)
                        i = tx.matmul(PS1[:, :], ONES_RB[:], BQR[0:1, og * 512:(og + 1) * 512],
                                      start=False, stop=True)
                        K.inc(i, "mm")
                    mmq = K.val("mm")

                    @blk.scalar
                    def _(sc, n=n, og=og, mmq=mmq):
                        K.wait(sc, "mm", mmq)
                        dst = (QTM, KTM, VTM)[og // 2]
                        i = sc.copy(dst[:, n * 1024 + (og % 2) * 512:n * 1024 + (og % 2 + 1) * 512],
                                    PS1[:, :])
                        K.inc(i, "ac")
            ac_qkv = K.val("ac")

            pairs = [(n, m) for n in range(NB) for m in (range(4) if n < 4 else range(8))]
            @blk.vector
            def _(v, tt=tt, ac_qkv=ac_qkv, dv_init=dv_init):
                K.wait(v, "ac", ac_qkv)
                K.wait(v, "dv", dv_init)
                for (n, m) in pairs:
                    i = v.tensor_tensor(PRD[:], QTM[:, n * 1024:(n + 1) * 1024],
                                        KTM[:, m * 1024:(m + 1) * 1024], ALU.mult)
                    i = v.tensor_tensor(FLD[:, 0:512],
                                        sap(PRD, 0, [[1024, 128], [64, 16], [1, 32]]),
                                        sap(PRD, 32, [[1024, 128], [64, 16], [1, 32]]), ALU.add)
                    i = v.tensor_tensor(PRD[:, 0:256],
                                        sap(FLD, 0, [[512, 128], [32, 16], [1, 16]]),
                                        sap(FLD, 16, [[512, 128], [32, 16], [1, 16]]), ALU.add)
                    i = v.tensor_tensor(FLD[:, 0:128],
                                        sap(PRD, 0, [[1024, 128], [16, 16], [1, 8]]),
                                        sap(PRD, 8, [[1024, 128], [16, 16], [1, 8]]), ALU.add)
                    i = v.tensor_tensor(PRD[:, 0:64],
                                        sap(FLD, 0, [[512, 128], [8, 16], [1, 4]]),
                                        sap(FLD, 4, [[512, 128], [8, 16], [1, 4]]), ALU.add)
                    i = v.tensor_tensor(FLD[:, 0:32],
                                        sap(PRD, 0, [[1024, 128], [4, 16], [1, 2]]),
                                        sap(PRD, 2, [[1024, 128], [4, 16], [1, 2]]), ALU.add)
                    i = v.tensor_tensor(sap(SCB, n * 128 + m, [[1024, 128], [8, 16]]),
                                        sap(FLD, 0, [[512, 128], [2, 16], [1, 1]]),
                                        sap(FLD, 1, [[512, 128], [2, 16], [1, 1]]), ALU.add)
                K.inc(i, "dv")
            dv_sc = K.val("dv")

            @blk.scalar
            def _(sc, dv_sc=dv_sc):
                K.wait(sc, "dv", dv_sc)
                i = sc.activation(EXC[:], SCB[:], AF.Exp, scale=0.125)
                K.inc(i, "ac")
            ac_ex = K.val("ac")

            @blk.vector
            def _(v, ac_ex=ac_ex):
                K.wait(v, "ac", ac_ex)
                i = v.tensor_tensor(SCB[:], EXC[:], MSK[:], ALU.mult)
                K.inc(i, "dv")
                K.wait(v, "dv")
                i = v.tensor_tensor(DEN[:], sap(SCB, 0, [[1024, 128], [8, 128]]),
                                    sap(SCB, 1, [[1024, 128], [8, 128]]), ALU.add)
                K.inc(i, "dv")
                cur, nxt = DEN, DEN2
                for m in range(2, 8):
                    K.wait(v, "dv")
                    i = v.tensor_tensor(nxt[:], cur[:], sap(SCB, m, [[1024, 128], [8, 128]]),
                                        ALU.add)
                    K.inc(i, "dv")
                    cur, nxt = nxt, cur
                K.wait(v, "dv")
                i = v.reciprocal(RC[:], cur[:])
                K.inc(i, "dv")
            dv_rc = K.val("dv")

            ob_so = {}
            for n in range(NB):
                dvn = K.val("dv"); acn = K.val("ac")
                @blk.vector
                def _(v, n=n, dvn=dvn, acn=acn):
                    K.wait(v, "dv", dvn)
                    K.wait(v, "ac", acn)
                    ms = list(range(4)) if n < 4 else list(range(8))
                    acc, oth = ACCA, ACCB
                    for ji, m in enumerate(ms):
                        asl = sap(SCB, n * 128 + m, [[1024, 128], [8, 16], [0, 64]])
                        if ji == 0:
                            i = v.tensor_tensor(acc[:], VTM[:, m * 1024:(m + 1) * 1024], asl,
                                                ALU.mult)
                        else:
                            i = v.tensor_tensor(PRD[:], VTM[:, m * 1024:(m + 1) * 1024], asl,
                                                ALU.mult)
                            K.inc(i, "dv")
                            K.wait(v, "dv")
                            i = v.tensor_tensor(oth[:], acc[:], PRD[:], ALU.add)
                            acc, oth = oth, acc
                        K.inc(i, "dv")
                        K.wait(v, "dv")
                    i = v.tensor_tensor(EXC[:], acc[:],
                                        sap(RC, n * 16, [[128, 128], [1, 16], [0, 64]]), ALU.mult)
                    K.inc(i, "dv")
                dv_o = K.val("dv")

                @blk.scalar
                def _(sc, dv_o=dv_o):
                    K.wait(sc, "dv", dv_o)
                    i = sc.copy(OUTB[:], EXC[:])
                    K.inc(i, "ac")
                ac_ob = K.val("ac")

                sop = K.val("so")
                @blk.sync
                def _(q, n=n, tt=tt, dv_o=dv_o, sop=sop):
                    K.wait(q, "dv", dv_o)
                    K.wait(q, "so", sop)
                    for k in range(8):
                        i = q.dma_start(out=G1[:, k * 128:(k + 1) * 128],
                                        in_=bass.AP(tensor=xs.tensor,
                                                    offset=n * D * TT_ + k * 128 * TT_ + tt * 128,
                                                    ap=[[TT_, 128], [1, 128]]))
                        K.inc(i, "lx", 16)
                lxr = K.val("lx")

                for ct in range(8):
                    acc2 = K.val("ac")
                    @blk.tensor
                    def _(tx, ct=ct, ac_ob=ac_ob, acc2=acc2):
                        K.wait(tx, "ac", ac_ob)
                        K.wait(tx, "ac", acc2)
                        i = tx.transpose(PS6[:, :], OUTB[:, ct * 128:(ct + 1) * 128], IDB2[:])
                        K.inc(i, "mm")
                    mmt = K.val("mm")

                    @blk.scalar
                    def _(sc, ct=ct, mmt=mmt):
                        K.wait(sc, "mm", mmt)
                        i = sc.copy(OTT[:, ct * 128:(ct + 1) * 128], PS6[:, :])
                        K.inc(i, "ac")
                ac_ot = K.val("ac")

                for o in range(8):
                    dvp2 = K.val("dv")
                    @blk.tensor
                    def _(tx, o=o, ac_ot=ac_ot, dvp2=dvp2):
                        K.wait(tx, "ac", ac_ot)
                        K.wait(tx, "dv", dvp2)
                        for kc in range(8):
                            i = tx.matmul(PS4[:, :],
                                          WPB[:, kc * 1024 + o * 128:kc * 1024 + (o + 1) * 128],
                                          OTT[:, kc * 128:(kc + 1) * 128],
                                          start=(kc == 0), stop=(kc == 7))
                        K.inc(i, "mm")
                    mmp2 = K.val("mm")

                    obuf = OB2A if (o % 2 == 0) else OB2B
                    @blk.vector
                    def _(v, o=o, mmp2=mmp2, obuf=obuf, lxr=lxr):
                        K.wait(v, "mm", mmp2)
                        K.wait(v, "lx", lxr)
                        if ob_so.get(o % 2) is not None:
                            K.wait(v, "so", ob_so[o % 2])
                        i = v.scalar_tensor_tensor(obuf[:], PS4[:, :], BPC[:, o:o + 1],
                                                   G1[:, o * 128:(o + 1) * 128],
                                                   ALU.add, ALU.add)
                        K.inc(i, "dv")
                    dv_ob = K.val("dv")

                    @blk.sync
                    def _(q, n=n, o=o, tt=tt, dv_ob=dv_ob, obuf=obuf):
                        K.wait(q, "dv", dv_ob)
                        i = q.dma_start(out=bass.AP(tensor=ys.tensor,
                                                    offset=n * D * TT_ + o * 128 * TT_ + tt * 128,
                                                    ap=[[TT_, 128], [1, 128]]),
                                        in_=obuf[:])
                        K.inc(i, "so", 16)
                    ob_so[o % 2] = K.val("so")

        so_fin = K.val("so")
        @blk.gpsimd
        def _(g):
            g.wait_ge(K.sems["so"], so_fin)
    return nc


def dev_run_l2attn(x, inp, pre="L2_attn_"):
    import ml_dtypes
    nc = _build_l2attn_launch()
    TT_ = 512
    msk = np.zeros((8, 16, 8), np.float32)
    for n in range(8):
        for m in range(8):
            if n >= 4 or m < 4:
                msk[n, :, m] = 1.0
    mrow = msk.reshape(1, 1024)
    mfull = np.broadcast_to(mrow, (128, 1024)).astype(ml_dtypes.bfloat16)
    ide_b = np.eye(128).astype(ml_dtypes.bfloat16)
    feeds = []
    for c in range(8):
        xs = np.ascontiguousarray(
            np.stack([x[n, c * TT_:(c + 1) * TT_, :].T for n in range(8)]))
        feeds.append({
            "xs": xs,
            "wqkv": np.asarray(inp[pre + "Wqkv"], np.float32),
            "bqkv": np.asarray(inp[pre + "bqkv"], np.float32),
            "wproj": np.asarray(inp[pre + "Wproj"], np.float32),
            "bproj": np.asarray(inp[pre + "bproj"], np.float32),
            "mskd": np.ascontiguousarray(mfull), "idenb": ide_b,
        })
    res = run_bass_kernel_spmd(nc, feeds, core_ids=list(range(8)))
    out = np.zeros_like(x)
    for c in range(8):
        ysc = np.asarray(res.results[c]["ys"], np.float32)
        for n in range(8):
            out[n, c * TT_:(c + 1) * TT_, :] = ysc[n].T
    return out


# version 04 (final)
# speedup vs baseline: 1.1817x; 1.1817x over previous
import numpy as np
import ml_dtypes
from contextlib import ExitStack

import concourse.bass as bass
import concourse.mybir as mybir
from concourse.bass_utils import run_bass_kernel_spmd

F32 = mybir.dt.float32
BF16 = mybir.dt.bfloat16
I32 = mybir.dt.int32
U32 = mybir.dt.uint32
AF = mybir.ActivationFunctionType
ALU = mybir.AluOpType
AX = mybir.AxisListType

D = 1024
T = 4096
NCORES = 8
NDT = D // 128
TC = 512
NCHUNK = T // TC
LN_EPS = 1e-5


class Ctx:

    def __init__(self, nc, blk, stack):
        self.nc = nc
        self.blk = blk
        self.stack = stack
        self.sems = {}
        self.cnt = {}

    def sem(self, name):
        if name not in self.sems:
            self.sems[name] = self.stack.enter_context(self.nc.semaphore(name))
            self.cnt[name] = 0
        return self.sems[name]

    def inc(self, inst, name, n=1):
        s = self.sem(name)
        inst.then_inc(s, n)
        self.cnt[name] += n
        return self.cnt[name]

    def val(self, name):
        return self.cnt.get(name, 0)

    def wait(self, eng, name, v=None):
        if v is None:
            v = self.cnt[name]
        assert v <= self.cnt.get(name, 0), \
            f"wait {name}>={v} but only {self.cnt.get(name, 0)} emitted"
        if v > 0:
            eng.wait_ge(self.sems[name], v)

    def sb(self, name, shape, dtype):
        return self.stack.enter_context(self.nc.sbuf_tensor(name, shape, dtype))


def _bcast16(t, g):
    free = 1
    for s in t.shape[1:]:
        free *= s
    return bass.AP(tensor=t.ap().tensor, offset=0, ap=[[free, 128], [1, g], [0, 16]])


def _view3(t, ncols, inner, outer_stride=None):
    free = 1
    for s in t.shape[1:]:
        free *= s
    return bass.AP(tensor=t.ap().tensor, offset=0,
                   ap=[[free, 128], [inner, ncols // inner], [1, inner]])



def emit_fence(K, engines=("tensor", "vector", "scalar")):
    blk = K.blk
    snap = {n: K.val(n) for n in ("mm", "dv", "ac", "gp")}

    def waits(eng, skip):
        for n, v in snap.items():
            if n != skip and v > 0:
                K.wait(eng, n, v)

    if "tensor" in engines:
        @blk.tensor
        def _(tx):
            waits(tx, "mm")
    if "vector" in engines:
        @blk.vector
        def _(v):
            waits(v, "dv")
    if "scalar" in engines:
        @blk.scalar
        def _(sc):
            waits(sc, "ac")

def emit_ln_stats(K, rhs_tiles, rhs_sl, ones_rhs, sq_buf, ones_sq,
                  ps_sum, ps_ss, mu, tmp, ps_mur, ps_rsr, ones_r,
                  dep_data=()):
    blk = K.blk

    ac_pre = K.val("ac")

    @blk.scalar
    def _(sc):
        for s, v in dep_data:
            K.wait(sc, s, v)
        K.wait(sc, "mm", K.val("mm"))
        if DEV_LN != -1:
            for k in range(NDT):
                i = sc.activation(sq_buf[:, k * TC:(k + 1) * TC], rhs_tiles[k][:, rhs_sl],
                                  AF.Square)
            K.inc(i, "ac")
    ac_sq = K.val("ac")

    @blk.tensor
    def _(tx):
        K.wait(tx, "dv", K.val("dv"))
        K.wait(tx, "ac", ac_pre)
        for s, v in dep_data:
            K.wait(tx, s, v)
        if DEV_LN != 0:
            for k in range(NDT):
                i = tx.matmul(ps_sum, ones_rhs[:], rhs_tiles[k][:, rhs_sl],
                              start=(k == 0), stop=(k == NDT - 1))
            K.inc(i, "mm")
        K.wait(tx, "ac", ac_sq)
        if DEV_LN >= 1:
            for k in range(NDT):
                i = tx.matmul(ps_ss, ones_sq[:], sq_buf[:, k * TC:(k + 1) * TC],
                              start=(k == 0), stop=(k == NDT - 1))
            K.inc(i, "mm")
    mm_stats = K.val("mm")
    if DEV_LN < 2:
        return mm_stats

    @blk.vector
    def _(v):
        K.wait(v, "mm", mm_stats)
        i = v.tensor_scalar(mu[:], ps_sum, 1.0 / D, None, ALU.mult)
        K.inc(i, "dv")
    dv_mu = K.val("dv")
    if DEV_LN < 3:
        return K.val("mm")

    @blk.tensor
    def _(tx):
        K.wait(tx, "dv", dv_mu)
        i = tx.matmul(ps_mur, ones_r[:] if hasattr(ones_r, "ap") else ones_r, mu[:], start=True, stop=True)
        K.inc(i, "mm")
    mm_mur = K.val("mm")
    if DEV_LN < 4:
        return mm_mur

    @blk.vector
    def _(v):
        v.tensor_tensor(tmp, mu[:], mu[:], ALU.mult)
        K.wait(v, "mm", mm_mur)
        v.tensor_scalar(mu[:], ps_ss, 1.0 / D, LN_EPS, ALU.mult, ALU.add)
        i = v.tensor_tensor(mu[:], mu[:], tmp, ALU.subtract)
        K.inc(i, "dv")
    dv_var = K.val("dv")

    @blk.scalar
    def _(sc):
        K.wait(sc, "dv", dv_var)
        i = sc.activation(mu[:], mu[:], AF.Sqrt)
        K.inc(i, "ac")
    ac_sd = K.val("ac")

    @blk.vector
    def _(v):
        K.wait(v, "ac", ac_sd)
        i = v.reciprocal(mu[:], mu[:])
        K.inc(i, "dv")
    dv_rs = K.val("dv")
    if DEV_LN < 5:
        return K.val("mm")

    @blk.tensor
    def _(tx):
        K.wait(tx, "dv", dv_rs)
        i = tx.matmul(ps_rsr, ones_r[:], mu[:], start=True, stop=True)
        K.inc(i, "mm")
    return K.val("mm")




def emit_ln_x(K, X, sl, XH, XHB, G1, P, PS_SUM, PS_SS, MU, PS_MUR, PS_RSR, dep_dv,
              ones_sq=None):
    blk = K.blk
    mm_rep = emit_ln_stats(K, [x[:] for x in X], sl, P["ONES_F"], G1,
                           ones_sq if ones_sq is not None else P["ONES_B"],
                           PS_SUM, PS_SS, MU, PS_SUM, PS_MUR, PS_RSR,
                           P["ONES_R"], dep_data=[("dv", dep_dv)])

    @blk.vector
    def _(v):
        K.wait(v, "mm", mm_rep)
        for k in range(NDT):
            hk = XH[:, k * TC:(k + 1) * TC]
            v.tensor_tensor(hk, X[k][:, sl], PS_MUR, ALU.subtract)
            v.tensor_tensor(hk, hk, PS_RSR, ALU.mult)
            i = v.tensor_copy(XHB[:, k * TC:(k + 1) * TC], hk)
        K.inc(i, "dv")
    return K.val("dv")



def emit_l0ffn(K, X, wtc, latc, lbtc, P, nchunk=NCHUNK):
    nc, blk = K.nc, K.blk
    with ExitStack() as SS:
        E = SS.enter_context
        H = E(nc.sbuf_tensor("l0f_H", [128, NDT * TC], F32))
        HB = E(nc.sbuf_tensor("l0f_HB", [128, NDT * TC], BF16))
        Y1B = E(nc.sbuf_tensor("l0f_Y1B", [128, NDT * TC], BF16))
        G1 = E(nc.sbuf_tensor("l0f_G1", [128, NDT * TC], BF16))
        WB = E(nc.sbuf_tensor("l0f_WB", [128, 2 * NDT * 1024], BF16))
        LBT2 = E(nc.sbuf_tensor("l0f_LBT2", [32, 1024], BF16))
        LAT2 = E(nc.sbuf_tensor("l0f_LAT2", [128, NDT * 32], BF16))
        Z1 = E(nc.sbuf_tensor("l0f_Z1", [32, TC], BF16))
        MU = E(nc.sbuf_tensor("l0f_MU", [1, TC], F32))
        PS_SUM = E(nc.psum_tensor("l0f_ps_sum", [1, TC], F32))
        PS_SS = E(nc.psum_tensor("l0f_ps_ss", [1, TC], F32))
        PS_MUR = E(nc.psum_tensor("l0f_ps_mur", [128, TC], F32))
        PS_RSR = E(nc.psum_tensor("l0f_ps_rsr", [128, TC], F32))
        PS_Y = [E(nc.psum_tensor(f"l0f_ps_y{i}", [128, TC], F32)) for i in range(3)]
        PS_Z = E(nc.psum_tensor("l0f_ps_z", [32, TC], F32))

        def wslot(slot, dtile, otile):
            base = slot * (NDT * 1024) + dtile * 1024 + otile * 128
            return WB[:, base:base + 128]

        slot_ready = [0, 0]

        def load_w(q, m, slot):
            i = q.dma_start(out=WB[:, slot * (NDT * 1024):(slot + 1) * (NDT * 1024)],
                            in_=wtc[m * 128:(m + 1) * 128, :])
            slot_ready[slot] = K.inc(i, f"lw{slot}", 16)

        lora_ready = [0]

        def load_lora(q, m):
            i = q.dma_start(out=LAT2[:], in_=latc[m * 128:(m + 1) * 128, :])
            K.inc(i, "lwl", 16)
            i = q.dma_start(out=LBT2[:], in_=lbtc[m * 32:(m + 1) * 32, :])
            lora_ready[0] = K.inc(i, "lwl", 16)

        @blk.sync
        def _(q):
            load_w(q, 0, 0)
            load_w(q, 1, 1)
            load_lora(q, 0)

        yread = {0: 0, 1: 0, 2: 0}
        dv_h = K.val("dv")

        def emit_qlora(m, slot, rhs_buf, dep_rhs, out_kind):
            @blk.tensor
            def _(tx):
                K.wait(tx, "dv", dep_rhs)
                K.wait(tx, "dv", K.val("dv"))
                K.wait(tx, "lwl", lora_ready[0])
                for k in range(NDT):
                    i = tx.matmul(PS_Z[:], LAT2[:, k * 32:(k + 1) * 32],
                                  rhs_buf[:, k * TC:(k + 1) * TC],
                                  start=(k == 0), stop=(k == NDT - 1))
                K.inc(i, "mm")
            mm_z = K.val("mm")

            @blk.vector
            def _(v):
                K.wait(v, "mm", mm_z)
                i = v.tensor_copy(Z1[:], PS_Z[:])
                K.inc(i, "dv")
            dv_z = K.val("dv")

            for o in range(NDT):
                bank = o % 3
                @blk.tensor
                def _(tx, o=o, bank=bank):
                    K.wait(tx, f"lw{slot}", slot_ready[slot])
                    if yread[bank] > 0:
                        K.wait(tx, "dv", yread[bank])
                    for k in range(NDT):
                        tx.matmul(PS_Y[bank][:], wslot(slot, k, o),
                                  rhs_buf[:, k * TC:(k + 1) * TC],
                                  start=(k == 0), stop=False)
                    K.wait(tx, "dv", dv_z)
                    i = tx.matmul(PS_Y[bank][:], LBT2[:, o * 128:(o + 1) * 128],
                                  Z1[:], start=False, stop=True)
                    K.inc(i, "mm")
                mm_y = K.val("mm")

                @blk.vector
                def _(v, o=o, bank=bank, mm_y=mm_y):
                    K.wait(v, "mm", mm_y)
                    if out_kind == "y1b":
                        i = v.tensor_scalar(Y1B[:, o * TC:(o + 1) * TC], PS_Y[bank][:],
                                            P["BIAS"][:, m * 8 + o:m * 8 + o + 1], None, ALU.add)
                    else:
                        ho = H[:, o * TC:(o + 1) * TC]
                        v.scalar_tensor_tensor(ho, PS_Y[bank][:],
                                               P["BIAS"][:, m * 8 + o:m * 8 + o + 1],
                                               ho, ALU.add, ALU.add)
                        i = v.tensor_copy(HB[:, o * TC:(o + 1) * TC], ho)
                    yread[bank] = K.inc(i, "dv")
            return K.val("dv")

        for c in range(nchunk):
            sl = slice(c * TC, (c + 1) * TC)
            mm_rep = emit_ln_stats(K, [x[:] for x in X], sl, P["ONES_F"], G1, P["ONES_B"],
                                   PS_SUM[:], PS_SS[:], MU, PS_SUM[:], PS_MUR[:], PS_RSR[:],
                                   P["ONES_R"], dep_data=[("dv", dv_h)])

            @blk.vector
            def _(v):
                K.wait(v, "mm", mm_rep)
                for k in range(NDT):
                    hk = H[:, k * TC:(k + 1) * TC]
                    if DEV_LN >= 5:
                        v.tensor_tensor(hk, X[k][:, sl], PS_MUR[:], ALU.subtract)
                        v.tensor_tensor(hk, hk, PS_RSR[:], ALU.mult)
                    else:
                        v.tensor_copy(hk, X[k][:, sl])
                    i = v.tensor_copy(HB[:, k * TC:(k + 1) * TC], hk)
                K.inc(i, "dv")
            dv_h = K.val("dv")

            for b in range(3):
                if DEV_SUB < 2:
                    break
                m0, m1 = 2 * b, 2 * b + 1
                dv_y1 = emit_qlora(m0, 0, HB, dv_h, "y1b")

                mm_now0 = K.val("mm")
                @blk.sync
                def _(q, m1=m1, mm_now0=mm_now0):
                    K.wait(q, "mm", mm_now0)
                    load_lora(q, m1)

                if DEV_SUB < 3:
                    break
                mm_rep2 = emit_ln_stats(K, [Y1B[:, k * TC:(k + 1) * TC] for k in range(NDT)],
                                        slice(None), P["ONES_B"], G1, P["ONES_B"],
                                        PS_SUM[:], PS_SS[:], MU, PS_SUM[:], PS_MUR[:], PS_RSR[:],
                                        P["ONES_R"], dep_data=[("dv", dv_y1)])

                @blk.vector
                def _(v, b=b):
                    K.wait(v, "mm", mm_rep2)
                    for o in range(NDT):
                        g = G1[:, o * TC:(o + 1) * TC]
                        v.tensor_tensor(g, Y1B[:, o * TC:(o + 1) * TC], PS_MUR[:], ALU.subtract)
                        i = v.tensor_tensor(g, g, PS_RSR[:], ALU.mult)
                    K.inc(i, "dv")
                dv_n = K.val("dv")

                @blk.scalar
                def _(sc, b=b):
                    K.wait(sc, "dv", dv_n)
                    for o in range(NDT):
                        g = G1[:, o * TC:(o + 1) * TC]
                        i = sc.activation(g, g, AF.Gelu,
                                          scale=P["GAM"][:, b * 8 + o:b * 8 + o + 1],
                                          bias=P["BET"][:, b * 8 + o:b * 8 + o + 1])
                    K.inc(i, "ac")
                ac_g = K.val("ac")

                @blk.tensor
                def _(tx):
                    K.wait(tx, "ac", ac_g)

                if DEV_SUB < 4:
                    break
                dv_h = emit_qlora(m1, 1, G1, K.val("dv"), "h")

                nxt = b + 1
                if not (c == nchunk - 1 and nxt == 3):
                    nm0, nm1 = (2 * nxt, 2 * nxt + 1) if nxt < 3 else (0, 1)
                    mm_now = K.val("mm")
                    @blk.sync
                    def _(q, nm0=nm0, nm1=nm1, mm_now=mm_now):
                        K.wait(q, "mm", mm_now)
                        load_w(q, nm0, 0)
                        load_w(q, nm1, 1)
                        load_lora(q, nm0)

            if DEV_SUB < 5:
                continue
            @blk.vector
            def _(v):
                for k in range(NDT):
                    i = v.scalar_tensor_tensor(X[k][:, sl], H[:, k * TC:(k + 1) * TC],
                                               0.5, X[k][:, sl], ALU.mult, ALU.add)
                K.inc(i, "dv")
            dv_h = K.val("dv")


def emit_l0ffn_dequant(K, qw, qs, wtc, ident_bf):
    nc, blk = K.nc, K.blk
    with ExitStack() as SS:
        E = SS.enter_context
        QW = [E(nc.sbuf_tensor(f"dq_qw{i}", [128, 1024], I32)) for i in range(2)]
        SC = [E(nc.sbuf_tensor(f"dq_sc{i}", [128, 64], F32)) for i in range(2)]
        WD = [E(nc.sbuf_tensor(f"dq_wd{i}", [128, 1024], BF16)) for i in range(2)]
        WT = E(nc.sbuf_tensor("dq_wt", [128, NDT * 1024], BF16))
        PT = [E(nc.psum_tensor(f"dq_pt{i}", [128, 128], BF16)) for i in range(2)]

        steps = [(m, ot) for m in range(6) for ot in range(NDT)]
        dv_done = {}
        mm_done = {}
        ac_done = {}
        st_done = {}
        for idx, (m, ot) in enumerate(steps):
            p = idx % 2

            @blk.sync
            def _(q, m=m, ot=ot, p=p, idx=idx):
                if idx >= 2:
                    K.wait(q, "dv", dv_done[idx - 2])
                i = q.dma_start(out=QW[p][:], in_=qw[m, ot * 128:(ot + 1) * 128, :])
                K.inc(i, "lqw", 16)
                i = q.dma_start(out=SC[p][:], in_=qs[m, ot * 128:(ot + 1) * 128, :])
                K.inc(i, "lqw", 16)
            lx_now = K.val("lqw")

            @blk.vector
            def _(v, p=p, idx=idx, lx_now=lx_now):
                K.wait(v, "lqw", lx_now)
                if idx >= 2:
                    K.wait(v, "mm", mm_done[idx - 2])
                v.tensor_scalar(WD[p][:], QW[p][:], 1.0 / 7.5, -1.0, ALU.mult, ALU.add)
                i = v.tensor_tensor(_view3(WD[p], 1024, 16), _view3(WD[p], 1024, 16),
                                    _bcast16(SC[p], 64), ALU.mult)
                dv_done[idx] = K.inc(i, "dv")

            for k in range(NDT):
                @blk.tensor
                def _(tx, p=p, idx=idx, k=k):
                    if k == 0:
                        K.wait(tx, "dv", dv_done[idx])
                    if idx * NDT + k >= 2:
                        K.wait(tx, "ac", K.val("ac") - 1)
                    i = tx.transpose(PT[k % 2][:], WD[p][:, k * 128:(k + 1) * 128],
                                     ident_bf[:])
                    K.inc(i, "mm")
                mmv = K.val("mm")

                @blk.scalar
                def _(sc, p=p, idx=idx, m=m, ot=ot, k=k, mmv=mmv):
                    K.wait(sc, "mm", mmv)
                    if k == 0 and ot == 0 and m >= 1:
                        K.wait(sc, "lqs", st_done[m - 1])
                    i = sc.copy(WT[:, k * 1024 + ot * 128:k * 1024 + (ot + 1) * 128],
                                PT[k % 2][:])
                    K.inc(i, "ac")
            ac_done[idx] = K.val("ac")
            mm_done[idx] = K.val("mm")

            if ot == NDT - 1:
                @blk.sync
                def _(q, m=m, idx=idx):
                    K.wait(q, "ac", ac_done[idx])
                    i = q.dma_start(out=wtc[m * 128:(m + 1) * 128, :], in_=WT[:])
                    st_done[m] = K.inc(i, "lqs", 16)


def emit_transpose_small(K, dst_ap, src_ap, pt, ident, np_, evac_rows):
    blk = K.blk

    @blk.tensor
    def _(tx):
        K.wait(tx, "ac", K.val("ac"))
        i = tx.transpose(pt, src_ap, ident)
        K.inc(i, "mm")
    mmv = K.val("mm")

    @blk.scalar
    def _(sc):
        K.wait(sc, "mm", mmv)
        i = sc.copy(dst_ap, pt if evac_rows is None else pt)
        K.inc(i, "ac")


def emit_l0ffn_params(K, dram, ident_bf, ident_f, P):
    nc, blk = K.nc, K.blk
    BIAS, GAM, BET = P["BIAS"], P["GAM"], P["BET"]
    ONES_F, ONES_B, ONES_R = P["ONES_F"], P["ONES_B"], P["ONES_R"]

    with ExitStack() as SS:
        E = SS.enter_context
        LA_ST = E(nc.sbuf_tensor("p_LAST", [32, 6 * 1024], BF16))
        LA_EV = E(nc.sbuf_tensor("p_LAEV", [128, 256], BF16))
        LB_EV = E(nc.sbuf_tensor("p_LBEV", [32, 1024], BF16))
        IDF_ = None
        LB_ST = E(nc.sbuf_tensor("p_LBST", [128, 6 * 8 * 32], BF16))
        QB_ST = E(nc.sbuf_tensor("p_QBST", [48, 128], F32))
        LG_ST = E(nc.sbuf_tensor("p_LGST", [24, 128], F32))
        LB2_ST = E(nc.sbuf_tensor("p_LB2ST", [24, 128], F32))
        PTB = E(nc.psum_tensor("p_PTB", [128, 128], BF16))
        PTF = E(nc.psum_tensor("p_PTF", [128, 128], F32))

        @blk.vector
        def _(v):
            v.memset(ONES_F[:], 1.0)
            v.memset(ONES_B[:], 1.0)
            i = v.memset(ONES_R[:], 1.0)
            K.inc(i, "dv")

        @blk.sync
        def _(q):
            i = q.dma_start(out=QB_ST[:], in_=dram["qb"].rearrange("m (a p) -> (m a) p", p=128))
            K.inc(i, "lpr", 16)
            i = q.dma_start(out=LG_ST[:], in_=dram["lg"].rearrange("m (a p) -> (m a) p", p=128))
            K.inc(i, "lpr", 16)
            i = q.dma_start(out=LB2_ST[:], in_=dram["lbeta"].rearrange("m (a p) -> (m a) p", p=128))
            K.inc(i, "lpr", 16)

        @blk.gpsimd
        def _(g):
            for m in range(6):
                i = g.dma_start(out=LA_ST[:, m * 1024:(m + 1) * 1024], in_=dram["la"][m])
                K.inc(i, "lw", 16)
                i = g.dma_start(out=LB_ST[:, m * 256:(m + 1) * 256],
                                in_=bass.AP(tensor=dram["lb"].tensor, offset=m * D * 32,
                                            ap=[[32, 128], [128 * 32, 8], [1, 32]]))
                K.inc(i, "lw", 16)
        lw_all = K.val("lw")
        lx_all = K.val("lpr")

        for (src, dst, rows) in ((QB_ST, BIAS, 48), (LG_ST, GAM, 24), (LB2_ST, BET, 24)):
            @blk.tensor
            def _(tx, src=src, rows=rows):
                K.wait(tx, "lpr", lx_all)
                K.wait(tx, "ac", K.val("ac"))
                i = tx.transpose(PTF[:, 0:rows], src[:], ident_f[0:rows, 0:rows])
                K.inc(i, "mm")
            mmv = K.val("mm")

            @blk.scalar
            def _(sc, dst=dst, rows=rows, mmv=mmv):
                K.wait(sc, "mm", mmv)
                i = sc.copy(dst[:], PTF[:, 0:rows])
                K.inc(i, "ac")

        for m in range(6):
            for k in range(NDT):
                @blk.tensor
                def _(tx, m=m, k=k):
                    K.wait(tx, "lw", lw_all)
                    K.wait(tx, "ac", K.val("ac"))
                    if k == 0:
                        K.wait(tx, "lps", K.val("lps"))
                    i = tx.transpose(PTB[:, 0:32],
                                     LA_ST[:, m * 1024 + k * 128:m * 1024 + (k + 1) * 128],
                                     ident_bf[0:32, 0:32])
                    K.inc(i, "mm")
                mmv = K.val("mm")

                @blk.scalar
                def _(sc, m=m, k=k, mmv=mmv):
                    K.wait(sc, "mm", mmv)
                    i = sc.copy(LA_EV[:, k * 32:(k + 1) * 32], PTB[:, 0:32])
                    K.inc(i, "ac")
            acv = K.val("ac")

            @blk.sync
            def _(q, m=m, acv=acv):
                K.wait(q, "ac", acv)
                i = q.dma_start(out=dram["latc"][m * 128:(m + 1) * 128, :], in_=LA_EV[:])
                K.inc(i, "lps", 16)

        for m in range(6):
            for ot in range(NDT):
                @blk.tensor
                def _(tx, m=m, ot=ot):
                    K.wait(tx, "lw", lw_all)
                    K.wait(tx, "ac", K.val("ac"))
                    if ot == 0:
                        K.wait(tx, "lps", K.val("lps"))
                    i = tx.transpose(PTB[0:32, 0:128],
                                     LB_ST[:, m * 256 + ot * 32:m * 256 + (ot + 1) * 32],
                                     ident_bf[:])
                    K.inc(i, "mm")
                mmv = K.val("mm")

                @blk.scalar
                def _(sc, m=m, ot=ot, mmv=mmv):
                    K.wait(sc, "mm", mmv)
                    i = sc.copy(LB_EV[:, ot * 128:(ot + 1) * 128], PTB[0:32, 0:128])
                    K.inc(i, "ac")
            acv = K.val("ac")

            @blk.sync
            def _(q, m=m, acv=acv):
                K.wait(q, "ac", acv)
                i = q.dma_start(out=dram["lbtc"][m * 32:(m + 1) * 32, :], in_=LB_EV[:])
                K.inc(i, "lps", 16)

        @blk.tensor
        def _(tx):
            K.wait(tx, "ac", K.val("ac"))

    return P



DEV_PHASE = 99
DEV_SUB = 99
DEV_LN = 99

def _build_l0ffn_launch():
    nc = bass.Bass("TRN2", target_bir_lowering=False, detect_race_conditions=False)
    xT = nc.dram_tensor("xT", [D, T], F32, kind="ExternalInput").ap()
    qw = nc.dram_tensor("qw", [6, D, D], I32, kind="ExternalInput").ap()
    qs = nc.dram_tensor("qs", [6, D, 64], F32, kind="ExternalInput").ap()
    qb = nc.dram_tensor("qb", [6, D], F32, kind="ExternalInput").ap()
    la = nc.dram_tensor("la", [6, 32, D], F32, kind="ExternalInput").ap()
    lb = nc.dram_tensor("lb", [6, D, 32], F32, kind="ExternalInput").ap()
    lg = nc.dram_tensor("lg", [3, D], F32, kind="ExternalInput").ap()
    lbeta = nc.dram_tensor("lbeta", [3, D], F32, kind="ExternalInput").ap()
    idenb = nc.dram_tensor("idenb", [128, 128], BF16, kind="ExternalInput").ap()
    idenf = nc.dram_tensor("idenf", [128, 128], F32, kind="ExternalInput").ap()
    yT = nc.dram_tensor("yT", [D, T], F32, kind="ExternalOutput").ap()
    wtc = nc.dram_tensor("wtc", [6 * 128, NDT * 1024], BF16, kind="Internal").ap()
    lbtc = nc.dram_tensor("lbtc", [6 * 32, D], BF16, kind="Internal").ap()
    latc = nc.dram_tensor("latc", [6 * 128, 256], BF16, kind="Internal").ap()

    with ExitStack() as stack:
        blk = stack.enter_context(nc.Block())
        K = Ctx(nc, blk, stack)
        for s in ("lx", "lw", "lwl", "lw0", "lw1", "lpr", "lps", "lqw", "lqs", "mm", "dv", "ac", "so"):
            K.sem(s)
        X = [K.sb(f"X{k}", [128, T], F32) for k in range(NDT)]
        IDB = K.sb("IDB", [128, 128], BF16)
        IDF = K.sb("IDF", [128, 128], F32)

        @blk.sync
        def _(q):
            i = q.dma_start(out=IDB[:], in_=idenb)
            K.inc(i, "lx", 16)
            i = q.dma_start(out=IDF[:], in_=idenf)
            K.inc(i, "lx", 16)
            for k in range(NDT):
                i = q.dma_start(out=X[k][:], in_=xT[k * 128:(k + 1) * 128, :])
                K.inc(i, "lx", 16)
        lx_x = K.val("lx")

        @blk.tensor
        def _(tx):
            K.wait(tx, "lx", lx_x)
        @blk.vector
        def _(v):
            K.wait(v, "lx", lx_x)
        @blk.scalar
        def _(sc):
            K.wait(sc, "lx", lx_x)

        P0 = dict(
            BIAS=K.sb("p_BIAS", [128, 48], F32),
            GAM=K.sb("p_GAM", [128, 24], F32),
            BET=K.sb("p_BET", [128, 24], F32),
            ONES_F=K.sb("p_ONESF", [128, 1], F32),
            ONES_B=K.sb("p_ONESB", [128, 1], BF16),
            ONES_R=K.sb("p_ONESR", [1, 128], F32),
        )
        prm = emit_l0ffn_params(K, dict(la=la, lb=lb, qb=qb, lg=lg, lbeta=lbeta,
                                        lbtc=lbtc, latc=latc), IDB, IDF, P0)
        if DEV_PHASE >= 2:
            emit_l0ffn_dequant(K, qw, qs, wtc, IDB)
        emit_fence(K)

        @blk.sync
        def _(q):
            K.wait(q, "lqs", K.val("lqs"))
            K.wait(q, "lps", K.val("lps"))

        if DEV_PHASE >= 3:
            emit_l0ffn(K, X, wtc, latc, lbtc, prm,
                       nchunk=(1 if DEV_PHASE == 3 else NCHUNK))

        dv_done = K.val("dv")

        @blk.sync
        def _(q):
            K.wait(q, "dv", dv_done)
            for k in range(NDT):
                i = q.dma_start(out=yT[k * 128:(k + 1) * 128, :], in_=X[k][:])
                K.inc(i, "so", 16)
            K.wait(q, "so", K.val("so"))
    return nc


def dev_run_l0ffn(x, inp):
    nc = _build_l0ffn_launch()
    idenb = np.eye(128).astype(ml_dtypes.bfloat16)
    idenf = np.eye(128).astype(np.float32)
    in_maps = []
    for b in range(NCORES):
        in_maps.append({
            "xT": np.ascontiguousarray(x[b].T),
            "qw": np.asarray(inp["L0_ffn_qweights"]), "qs": np.asarray(inp["L0_ffn_scales"]),
            "qb": np.asarray(inp["L0_ffn_biases"]), "la": np.asarray(inp["L0_ffn_lora_a"]),
            "lb": np.asarray(inp["L0_ffn_lora_b"]), "lg": np.asarray(inp["L0_ffn_ln_gamma"]),
            "lbeta": np.asarray(inp["L0_ffn_ln_beta"]), "idenb": idenb, "idenf": idenf,
        })
    res = run_bass_kernel_spmd(nc, in_maps, core_ids=list(range(NCORES)))
    return np.stack([np.asarray(res.results[b]["yT"], np.float32).T for b in range(NCORES)])


def kernel(**inputs):
    x = np.asarray(inputs["L0_attn_x"], np.float32)
    x = dev_run_l0ffn(x, inputs)
    x = dev_run_l0attn(x, inputs)
    x = dev_run_l0ffn(x, inputs)
    x = dev_run_l1ffn(x, inputs)
    x = dev_run_l1attn(x, inputs)
    x = dev_run_l1ffn(x, inputs)
    x = dev_run_l2ffn(x, inputs)
    x = dev_run_l2attn(x, inputs)
    x = dev_run_l2ffn(x, inputs)
    x = dev_run_l3ffn(x, inputs)
    x = dev_run_l3attn(x, inputs)
    x = dev_run_l3ffn(x, inputs)
    return x



_RPC = [0]

def emit_repack_up(K, raw, cache, E_, H_):
    nc, blk = K.nc, K.blk
    np_ = H_ // 512
    _RPC[0] += 1
    uid = _RPC[0]
    with ExitStack() as SS:
        En = SS.enter_context
        ST = [En(nc.sbuf_tensor(f"rp{uid}_st{i}", [128, H_], BF16)) for i in range(2)]
        ld_done = {}
        st_done = {}
        for e in range(E_):
            for k in range(NDT):
                idx = e * NDT + k
                p = idx % 2
                @blk.gpsimd
                def _(g, e=e, k=k, p=p, idx=idx):
                    if idx >= 2:
                        K.wait(g, "lqs", st_done[idx - 2])
                    i = g.dma_start(out=ST[p][:], in_=raw[e, k * 128:(k + 1) * 128, :])
                    ld_done[idx] = K.inc(i, "lqw", 16)

                @blk.sync
                def _(q, e=e, k=k, p=p, idx=idx):
                    K.wait(q, "lqw", ld_done[idx])
                    for qq in range(np_):
                        i = q.dma_start(
                            out=cache[(e * np_ + qq) * 128:(e * np_ + qq + 1) * 128,
                                      k * 512:(k + 1) * 512],
                            in_=ST[p][:, qq * 512:(qq + 1) * 512])
                        K.inc(i, "lqs", 16)
                    st_done[idx] = K.val("lqs")


def emit_repack_dn(K, raw, cache, E_, IN_, OUT_):
    nc, blk = K.nc, K.blk
    NI = IN_ // 128
    _RPC[0] += 1
    uid = _RPC[0]
    with ExitStack() as SS:
        En = SS.enter_context
        ST = [En(nc.sbuf_tensor(f"rd{uid}_st{i}", [128, OUT_], BF16)) for i in range(2)]
        ld_done = {}
        st_done = {}
        for e in range(E_):
            for k in range(NI):
                idx = e * NI + k
                p = idx % 2
                @blk.gpsimd
                def _(g, e=e, k=k, p=p, idx=idx):
                    if idx >= 2:
                        K.wait(g, "lqs", st_done[idx - 2])
                    i = g.dma_start(out=ST[p][:], in_=raw[e, k * 128:(k + 1) * 128, :])
                    ld_done[idx] = K.inc(i, "lqw", 16)

                @blk.sync
                def _(q, e=e, k=k, p=p, idx=idx):
                    K.wait(q, "lqw", ld_done[idx])
                    i = q.dma_start(out=cache[e * 128:(e + 1) * 128, k * OUT_:(k + 1) * OUT_],
                                    in_=ST[p][:])
                    st_done[idx] = K.inc(i, "lqs", 16)


def emit_repack_tr(K, raw, cache_up, e_range, H_, ident_bf):
    nc, blk = K.nc, K.blk
    HQ = H_ // 4
    NO = H_ // 128
    _RPC[0] += 1
    uid = _RPC[0]
    with ExitStack() as SS:
        En = SS.enter_context
        ST = [En(nc.sbuf_tensor(f"rt{uid}_st{i}", [128, 1024], BF16)) for i in range(2)]
        PT = [En(nc.psum_tensor(f"rt{uid}_pt{i}", [128, 128], BF16)) for i in range(2)]
        EV = En(nc.sbuf_tensor(f"rt{uid}_ev", [128, 128], BF16))
        ld_done = {}
        st_done = {}
        pt_free = {0: 0, 1: 0}
        for e in e_range:
            for ko in range(NO):
                idx = e * NO + ko
                p = idx % 2
                @blk.gpsimd
                def _(g, e=e, ko=ko, p=p, idx=idx):
                    if idx >= 2:
                        K.wait(g, "lqs", st_done[idx - 2])
                    i = g.dma_start(out=ST[p][:], in_=raw[e, ko * 128:(ko + 1) * 128, :])
                    ld_done[idx] = K.inc(i, "lqw", 16)

                for k in range(NDT):
                    @blk.tensor
                    def _(tx, p=p, k=k, idx=idx):
                        if k == 0:
                            K.wait(tx, "lqw", ld_done[idx])
                        if pt_free[k % 2] > 0:
                            K.wait(tx, "ac", pt_free[k % 2])
                        i = tx.transpose(PT[k % 2][:], ST[p][:, k * 128:(k + 1) * 128],
                                         ident_bf[:])
                        K.inc(i, "mm")
                    mmv = K.val("mm")

                    @blk.scalar
                    def _(sc, k=k, mmv=mmv):
                        K.wait(sc, "mm", mmv)
                        K.wait(sc, "lqs", K.val("lqs"))
                        i = sc.copy(EV[:], PT[k % 2][:])
                        pt_free[k % 2] = K.inc(i, "ac")
                    acv = K.val("ac")

                    @blk.sync
                    def _(q, e=e, ko=ko, k=k, acv=acv, idx=idx):
                        K.wait(q, "ac", acv)
                        qq = ko // (NO // 4)
                        olocal = (ko % (NO // 4)) * 128
                        i = q.dma_start(
                            out=cache_up[(e * 4 + qq) * 128:(e * 4 + qq + 1) * 128,
                                         k * HQ + olocal:k * HQ + olocal + 128],
                            in_=EV[:])
                        st_done[idx] = K.inc(i, "lqs", 16)



def emit_gating(K, LG, GT, GTB, SCR, PTF, ident_f, ones_r, dv_lg, E_, var,
                scale=0.5, keep_fn=None):
    nc, blk = K.nc, K.blk
    NT_ = TC // 128
    NE = NT_ * E_
    eqs = SCR[:, 0:NE]
    eq2s = SCR[:, NE:2 * NE]
    eq3s = SCR[:, 2 * NE:3 * NE]
    m1 = SCR[:, 3 * NE:3 * NE + NT_]
    m2 = SCR[:, 3 * NE + NT_:3 * NE + 2 * NT_]
    w1 = SCR[:, 3 * NE + 2 * NT_:3 * NE + 3 * NT_]
    w2 = SCR[:, 3 * NE + 3 * NT_:3 * NE + 4 * NT_]

    def step(v, i):
        K.inc(i, "dv")
        K.wait(v, "dv")

    @blk.vector
    def _(v):
        K.wait(v, "dv", dv_lg)
        for tt in range(NT_):
            i = v.tensor_reduce(m1[:, tt:tt + 1], LG[:, tt * E_:(tt + 1) * E_], AX.X, ALU.max)
        step(v, i)
        for tt in range(NT_):
            i = v.tensor_scalar(eqs[:, tt * E_:(tt + 1) * E_], LG[:, tt * E_:(tt + 1) * E_],
                                m1[:, tt:tt + 1], None, ALU.is_equal)
        step(v, i)
        for tt in range(NT_):
            i = v.scalar_tensor_tensor(eq3s[:, tt * E_:(tt + 1) * E_],
                                       eqs[:, tt * E_:(tt + 1) * E_], -1e30,
                                       LG[:, tt * E_:(tt + 1) * E_], ALU.mult, ALU.add)
        step(v, i)
        for tt in range(NT_):
            i = v.tensor_reduce(m2[:, tt:tt + 1], eq3s[:, tt * E_:(tt + 1) * E_], AX.X, ALU.max)
        step(v, i)
        for tt in range(NT_):
            i = v.tensor_scalar(eq2s[:, tt * E_:(tt + 1) * E_], eq3s[:, tt * E_:(tt + 1) * E_],
                                m2[:, tt:tt + 1], None, ALU.is_equal)
        K.inc(i, "dv")
    dv2 = K.val("dv")

    if var == "top2":
        @blk.vector
        def _(v):
            K.wait(v, "dv", dv2)
            i = v.tensor_tensor(w1[:], m1[:], m2[:], ALU.subtract)
            K.inc(i, "dv")
        dv3 = K.val("dv")

        @blk.scalar
        def _(sc):
            K.wait(sc, "dv", dv3)
            i = sc.activation(w2[:], w1[:], AF.Sigmoid)
            K.inc(i, "ac")
        ac2 = K.val("ac")

        @blk.vector
        def _(v):
            K.wait(v, "ac", ac2)
            for tt in range(NT_):
                i = v.tensor_tensor(eq3s[:, tt * E_:(tt + 1) * E_],
                                    eqs[:, tt * E_:(tt + 1) * E_],
                                    eq2s[:, tt * E_:(tt + 1) * E_], ALU.subtract)
            step(v, i)
            for tt in range(NT_):
                i = v.tensor_scalar(eqs[:, tt * E_:(tt + 1) * E_],
                                    eq3s[:, tt * E_:(tt + 1) * E_],
                                    w2[:, tt:tt + 1], scale, ALU.mult, ALU.mult)
            step(v, i)
            for tt in range(NT_):
                i = v.tensor_scalar(eq3s[:, tt * E_:(tt + 1) * E_],
                                    eq2s[:, tt * E_:(tt + 1) * E_], scale, None, ALU.mult)
            step(v, i)
            for tt in range(NT_):
                i = v.tensor_tensor(LG[:, tt * E_:(tt + 1) * E_],
                                    eqs[:, tt * E_:(tt + 1) * E_],
                                    eq3s[:, tt * E_:(tt + 1) * E_], ALU.add)
            K.inc(i, "dv")
        dv4 = K.val("dv")
    else:
        dvk, k0, k1 = keep_fn(K, eqs, eq2s, dv2)

        @blk.vector
        def _(v):
            K.wait(v, "dv", dvk)
            for tt in range(NT_):
                i = v.tensor_scalar(eq3s[:, tt * E_:(tt + 1) * E_],
                                    eqs[:, tt * E_:(tt + 1) * E_],
                                    m1[:, tt:tt + 1], k0[:, tt:tt + 1], ALU.mult, ALU.mult)
            step(v, i)
            for tt in range(NT_):
                i = v.tensor_scalar(eqs[:, tt * E_:(tt + 1) * E_],
                                    eq3s[:, tt * E_:(tt + 1) * E_], scale, None, ALU.mult)
            step(v, i)
            for tt in range(NT_):
                i = v.tensor_scalar(eq3s[:, tt * E_:(tt + 1) * E_],
                                    eq2s[:, tt * E_:(tt + 1) * E_],
                                    m2[:, tt:tt + 1], k1[:, tt:tt + 1], ALU.mult, ALU.mult)
            step(v, i)
            for tt in range(NT_):
                i = v.tensor_scalar(eq2s[:, tt * E_:(tt + 1) * E_],
                                    eq3s[:, tt * E_:(tt + 1) * E_], scale, None, ALU.mult)
            step(v, i)
            for tt in range(NT_):
                i = v.tensor_tensor(LG[:, tt * E_:(tt + 1) * E_],
                                    eqs[:, tt * E_:(tt + 1) * E_],
                                    eq2s[:, tt * E_:(tt + 1) * E_], ALU.add)
            K.inc(i, "dv")
        dv4 = K.val("dv")

    gat = LG

    for tt in range(NT_):
        @blk.tensor
        def _(tx, tt=tt):
            K.wait(tx, "dv", dv4)
            K.wait(tx, "ac", K.val("ac"))
            i = tx.transpose(PTF[0:E_, tt * 128:(tt + 1) * 128],
                             gat[:, tt * E_:(tt + 1) * E_], ident_f[:])
            K.inc(i, "mm")
        mmv = K.val("mm")

        @blk.scalar
        def _(sc, tt=tt, mmv=mmv):
            K.wait(sc, "mm", mmv)
            i = sc.copy(GTB[0:E_, tt * 128:(tt + 1) * 128], PTF[0:E_, tt * 128:(tt + 1) * 128])
            K.inc(i, "ac")
    for e in range(E_):
        for tt in range(NT_):
            @blk.tensor
            def _(tx, e=e, tt=tt):
                K.wait(tx, "ac", K.val("ac"))
                i = tx.transpose(PTF[0:1, (e % 4) * 128:(e % 4) * 128 + 128],
                                 gat[:, tt * E_ + e:tt * E_ + e + 1], ident_f[:])
                K.inc(i, "mm")
            mmv = K.val("mm")

            @blk.scalar
            def _(sc, e=e, tt=tt, mmv=mmv):
                K.wait(sc, "mm", mmv)
                i = sc.copy(GT[0:1, e * TC + tt * 128:e * TC + (tt + 1) * 128],
                            PTF[0:1, (e % 4) * 128:(e % 4) * 128 + 128])
                K.inc(i, "ac")
    return K.val("ac")



MOE_NCHUNK = NCHUNK

def emit_moe(K, xd_in, xd_out, P, cfg):
    nc, blk = K.nc, K.blk
    E_, H_, var = cfg["E"], cfg["H"], cfg["var"]
    NH = H_ // 128
    NPU = H_ // 512
    HGRP = 8 if NH >= 16 else NH
    NGRP = NH // HGRP
    DN_T = cfg.get("dn_tiles", HGRP)
    w_up, w_dn = cfg["w_up"], cfg["w_dn"]
    nup = len(w_up)
    scale = cfg.get("scale", 0.5)

    with ExitStack() as SS:
        E = SS.enter_context
        XC = [E(nc.sbuf_tensor(f"mo_XC{k}", [128, TC], F32)) for k in range(NDT)]
        XHB = E(nc.sbuf_tensor("mo_XHB", [128, NDT * TC], BF16))
        G1 = E(nc.sbuf_tensor("mo_G1", [128, NDT * TC], F32))
        ACC = E(nc.sbuf_tensor("mo_ACC", [128, NDT * TC], F32))
        ACT = E(nc.sbuf_tensor("mo_ACT", [128, HGRP * TC], BF16))
        S0 = E(nc.sbuf_tensor("mo_S0", [128, 2 * TC], BF16))
        WU = [E(nc.sbuf_tensor(f"mo_WU{j}", [128, 2 * NDT * 512], BF16)) for j in range(nup)]
        WD = E(nc.sbuf_tensor("mo_WD", [128, 2 * DN_T * 1024], BF16))
        RT = E(nc.sbuf_tensor("mo_RT", [128, NDT * E_], BF16))
        LG = E(nc.sbuf_tensor("mo_LG", [128, (TC // 128) * E_], F32))
        GT = E(nc.sbuf_tensor("mo_GT", [1, E_ * TC], F32))
        GTB = E(nc.sbuf_tensor("mo_GTB", [E_, TC], BF16))
        SCR = E(nc.sbuf_tensor("mo_SCR", [128, (TC // 128) * (3 * E_ + 6)], F32))
        MU = E(nc.sbuf_tensor("mo_MU", [1, TC], F32))
        PS_SUM = E(nc.psum_tensor("mo_ps_sum", [1, TC], F32))
        PS_SS = E(nc.psum_tensor("mo_ps_ss", [1, TC], F32))
        PS_MUR = E(nc.psum_tensor("mo_ps_mur", [128, TC], F32))
        PS_RSR = E(nc.psum_tensor("mo_ps_rsr", [128, TC], F32))
        PS_H = [E(nc.psum_tensor(f"mo_ps_h{i}", [128, TC], F32)) for i in range(2)]
        PS_O = [E(nc.psum_tensor(f"mo_ps_o{i}", [128, TC], F32)) for i in range(2)]
        K.moe_ps_scratch = PS_O[0]

        setup_done = cfg["setup"](K, RT)

        up_ready = {}
        dn_ready = {}

        def load_up(q, j, e, piece, slot):
            i = q.dma_start(out=WU[j][:, slot * NDT * 512:(slot + 1) * NDT * 512],
                            in_=w_up[j][(e * NPU + piece) * 128:(e * NPU + piece + 1) * 128, :])
            up_ready[(j, slot)] = K.inc(i, f"lu{j}", 16)

        def load_dn(q, e, grp, slot):
            i = q.dma_start(out=WD[:, slot * DN_T * 1024:(slot + 1) * DN_T * 1024],
                            in_=w_dn[e * 128:(e + 1) * 128,
                                     grp * DN_T * 1024:(grp + 1) * DN_T * 1024])
            dn_ready[slot] = K.inc(i, "ldn", 16)

        hread = {0: 0, 1: 0}
        oread = {0: 0, 1: 0}
        dv_prev = K.val("dv")

        for c in range(globals()['MOE_NCHUNK']):
            @blk.sync
            def _(q, c=c):
                K.wait(q, "so", K.val("so"))
                K.wait(q, "dv", K.val("dv"))
                for k in range(NDT):
                    i = q.dma_start(out=XC[k][:], in_=xd_in[k * 128:(k + 1) * 128,
                                                           c * TC:(c + 1) * TC])
                    K.inc(i, "lx", 16)
            lx_x = K.val("lx")

            @blk.vector
            def _(v):
                K.wait(v, "lx", lx_x)
            @blk.scalar
            def _(sc):
                K.wait(sc, "lx", lx_x)
            @blk.tensor
            def _(tx):
                K.wait(tx, "lx", lx_x)

            dv_x = emit_ln_x(K, XC, slice(0, TC), ACC, XHB, G1, P, PS_SUM[:], PS_SS[:],
                             MU, PS_MUR[:], PS_RSR[:], dv_prev, ones_sq=P["ONES_F"])

            for tt in range(TC // 128):
                @blk.tensor
                def _(tx, tt=tt):
                    K.wait(tx, "dv", dv_x)
                    K.wait(tx, "ac", setup_done)
                    K.wait(tx, "dv", K.val("dv"))
                    gb1 = cfg.get("gate_bias1")
                    for k in range(NDT):
                        i = tx.matmul(PS_H[0][0:128, tt * E_:(tt + 1) * E_],
                                      XHB[:, k * TC + tt * 128:k * TC + (tt + 1) * 128],
                                      RT[:, k * E_:(k + 1) * E_],
                                      start=(k == 0),
                                      stop=(k == NDT - 1 and gb1 is None))
                    if gb1 is not None:
                        i = tx.matmul(PS_H[0][0:128, tt * E_:(tt + 1) * E_],
                                      P["ONES_R"][:], gb1[:], start=False, stop=True)
                    K.inc(i, "mm")
                mmv = K.val("mm")

                @blk.vector
                def _(v, tt=tt, mmv=mmv):
                    K.wait(v, "mm", mmv)
                    i = v.tensor_copy(LG[:, tt * E_:(tt + 1) * E_],
                                      PS_H[0][0:128, tt * E_:(tt + 1) * E_])
                    K.inc(i, "dv")
            dv_lg = K.val("dv")

            if cfg.get("pre_gate"):
                dv_lg = cfg["pre_gate"](K, LG, SCR, MU, dv_lg)

            ac_gt = emit_gating(K, LG, GT, GTB, SCR, PS_RSR[:], P["IDF"], P["ONES_R"],
                                dv_lg, E_, "l1" if var == "l1" else "top2", scale,
                                keep_fn=cfg.get("keep_fn"))
            if c == 0 and globals().get("DEV_DBG") is not None:
                dbg = globals()["DEV_DBG"]
                acn = K.val("ac")
                @blk.sync
                def _(q, acn=acn):
                    K.wait(q, "ac", acn)
                    K.wait(q, "dv", K.val("dv"))
                    i = q.dma_start(out=dbg["gt"], in_=GT[:])
                    K.inc(i, "so", 16)
                    i = q.dma_start(out=dbg["gtb"], in_=GTB[:])
                    K.inc(i, "so", 16)
                    i = q.dma_start(out=dbg["lg"], in_=LG[:])
                    K.inc(i, "so", 16)
                    i = q.dma_start(out=dbg["scr"], in_=SCR[:])
                    K.inc(i, "so", 16)
            if globals().get("DEV_CONST_GATE"):
                @blk.vector
                def _(v):
                    K.wait(v, "ac", ac_gt)
                    v.memset(GT[:], 0.0625)
                    i = v.memset(GTB[:], 0.0625)
                    K.inc(i, "dv")
                dvg = K.val("dv")
                @blk.tensor
                def _(tx, dvg=dvg):
                    K.wait(tx, "dv", dvg)

            @blk.sync
            def _(q):
                K.wait(q, "mm", K.val("mm"))
                for j in range(nup):
                    load_up(q, j, 0, 0, 0)
                load_dn(q, 0, 0, 0)

            for e in range(E_):
                @blk.tensor
                def _(tx, e=e):
                    K.wait(tx, "ac", ac_gt)
                    K.wait(tx, "dv", K.val("dv"))
                    i = tx.matmul(PS_MUR[:], P["ONES_R"][:], GT[0:1, e * TC:(e + 1) * TC], start=True, stop=True)
                    K.inc(i, "mm")
                mm_g = K.val("mm")

                for grp in range(NGRP):
                    for oi in range(HGRP):
                        o = grp * HGRP + oi
                        piece, opos = o // 4, o % 4
                        uslot = piece % 2
                        hb = oi % 2
                        for j in range(nup):
                            @blk.tensor
                            def _(tx, j=j, o=o, piece=piece, opos=opos, uslot=uslot, hb=hb):
                                K.wait(tx, f"lu{j}", up_ready[(j, uslot)])
                                if j == 0 and hread[hb] > 0:
                                    K.wait(tx, "dv", hread[hb])
                                if j == 0 and hb == 0:
                                    K.wait(tx, "ac", K.val("ac"))
                                for k in range(NDT):
                                    i = tx.matmul(
                                        PS_H[hb][:] if nup == 1 else PS_H[j][:],
                                        WU[j][:, uslot * NDT * 512 + k * 512 + opos * 128:
                                              uslot * NDT * 512 + k * 512 + (opos + 1) * 128],
                                        XHB[:, k * TC:(k + 1) * TC],
                                        start=(k == 0), stop=(k == NDT - 1))
                                K.inc(i, "mm")
                        mm_h = K.val("mm")

                        a = ACT[:, oi * TC:(oi + 1) * TC]
                        if var == "l1":
                            @blk.vector
                            def _(v, e=e, o=o, hb=hb, mm_h=mm_h):
                                K.wait(v, "mm", mm_h)
                                v.tensor_scalar(a, PS_H[hb][:],
                                                cfg["b_up"][0][:, e * NH + o:e * NH + o + 1],
                                                0.0, ALU.add, ALU.max)
                                i = v.tensor_tensor(a, a, PS_MUR[:], ALU.mult)
                                hread[hb] = K.inc(i, "dv")
                        else:
                            sidx = 1 if var == "l2" else 0
                            oth = 1 - sidx
                            sp = oi % 2
                            s0 = S0[:, sp * TC:(sp + 1) * TC]
                            @blk.scalar
                            def _(sc, e=e, o=o, s0=s0, mm_h=mm_h):
                                K.wait(sc, "mm", mm_h)
                                if var == "l2":
                                    i = sc.activation(s0, PS_H[sidx][:], AF.Silu,
                                                      bias=cfg["b_up"][1][:, e * NH + o:e * NH + o + 1])
                                else:
                                    i = sc.activation(s0, PS_H[sidx][:], AF.Silu)
                                K.inc(i, "ac")
                            ac_s = K.val("ac")

                            @blk.vector
                            def _(v, e=e, o=o, oth=oth, s0=s0, ac_s=ac_s, mm_h=mm_h):
                                K.wait(v, "mm", mm_h)
                                K.wait(v, "ac", ac_s)
                                if var == "l2":
                                    v.scalar_tensor_tensor(a, PS_H[oth][:],
                                                           cfg["b_up"][0][:, e * NH + o:e * NH + o + 1],
                                                           s0, ALU.add, ALU.mult)
                                else:
                                    v.tensor_tensor(a, PS_H[oth][:], s0, ALU.mult)
                                i = v.tensor_tensor(a, a, PS_MUR[:], ALU.mult)
                                hread[0] = hread[1] = K.inc(i, "dv")
                        if opos == 3:
                            nxt = piece + 1
                            ne, npi = (e, nxt) if nxt < NPU else ((e + 1) % E_, 0)
                            if not (c == globals()['MOE_NCHUNK'] - 1 and e == E_ - 1 and nxt >= NPU):
                                mm_now = K.val("mm")
                                @blk.sync
                                def _(q, ne=ne, npi=npi, mm_now=mm_now):
                                    K.wait(q, "mm", mm_now)
                                    for j in range(nup):
                                        load_up(q, j, ne, npi, npi % 2)
                    dv_act = K.val("dv")

                    dslot = grp % 2
                    for o in range(NDT):
                        ob = o % 2
                        @blk.tensor
                        def _(tx, e=e, grp=grp, o=o, ob=ob, dslot=dslot, dv_act=dv_act):
                            K.wait(tx, "ldn", dn_ready[dslot])
                            K.wait(tx, "dv", dv_act)
                            if oread[ob] > 0:
                                K.wait(tx, "dv", oread[ob])
                            with_b2 = cfg.get("b_dn") is not None and grp == 0 and e == 0
                            for kk in range(HGRP):
                                i = tx.matmul(PS_O[ob][:],
                                              WD[:, dslot * DN_T * 1024 + kk * 1024 + o * 128:
                                                 dslot * DN_T * 1024 + kk * 1024 + (o + 1) * 128],
                                              ACT[:, kk * TC:(kk + 1) * TC],
                                              start=(kk == 0),
                                              stop=(kk == HGRP - 1 and not with_b2))
                            if with_b2:
                                i = tx.matmul(PS_O[ob][:],
                                              cfg["b_dn"][0:E_, o * 128:(o + 1) * 128],
                                              GTB[0:E_, :], start=False, stop=True)
                            K.inc(i, "mm")
                        mm_o = K.val("mm")

                        @blk.vector
                        def _(v, e=e, grp=grp, o=o, ob=ob, mm_o=mm_o):
                            K.wait(v, "mm", mm_o)
                            dst = ACC[:, o * TC:(o + 1) * TC]
                            if e == 0 and grp == 0:
                                i = v.tensor_copy(dst, PS_O[ob][:])
                            else:
                                i = v.tensor_tensor(dst, dst, PS_O[ob][:], ALU.add)
                            oread[ob] = K.inc(i, "dv")
                    ngrp = grp + 1
                    nde, ng = (e, ngrp) if ngrp < NGRP else ((e + 1) % E_, 0)
                    if not (c == globals()['MOE_NCHUNK'] - 1 and e == E_ - 1 and ngrp >= NGRP):
                        mm_now = K.val("mm")
                        @blk.sync
                        def _(q, nde=nde, ng=ng, mm_now=mm_now):
                            K.wait(q, "mm", mm_now)
                            load_dn(q, nde, ng, ng % 2)

            dv_out = K.val("dv")
            @blk.vector
            def _(v, dv_out=dv_out):
                K.wait(v, "dv", dv_out)
                for k in range(NDT):
                    i = v.tensor_tensor(XC[k][:], XC[k][:], ACC[:, k * TC:(k + 1) * TC],
                                        ALU.add)
                K.inc(i, "dv")
            dv_prev = K.val("dv")

            @blk.sync
            def _(q, c=c, dv_prev=dv_prev):
                K.wait(q, "dv", dv_prev)
                for k in range(NDT):
                    i = q.dma_start(out=xd_out[k * 128:(k + 1) * 128, c * TC:(c + 1) * TC],
                                    in_=XC[k][:])
                    K.inc(i, "so", 16)



def _build_l3ffn_launch():
    nc = bass.Bass("TRN2", target_bir_lowering=False, detect_race_conditions=False)
    xT = nc.dram_tensor("xT", [D, T], F32, kind="ExternalInput").ap()
    wr = nc.dram_tensor("wr", [D, 8], F32, kind="ExternalInput").ap()
    w0 = nc.dram_tensor("w0", [8, D, 2048], F32, kind="ExternalInput").ap()
    w1 = nc.dram_tensor("w1", [8, D, 2048], F32, kind="ExternalInput").ap()
    wo = nc.dram_tensor("wo", [8, 2048, D], F32, kind="ExternalInput").ap()
    idenf = nc.dram_tensor("idenf", [128, 128], F32, kind="ExternalInput").ap()
    yT = nc.dram_tensor("yT", [D, T], F32, kind="ExternalOutput").ap()
    c0 = nc.dram_tensor("c0", [8 * 4 * 128, NDT * 512], BF16, kind="Internal").ap()
    c1 = nc.dram_tensor("c1", [8 * 4 * 128, NDT * 512], BF16, kind="Internal").ap()
    cdn = nc.dram_tensor("cdn", [8 * 128, 16 * 1024], BF16, kind="Internal").ap()
    if globals().get("DEV_DBG") is not None:
        globals()["DEV_DBG"] = {
            "gt": nc.dram_tensor("dbg_gt", [1, 8 * TC], F32, kind="ExternalOutput").ap(),
            "gtb": nc.dram_tensor("dbg_gtb", [8, TC], BF16, kind="ExternalOutput").ap(),
            "lg": nc.dram_tensor("dbg_lg", [128, 4 * 8], F32, kind="ExternalOutput").ap(),
            "scr": nc.dram_tensor("dbg_scr", [128, 4 * 30], F32, kind="ExternalOutput").ap(),
        }

    with ExitStack() as stack:
        blk = stack.enter_context(nc.Block())
        K = Ctx(nc, blk, stack)
        for s_ in ("lx", "lw", "lu0", "lu1", "ldn", "lqw", "lqs", "mm", "dv", "ac", "so"):
            K.sem(s_)
        P0 = dict(
            ONES_F=K.sb("p_ONESF", [128, 1], F32),
            ONES_B=K.sb("p_ONESB", [128, 1], BF16),
            ONES_R=K.sb("p_ONESR", [1, 128], F32),
            IDF=K.sb("p_IDF", [128, 128], F32),
        )

        @blk.vector
        def _(v):
            v.memset(P0["ONES_F"][:], 1.0)
            v.memset(P0["ONES_B"][:], 1.0)
            i = v.memset(P0["ONES_R"][:], 1.0)
            K.inc(i, "dv")

        @blk.sync
        def _(q):
            i = q.dma_start(out=P0["IDF"][:], in_=idenf)
            K.inc(i, "lx", 16)

        emit_repack_up(K, w0, c0, 8, 2048)
        emit_repack_up(K, w1, c1, 8, 2048)
        emit_repack_dn(K, wo, cdn, 8, 2048, 1024)
        emit_fence(K)

        @blk.sync
        def _(q):
            K.wait(q, "lqs", K.val("lqs"))

        def setup(K2, RT):
            @blk.gpsimd
            def _(g):
                for k in range(NDT):
                    i = g.dma_start(out=RT[:, k * 8:(k + 1) * 8],
                                    in_=wr[k * 128:(k + 1) * 128, :])
                    K.inc(i, "lw", 16)
            lwv = K.val("lw")

            @blk.tensor
            def _(tx):
                K.wait(tx, "lw", lwv)
            @blk.vector
            def _(v):
                K.wait(v, "lw", lwv)
                K.wait(v, "lx", 16)
            return K.val("ac")

        cfg = dict(E=8, H=2048, var="l3", w_up=[c0, c1], w_dn=cdn, setup=setup, scale=0.5)
        emit_moe(K, xT, yT, P0, cfg)

        @blk.sync
        def _(q):
            K.wait(q, "so", K.val("so"))
    return nc


def dev_run_l3ffn(x, inp):
    nc = _build_l3ffn_launch()
    idenf = np.eye(128).astype(np.float32)
    in_maps = []
    for b in range(NCORES):
        in_maps.append({
            "xT": np.ascontiguousarray(x[b].T),
            "wr": np.asarray(inp["L3_ffn_w_router"]),
            "w0": np.asarray(inp["L3_ffn_w0"]),
            "w1": np.asarray(inp["L3_ffn_w1"]),
            "wo": np.asarray(inp["L3_ffn_wo"]),
            "idenf": idenf,
        })
    res = run_bass_kernel_spmd(nc, in_maps, core_ids=list(range(NCORES)))
    return np.stack([np.asarray(res.results[b]["yT"], np.float32).T for b in range(NCORES)])


def _build_l2ffn_launch():
    nc = bass.Bass("TRN2", target_bir_lowering=False, detect_race_conditions=False)
    xT = nc.dram_tensor("xT", [D, T], F32, kind="ExternalInput").ap()
    gw = nc.dram_tensor("gw", [8, D], F32, kind="ExternalInput").ap()
    gb = nc.dram_tensor("gb", [8], F32, kind="ExternalInput").ap()
    w0 = nc.dram_tensor("w0", [8, 2048, D], F32, kind="ExternalInput").ap()
    b0 = nc.dram_tensor("b0", [8, 2048], F32, kind="ExternalInput").ap()
    w1 = nc.dram_tensor("w1", [8, 2048, D], F32, kind="ExternalInput").ap()
    b1 = nc.dram_tensor("b1", [8, 2048], F32, kind="ExternalInput").ap()
    w2 = nc.dram_tensor("w2", [8, D, 2048], F32, kind="ExternalInput").ap()
    b2 = nc.dram_tensor("b2", [8, D], F32, kind="ExternalInput").ap()
    idenf = nc.dram_tensor("idenf", [128, 128], F32, kind="ExternalInput").ap()
    idenb = nc.dram_tensor("idenb", [128, 128], BF16, kind="ExternalInput").ap()
    yT = nc.dram_tensor("yT", [D, T], F32, kind="ExternalOutput").ap()
    c0 = nc.dram_tensor("c0", [8 * 4 * 128, NDT * 512], BF16, kind="Internal").ap()
    c1 = nc.dram_tensor("c1", [8 * 4 * 128, NDT * 512], BF16, kind="Internal").ap()
    cdn = nc.dram_tensor("cdn", [8 * 128, 16 * 1024], BF16, kind="Internal").ap()

    with ExitStack() as stack:
        blk = stack.enter_context(nc.Block())
        K = Ctx(nc, blk, stack)
        for s_ in ("lx", "lw", "lu0", "lu1", "ldn", "lqw", "lqs", "mm", "dv", "ac", "so"):
            K.sem(s_)
        P0 = dict(
            ONES_F=K.sb("p_ONESF", [128, 1], F32),
            ONES_B=K.sb("p_ONESB", [128, 1], BF16),
            ONES_R=K.sb("p_ONESR", [1, 128], F32),
            IDF=K.sb("p_IDF", [128, 128], F32),
        )
        IDB = K.sb("p_IDB", [128, 128], BF16)
        B0S = K.sb("p_B0S", [128, 128], F32)
        B1S = K.sb("p_B1S", [128, 128], F32)
        B2S = K.sb("p_B2S", [8, D], BF16)
        GWS = K.sb("p_GWS", [128, NDT * 8], BF16)
        GB1 = K.sb("p_GB1", [1, 8], F32)

        @blk.vector
        def _(v):
            v.memset(P0["ONES_F"][:], 1.0)
            v.memset(P0["ONES_B"][:], 1.0)
            i = v.memset(P0["ONES_R"][:], 1.0)
            K.inc(i, "dv")

        @blk.sync
        def _(q):
            i = q.dma_start(out=P0["IDF"][:], in_=idenf)
            K.inc(i, "lx", 16)
            i = q.dma_start(out=IDB[:], in_=idenb)
            K.inc(i, "lx", 16)

        @blk.gpsimd
        def _(g):
            for k in range(NDT):
                i = g.dma_start(out=GWS[:, k * 8:(k + 1) * 8],
                                in_=bass.AP(tensor=gw.tensor, offset=k * 128,
                                            ap=[[1, 128], [1024, 8]]),
                                allow_slow_non_contiguous=True)
                K.inc(i, "lw", 16)
            i = g.dma_start(out=B2S[:], in_=b2)
            K.inc(i, "lw", 16)
            i = g.dma_start(out=GB1[:], in_=bass.AP(tensor=gb.tensor, offset=0,
                                                    ap=[[8, 1], [1, 8]]))
            K.inc(i, "lw", 16)

        with ExitStack() as SS:
            En = SS.enter_context
            BST = En(nc.sbuf_tensor("p_BST", [128, 256], F32))
            PTB = En(nc.psum_tensor("p_PTB", [128, 256], F32))

            @blk.sync
            def _(q):
                i = q.dma_start(out=BST[:, 0:128], in_=b0.rearrange("e (o p) -> (e o) p", p=128))
                K.inc(i, "lx", 16)
                i = q.dma_start(out=BST[:, 128:256], in_=b1.rearrange("e (o p) -> (e o) p", p=128))
                K.inc(i, "lx", 16)
            lxv = K.val("lx")

            @blk.tensor
            def _(tx):
                K.wait(tx, "lx", lxv)
                tx.transpose(PTB[:, 0:128], BST[:, 0:128], P0["IDF"][:])
                i = tx.transpose(PTB[:, 128:256], BST[:, 128:256], P0["IDF"][:])
                K.inc(i, "mm")
            mmv = K.val("mm")

            @blk.scalar
            def _(sc):
                K.wait(sc, "mm", mmv)
                sc.copy(B0S[:], PTB[:, 0:128])
                i = sc.copy(B1S[:], PTB[:, 128:256])
                K.inc(i, "ac")

            emit_repack_tr(K, w0, c0, range(8), 2048, IDB)
            emit_repack_tr(K, w1, c1, range(8), 2048, IDB)
            emit_repack_tr_dn(K, w2, cdn, 8, 2048, IDB)
        emit_fence(K)

        @blk.sync
        def _(q):
            K.wait(q, "lqs", K.val("lqs"))

        def setup(K2, RT):
            lwv = K.val("lw")
            acv = K.val("ac")
            @blk.vector
            def _(v):
                K.wait(v, "lw", lwv)
                K.wait(v, "ac", acv)
                i = v.tensor_copy(RT[:], GWS[:])
                K.inc(i, "dv")
            @blk.tensor
            def _(tx):
                K.wait(tx, "dv", K.val("dv"))
            return K.val("ac")

        cfg = dict(E=8, H=2048, var="l2", w_up=[c0, c1], w_dn=cdn, setup=setup, scale=0.5,
                   b_up=[B0S, B1S], b_dn=B2S, gate_bias1=GB1)
        emit_moe(K, xT, yT, P0, cfg)

        @blk.sync
        def _(q):
            K.wait(q, "so", K.val("so"))
    return nc


def emit_repack_tr_dn(K, raw, cache, E_, H_, ident_bf):
    nc, blk = K.nc, K.blk
    NO = D // 128
    NI = H_ // 128
    _RPC[0] += 1
    uid = _RPC[0]
    with ExitStack() as SS:
        En = SS.enter_context
        ST = [En(nc.sbuf_tensor(f"rtd{uid}_st{i}", [128, H_], BF16)) for i in range(2)]
        PT = [En(nc.psum_tensor(f"rtd{uid}_pt{i}", [128, 128], BF16)) for i in range(2)]
        EV = En(nc.sbuf_tensor(f"rtd{uid}_ev", [128, 128], BF16))
        ld_done = {}
        st_done = {}
        pt_free = {0: 0, 1: 0}
        for e in range(E_):
            for ko in range(NO):
                idx = e * NO + ko
                p = idx % 2
                @blk.gpsimd
                def _(g, e=e, ko=ko, p=p, idx=idx):
                    if idx >= 2:
                        K.wait(g, "lqs", st_done[idx - 2])
                    i = g.dma_start(out=ST[p][:], in_=raw[e, ko * 128:(ko + 1) * 128, :])
                    ld_done[idx] = K.inc(i, "lqw", 16)

                for k in range(NI):
                    @blk.tensor
                    def _(tx, p=p, k=k, idx=idx):
                        if k == 0:
                            K.wait(tx, "lqw", ld_done[idx])
                        if pt_free[k % 2] > 0:
                            K.wait(tx, "ac", pt_free[k % 2])
                        i = tx.transpose(PT[k % 2][:], ST[p][:, k * 128:(k + 1) * 128],
                                         ident_bf[:])
                        K.inc(i, "mm")
                    mmv = K.val("mm")

                    @blk.scalar
                    def _(sc, k=k, mmv=mmv):
                        K.wait(sc, "mm", mmv)
                        K.wait(sc, "lqs", K.val("lqs"))
                        i = sc.copy(EV[:], PT[k % 2][:])
                        pt_free[k % 2] = K.inc(i, "ac")
                    acv = K.val("ac")

                    @blk.sync
                    def _(q, e=e, ko=ko, k=k, acv=acv, idx=idx):
                        K.wait(q, "ac", acv)
                        i = q.dma_start(
                            out=cache[e * 128:(e + 1) * 128,
                                      k * D + ko * 128:k * D + (ko + 1) * 128],
                            in_=EV[:])
                        st_done[idx] = K.inc(i, "lqs", 16)


def dev_run_l2ffn(x, inp):
    nc = _build_l2ffn_launch()
    in_maps = []
    for b in range(NCORES):
        in_maps.append({
            "xT": np.ascontiguousarray(x[b].T),
            "gw": np.asarray(inp["L2_ffn_gate_w"]), "gb": np.asarray(inp["L2_ffn_gate_b"]),
            "w0": np.asarray(inp["L2_ffn_w0"]), "b0": np.asarray(inp["L2_ffn_b0"]),
            "w1": np.asarray(inp["L2_ffn_w1"]), "b1": np.asarray(inp["L2_ffn_b1"]),
            "w2": np.asarray(inp["L2_ffn_w2"]), "b2": np.asarray(inp["L2_ffn_b2"]),
            "idenf": np.eye(128, dtype=np.float32),
            "idenb": np.eye(128).astype(ml_dtypes.bfloat16),
        })
    res = run_bass_kernel_spmd(nc, in_maps, core_ids=list(range(NCORES)))
    return np.stack([np.asarray(res.results[b]["yT"], np.float32).T for b in range(NCORES)])



def _build_l1ffn_launch():
    nc = bass.Bass("TRN2", target_bir_lowering=False, detect_race_conditions=False)
    xT = nc.dram_tensor("xT", [D, T], F32, kind="ExternalInput").ap()
    wr = nc.dram_tensor("wr", [D, 8], F32, kind="ExternalInput").ap()
    br = nc.dram_tensor("br", [8], F32, kind="ExternalInput").ap()
    w1 = nc.dram_tensor("w1", [8, D, 4096], F32, kind="ExternalInput").ap()
    b1 = nc.dram_tensor("b1", [8, 4096], F32, kind="ExternalInput").ap()
    w2 = nc.dram_tensor("w2", [8, 4096, D], F32, kind="ExternalInput").ap()
    b2 = nc.dram_tensor("b2", [8, D], F32, kind="ExternalInput").ap()
    idenf = nc.dram_tensor("idenf", [128, 128], F32, kind="ExternalInput").ap()
    yT = nc.dram_tensor("yT", [D, T], F32, kind="ExternalOutput").ap()
    c0 = nc.dram_tensor("c0", [8 * 8 * 128, NDT * 512], BF16, kind="Internal").ap()
    cdn = nc.dram_tensor("cdn", [8 * 128, 32 * 1024], BF16, kind="Internal").ap()

    with ExitStack() as stack:
        blk = stack.enter_context(nc.Block())
        K = Ctx(nc, blk, stack)
        for s_ in ("lx", "lw", "lu0", "ldn", "lqw", "lqs", "mm", "dv", "ac", "so"):
            K.sem(s_)
        P0 = dict(
            ONES_F=K.sb("p_ONESF", [128, 1], F32),
            ONES_B=K.sb("p_ONESB", [128, 1], BF16),
            ONES_R=K.sb("p_ONESR", [1, 128], F32),
            IDF=K.sb("p_IDF", [128, 128], F32),
        )
        B1S = K.sb("p_B1S", [128, 256], F32)
        B2S = K.sb("p_B2S", [8, D], BF16)
        GB1 = K.sb("p_GB1", [1, 8], F32)
        ONE8 = K.sb("p_ONE8", [8, 1], F32)
        EQT = K.sb("cap_EQT", [8, 2 * TC], F32)
        POS = K.sb("cap_POS", [8, 2 * TC], F32)
        CAR = K.sb("cap_CAR", [8, 2], F32)
        KV = K.sb("cap_KV", [1, 2 * TC], F32)
        KV2 = K.sb("cap_KV2", [1, 2 * TC], F32)
        K0T = K.sb("cap_K0T", [128, 2 * (TC // 128)], F32)

        @blk.vector
        def _(v):
            v.memset(P0["ONES_F"][:], 1.0)
            v.memset(P0["ONES_B"][:], 1.0)
            v.memset(P0["ONES_R"][:], 1.0)
            v.memset(ONE8[:], 1.0)
            i = v.memset(CAR[:], 0.0)
            K.inc(i, "dv")

        @blk.sync
        def _(q):
            i = q.dma_start(out=P0["IDF"][:], in_=idenf)
            K.inc(i, "lx", 16)

        @blk.gpsimd
        def _(g):
            i = g.dma_start(out=B2S[:], in_=b2)
            K.inc(i, "lw", 16)
            i = g.dma_start(out=GB1[:], in_=bass.AP(tensor=br.tensor, offset=0,
                                                    ap=[[8, 1], [1, 8]]))
            K.inc(i, "lw", 16)

        with ExitStack() as SS:
            En = SS.enter_context
            BST = En(nc.sbuf_tensor("p_BST", [128, 256], F32))
            PTB = En(nc.psum_tensor("p_PTB", [128, 256], F32))

            @blk.sync
            def _(q):
                src = b1.rearrange("e (o p) -> (e o) p", p=128)
                i = q.dma_start(out=BST[:, 0:128], in_=src[0:128, :])
                K.inc(i, "lx", 16)
                i = q.dma_start(out=BST[:, 128:256], in_=src[128:256, :])
                K.inc(i, "lx", 16)
            lxv = K.val("lx")

            @blk.tensor
            def _(tx):
                K.wait(tx, "lx", lxv)
                tx.transpose(PTB[:, 0:128], BST[:, 0:128], P0["IDF"][:])
                i = tx.transpose(PTB[:, 128:256], BST[:, 128:256], P0["IDF"][:])
                K.inc(i, "mm")
            mmv = K.val("mm")

            @blk.scalar
            def _(sc):
                K.wait(sc, "mm", mmv)
                sc.copy(B1S[:, 0:128], PTB[:, 0:128])
                i = sc.copy(B1S[:, 128:256], PTB[:, 128:256])
                K.inc(i, "ac")

            emit_repack_up(K, w1, c0, 8, 4096)
            emit_repack_dn(K, w2, cdn, 8, 4096, 1024)
        emit_fence(K)

        @blk.sync
        def _(q):
            K.wait(q, "lqs", K.val("lqs"))

        def setup(K2, RT):
            @blk.gpsimd
            def _(g):
                for k in range(NDT):
                    i = g.dma_start(out=RT[:, k * 8:(k + 1) * 8],
                                    in_=wr[k * 128:(k + 1) * 128, :])
                    K.inc(i, "lw", 16)
            lwv = K.val("lw")
            @blk.tensor
            def _(tx):
                K.wait(tx, "lw", lwv)
            @blk.vector
            def _(v):
                K.wait(v, "lw", lwv)
                K.wait(v, "lx", 16)
            return K.val("ac")

        def pre_gate(K2, LG, SCR, MU, dv_lg):
            NT_ = TC // 128
            NE = NT_ * 8
            mx = SCR[:, 3 * NE + 4 * NT_:3 * NE + 5 * NT_]
            sm = SCR[:, 3 * NE + 5 * NT_:3 * NE + 6 * NT_]

            @blk.vector
            def _(v):
                K.wait(v, "dv", dv_lg)
                for tt in range(NT_):
                    i = v.tensor_reduce(mx[:, tt:tt + 1], LG[:, tt * 8:(tt + 1) * 8],
                                        AX.X, ALU.max)
                K.inc(i, "dv")
                K.wait(v, "dv")
                for tt in range(NT_):
                    i = v.tensor_scalar(mx[:, tt:tt + 1], mx[:, tt:tt + 1], -1.0, None, ALU.mult)
                K.inc(i, "dv")
            dv1 = K.val("dv")

            @blk.scalar
            def _(sc):
                K.wait(sc, "dv", dv1)
                for tt in range(NT_):
                    i = sc.activation(LG[:, tt * 8:(tt + 1) * 8], LG[:, tt * 8:(tt + 1) * 8],
                                      AF.Exp, bias=mx[:, tt:tt + 1])
                K.inc(i, "ac")
            ac1 = K.val("ac")

            @blk.vector
            def _(v):
                K.wait(v, "ac", ac1)
                for tt in range(NT_):
                    i = v.tensor_reduce(sm[:, tt:tt + 1], LG[:, tt * 8:(tt + 1) * 8],
                                        AX.X, ALU.add)
                K.inc(i, "dv")
                K.wait(v, "dv")
                for tt in range(NT_):
                    i = v.reciprocal(sm[:, tt:tt + 1], sm[:, tt:tt + 1])
                K.inc(i, "dv")
                K.wait(v, "dv")
                for tt in range(NT_):
                    i = v.tensor_scalar(LG[:, tt * 8:(tt + 1) * 8], LG[:, tt * 8:(tt + 1) * 8],
                                        sm[:, tt:tt + 1], None, ALU.mult)
                K.inc(i, "dv")
            return K.val("dv")


        def keep_fn(K2, eqs, eq2s, dv2):
            NT_ = TC // 128
            for tt in range(NT_):
                for s_i, buf in ((0, eqs), (1, eq2s)):
                    @blk.tensor
                    def _(tx, tt=tt, s_i=s_i, buf=buf):
                        K.wait(tx, "dv", dv2)
                        K.wait(tx, "ac", K.val("ac"))
                        i = tx.transpose(K.moe_ps_scratch[0:8, 0:128], buf[:, tt * 8:(tt + 1) * 8],
                                         P0["IDF"][:])
                        K.inc(i, "mm")
                    mmv = K.val("mm")

                    @blk.scalar
                    def _(sc, tt=tt, s_i=s_i, mmv=mmv):
                        K.wait(sc, "mm", mmv)
                        dst = bass.AP(tensor=EQT.ap().tensor, offset=tt * 256 + s_i,
                                      ap=[[2 * TC, 8], [2, 128]])
                        i = sc.copy(dst, K.moe_ps_scratch[0:8, 0:128])
                        K.inc(i, "ac")
            ac_eqt = K.val("ac")

            @blk.vector
            def _(v):
                K.wait(v, "ac", ac_eqt)
                i = v.tensor_tensor_scan(POS[:], EQT[:].broadcast_to((8, 2 * TC)) if False
                                         else bass.AP(tensor=P0["ONES_F"].ap().tensor,
                                                      offset=0, ap=[[1, 8], [0, 2 * TC]]),
                                         EQT[:], CAR[:, 0:1], ALU.mult, ALU.add)
                K.inc(i, "dv")
                K.wait(v, "dv")
                v.tensor_copy(CAR[:, 0:1], POS[:, 2 * TC - 1:2 * TC])
                i = v.tensor_tensor(POS[:], POS[:], EQT[:], ALU.subtract)
                K.inc(i, "dv")
                K.wait(v, "dv")
                i = v.tensor_scalar(POS[:], POS[:], 640.0, None, ALU.is_lt)
                K.inc(i, "dv")
                K.wait(v, "dv")
                i = v.tensor_tensor(POS[:], POS[:], EQT[:], ALU.mult)
                K.inc(i, "dv")
            dv_k = K.val("dv")

            @blk.tensor
            def _(tx):
                K.wait(tx, "dv", dv_k)
                K.wait(tx, "dv", K.val("dv"))
                for cc in range(2 * TC // 512):
                    i = tx.matmul(K.moe_ps_scratch[0:1, 0:512] if False else K.moe_ps_scratch[0:1, 0:512],
                                  ONE8[:], POS[:, cc * 512:(cc + 1) * 512],
                                  start=True, stop=True)
                    K.inc(i, "mm")
            return None
        def keep_fn2(K2, eqs, eq2s, dv2):
            NT_ = TC // 128
            for tt in range(NT_):
                for s_i, buf in ((0, eqs), (1, eq2s)):
                    @blk.tensor
                    def _(tx, tt=tt, s_i=s_i, buf=buf):
                        K.wait(tx, "dv", dv2)
                        K.wait(tx, "ac", K.val("ac"))
                        i = tx.transpose(K.moe_ps_scratch[0:8, 0:128], buf[:, tt * 8:(tt + 1) * 8],
                                         P0["IDF"][:])
                        K.inc(i, "mm")
                    mmv = K.val("mm")

                    @blk.scalar
                    def _(sc, tt=tt, s_i=s_i, mmv=mmv):
                        K.wait(sc, "mm", mmv)
                        dst = bass.AP(tensor=EQT.ap().tensor, offset=tt * 256 + s_i,
                                      ap=[[2 * TC, 8], [2, 128]])
                        i = sc.copy(dst, K.moe_ps_scratch[0:8, 0:128])
                        K.inc(i, "ac")
            ac_eqt = K.val("ac")

            @blk.vector
            def _(v):
                K.wait(v, "ac", ac_eqt)
                ones_b = bass.AP(tensor=P0["ONES_F"].ap().tensor, offset=0,
                                 ap=[[1, 8], [0, 2 * TC]])
                i = v.tensor_tensor_scan(POS[:], ones_b, EQT[:], CAR[:, 0:1],
                                         ALU.mult, ALU.add)
                K.inc(i, "dv")
                K.wait(v, "dv")
                v.tensor_copy(CAR[:, 0:1], POS[:, 2 * TC - 1:2 * TC])
                i = v.tensor_tensor(POS[:], POS[:], EQT[:], ALU.subtract)
                K.inc(i, "dv")
                K.wait(v, "dv")
                i = v.tensor_scalar(POS[:], POS[:], 640.0, None, ALU.is_lt)
                K.inc(i, "dv")
                K.wait(v, "dv")
                i = v.tensor_tensor(POS[:], POS[:], EQT[:], ALU.mult)
                K.inc(i, "dv")
            dv_k = K.val("dv")

            kv_done = []
            for cc in range(2 * TC // 512):
                @blk.tensor
                def _(tx, cc=cc):
                    K.wait(tx, "dv", dv_k)
                    K.wait(tx, "dv", K.val("dv"))
                    i = tx.matmul(K.moe_ps_scratch[0:1, 0:512], ONE8[:], POS[:, cc * 512:(cc + 1) * 512],
                                  start=True, stop=True)
                    K.inc(i, "mm")
                mmv = K.val("mm")

                @blk.vector
                def _(v, cc=cc, mmv=mmv):
                    K.wait(v, "mm", mmv)
                    i = v.tensor_copy(KV[:, cc * 512:(cc + 1) * 512], K.moe_ps_scratch[0:1, 0:512])
                    K.inc(i, "dv")
            dv_kv = K.val("dv")

            NT2 = TC // 128
            @blk.vector
            def _(v):
                K.wait(v, "dv", dv_kv)
                v.tensor_copy(KV2[0:1, 0:TC],
                              bass.AP(tensor=KV.ap().tensor, offset=0, ap=[[2 * TC, 1], [2, TC]]))
                i = v.tensor_copy(KV2[0:1, TC:2 * TC],
                                  bass.AP(tensor=KV.ap().tensor, offset=1, ap=[[2 * TC, 1], [2, TC]]))
                K.inc(i, "dv")
            dv_kv2 = K.val("dv")
            for tt in range(NT2):
                for s_i in range(2):
                    @blk.tensor
                    def _(tx, tt=tt, s_i=s_i):
                        K.wait(tx, "dv", dv_kv2)
                        K.wait(tx, "ac", K.val("ac"))
                        lhs = KV2[0:1, s_i * TC + tt * 128:s_i * TC + (tt + 1) * 128]
                        i = tx.matmul(K.moe_ps_scratch[0:128, 128 + s_i * NT2 + tt:129 + s_i * NT2 + tt],
                                      lhs, P0["ONES_R"][0:1, 0:1], start=True, stop=True)
                        K.inc(i, "mm")
                    mmv = K.val("mm")

                    @blk.scalar
                    def _(sc, tt=tt, s_i=s_i, mmv=mmv):
                        K.wait(sc, "mm", mmv)
                        i = sc.copy(K0T[:, s_i * NT2 + tt:s_i * NT2 + tt + 1],
                                    K.moe_ps_scratch[0:128, 128 + s_i * NT2 + tt:129 + s_i * NT2 + tt])
                        K.inc(i, "ac")
            acv = K.val("ac")

            @blk.vector
            def _(v):
                K.wait(v, "ac", acv)
            return K.val("dv"), K0T[:, 0:NT2], K0T[:, NT2:2 * NT2]

        cfg = dict(E=8, H=4096, var="l1", w_up=[c0], w_dn=cdn, setup=setup, scale=0.5,
                   b_up=[B1S], b_dn=B2S, gate_bias1=GB1, pre_gate=pre_gate,
                   keep_fn=keep_fn2)
        emit_moe(K, xT, yT, P0, cfg)

        @blk.sync
        def _(q):
            K.wait(q, "so", K.val("so"))
    return nc


def dev_run_l1ffn(x, inp):
    nc = _build_l1ffn_launch()
    in_maps = []
    for b in range(NCORES):
        in_maps.append({
            "xT": np.ascontiguousarray(x[b].T),
            "wr": np.asarray(inp["L1_ffn_Wr"]), "br": np.asarray(inp["L1_ffn_br"]),
            "w1": np.asarray(inp["L1_ffn_W1"]), "b1": np.asarray(inp["L1_ffn_b1"]),
            "w2": np.asarray(inp["L1_ffn_W2"]), "b2": np.asarray(inp["L1_ffn_b2"]),
            "idenf": np.eye(128, dtype=np.float32),
        })
    res = run_bass_kernel_spmd(nc, in_maps, core_ids=list(range(NCORES)))
    return np.stack([np.asarray(res.results[b]["yT"], np.float32).T for b in range(NCORES)])



def _build_l0attn_launch():
    nc = bass.Bass("TRN2", target_bir_lowering=False, detect_race_conditions=False)
    xT = nc.dram_tensor("xT", [D, T], F32, kind="ExternalInput").ap()
    wq = nc.dram_tensor("wq", [D, D], F32, kind="ExternalInput").ap()
    bq = nc.dram_tensor("bq", [D], F32, kind="ExternalInput").ap()
    wk = nc.dram_tensor("wk", [D, D], F32, kind="ExternalInput").ap()
    bk = nc.dram_tensor("bk", [D], F32, kind="ExternalInput").ap()
    wv = nc.dram_tensor("wv", [D, D], F32, kind="ExternalInput").ap()
    bv = nc.dram_tensor("bv", [D], F32, kind="ExternalInput").ap()
    wo = nc.dram_tensor("wo", [D, D], F32, kind="ExternalInput").ap()
    bo = nc.dram_tensor("bo", [D], F32, kind="ExternalInput").ap()
    cmask = nc.dram_tensor("cmask", [4 * 128, 512], BF16, kind="ExternalInput").ap()
    onesd = nc.dram_tensor("onesd", [1, T], BF16, kind="ExternalInput").ap()
    yT = nc.dram_tensor("yT", [D, T], F32, kind="ExternalOutput").ap()
    aod = nc.dram_tensor("aod", [D, T], BF16, kind="Internal").ap()

    HG = 2
    NQB = T // 512
    NTT = T // 128

    with ExitStack() as stack:
        blk = stack.enter_context(nc.Block())
        K = Ctx(nc, blk, stack)
        for s_ in ("lx", "lw", "lwq", "gp", "mm", "dv", "ac", "so"):
            K.sem(s_)
        ONES_F = K.sb("a_ONESF", [128, 1], F32)
        ONES_B = K.sb("a_ONESB", [128, 1], BF16)
        ONES_R = K.sb("a_ONESR", [1, 128], F32)
        XHB = K.sb("a_XHB", [128, NDT * T], BF16)
        AOS = K.sb("a_AOS", [64, 512], BF16)
        AOC = K.sb("a_AOC", [128, NDT * TC], BF16)
        QH = [K.sb(f"a_QH{i}", [66, T], BF16) for i in range(HG)]
        KH = [K.sb(f"a_KH{i}", [66, T], BF16) for i in range(HG)]
        VT = K.sb("a_VT", [128, NTT * (65 * HG)], BF16)
        WB = K.sb("a_WB", [128, NDT * 128], BF16)
        PTS = K.sb("a_PTS", [128, 512], BF16)
        PVS = K.sb("a_PVS", [65, 512], BF16)
        RL = K.sb("a_RL", [1, 512], F32)
        NRM = K.sb("a_NRM", [1, T], BF16)
        NR2 = K.sb("a_NR2", [1, T], BF16)
        CM = K.sb("a_CM", [128, 4 * 512], BF16)
        XC = K.sb("a_XC", [128, TC], F32)
        MU = K.sb("a_MU", [1, TC], F32)
        G1 = K.sb("a_G1", [128, NDT * TC], F32)
        SQB = K.sb("a_SQB", [128, NDT * TC], BF16)
        BQK = K.sb("a_BQK", [128, 2], F32)
        KMX = K.sb("a_KMX", [1, 1], F32)
        BOS = K.sb("a_BOS", [128, 8], F32)
        BVB = K.sb("a_BVB", [128, 8], BF16)
        WOB = K.sb("a_WOB", [128, NDT * 1024], BF16)
        PS_SUM = stack.enter_context(nc.psum_tensor("a_ps_sum", [1, TC], F32))
        PS_SS = stack.enter_context(nc.psum_tensor("a_ps_ss", [1, TC], F32))
        PS_MUR = stack.enter_context(nc.psum_tensor("a_ps_mur", [128, TC], F32))
        PS_RSR = stack.enter_context(nc.psum_tensor("a_ps_rsr", [128, TC], F32))
        PS_A = [stack.enter_context(nc.psum_tensor(f"a_ps_a{i}", [128, 512], F32)) for i in range(2)]
        PS_B = [stack.enter_context(nc.psum_tensor(f"a_ps_b{i}", [65, 512], F32)) for i in range(2)]
        P0 = dict(ONES_F=ONES_F, ONES_B=ONES_B, ONES_R=ONES_R)
        slopes = [2.0 ** (-0.5 * (i + 1)) for i in range(16)]

        @blk.vector
        def _(v):
            v.memset(ONES_F[:], 1.0)
            v.memset(ONES_B[:], 1.0)
            i = v.memset(ONES_R[:], 1.0)
            K.inc(i, "dv")

        @blk.sync
        def _(q):
            i = q.dma_start(out=CM[:], in_=bass.AP(tensor=cmask.tensor, offset=0,
                                                   ap=[[512, 128], [128 * 512, 4], [1, 512]]))
            K.inc(i, "lx", 16)
            i = q.dma_start(out=BOS[:], in_=bass.AP(tensor=bo.tensor, offset=0,
                                                    ap=[[1, 128], [128, 8]]),
                            allow_slow_non_contiguous=True)
            K.inc(i, "lx", 16)

        @blk.gpsimd
        def _(g):
            i = g.dma_start(out=BVB[:], in_=bass.AP(tensor=bv.tensor, offset=0,
                                                    ap=[[1, 128], [128, 8]]),
                            allow_slow_non_contiguous=True)
            K.inc(i, "lw", 16)

        dv_prev = K.val("dv")
        for c in range(NCHUNK):
            @blk.sync
            def _(q, c=c):
                K.wait(q, "mm", K.val("mm"))
                K.wait(q, "ac", K.val("ac"))
                K.wait(q, "dv", K.val("dv"))
                for k in range(NDT):
                    i = q.dma_start(out=G1[:, k * TC:(k + 1) * TC],
                                    in_=xT[k * 128:(k + 1) * 128, c * TC:(c + 1) * TC])
                    K.inc(i, "lx", 16)
            lxv = K.val("lx")
            for eng in ("vector", "scalar", "tensor"):
                @getattr(blk, eng)
                def _(e):
                    K.wait(e, "lx", lxv)
            XCl = [G1[:, k * TC:(k + 1) * TC] for k in range(NDT)]
            mm_rep = emit_ln_stats(K, XCl, slice(0, TC), ONES_F, SQB, ONES_B,
                                   PS_SUM[:], PS_SS[:], MU, PS_SUM[:],
                                   PS_MUR[:], PS_RSR[:], ONES_R,
                                   dep_data=[("dv", dv_prev)])

            @blk.vector
            def _(v, c=c, mm_rep=mm_rep):
                K.wait(v, "mm", mm_rep)
                for k in range(NDT):
                    dst = XHB[:, k * T + c * TC:k * T + (c + 1) * TC]
                    v.tensor_tensor(XC[:], G1[:, k * TC:(k + 1) * TC], PS_MUR[:], ALU.subtract)
                    i = v.tensor_tensor(dst, XC[:], PS_RSR[:], ALU.mult)
                K.inc(i, "dv")
            dv_prev = K.val("dv")
        dv_ln = K.val("dv")

        psA_free = [None, None]
        psB_free = [None, None]

        for g in range(16 // HG):
            h0 = g * HG
            for j, (wsrc, bsrc, kind) in enumerate(((wq, bq, "q"), (wk, bk, "k"),
                                                    (wv, bv, "v"))):
                mm_cons = K.val("mm")

                @blk.gpsimd
                def _(gq, j=j, wsrc=wsrc, bsrc=bsrc, mm_cons=mm_cons):
                    K.wait(gq, "mm", mm_cons)
                    for k in range(NDT):
                        i = gq.dma_start(
                            out=WB[:, k * 128:k * 128 + 64 * HG],
                            in_=bass.AP(tensor=wsrc.tensor, offset=k * 128 * 1024 + h0 * 64,
                                        ap=[[1024, 128], [1, 64 * HG]]))
                        K.inc(i, "lwq", 16)
                    if kind != "v":
                        i = gq.dma_start(out=BQK[:, j:j + 1],
                                         in_=bass.AP(tensor=bsrc.tensor, offset=h0 * 64,
                                                     ap=[[1, 64 * HG], [1, 1]]))
                        K.inc(i, "lwq", 16)
                lw_v = K.val("lwq")

                if kind != "v":
                    dsts = QH if kind == "q" else KH
                    for tb in range(NQB):
                        par = tb % 2
                        @blk.tensor
                        def _(tx, tb=tb, par=par, lw_v=lw_v):
                            K.wait(tx, "lwq", lw_v)
                            K.wait(tx, "dv", dv_ln)
                            if psA_free[par]:
                                K.wait(tx, psA_free[par][0], psA_free[par][1])
                            for k in range(NDT):
                                i = tx.matmul(PS_A[par][0:64 * HG, :],
                                              WB[:, k * 128:k * 128 + 64 * HG],
                                              XHB[:, k * T + tb * 512:k * T + (tb + 1) * 512],
                                              start=(k == 0), stop=(k == NDT - 1))
                            K.inc(i, "mm")
                        mmv = K.val("mm")

                        @blk.vector
                        def _(v, tb=tb, par=par, mmv=mmv, dsts=dsts, j=j):
                            K.wait(v, "mm", mmv)
                            K.wait(v, "lwq", lw_v)
                            for hh in range(HG):
                                i = v.tensor_scalar(
                                    dsts[hh][0:64, tb * 512:(tb + 1) * 512],
                                    PS_A[par][hh * 64:(hh + 1) * 64, :],
                                    BQK[hh * 64:(hh + 1) * 64, j:j + 1], None, ALU.add)
                            psA_free[par] = ("dv", K.inc(i, "dv"))
                else:
                    for ti in range(NTT):
                        par = ti % 2
                        @blk.tensor
                        def _(tx, ti=ti, par=par, lw_v=lw_v):
                            K.wait(tx, "lwq", lw_v)
                            K.wait(tx, "dv", dv_ln)
                            if psA_free[par]:
                                K.wait(tx, psA_free[par][0], psA_free[par][1])
                            for k in range(NDT):
                                i = tx.matmul(PS_A[par][0:128, 0:64 * HG],
                                              XHB[:, k * T + ti * 128:k * T + (ti + 1) * 128],
                                              WB[:, k * 128:k * 128 + 64 * HG],
                                              start=(k == 0), stop=(k == NDT - 1))
                            K.inc(i, "mm")
                        mmv = K.val("mm")

                        @blk.vector
                        def _(v, ti=ti, par=par, mmv=mmv):
                            K.wait(v, "mm", mmv)
                            for hh in range(HG):
                                i = v.tensor_copy(
                                    VT[:, ti * 65 * HG + hh * 65:ti * 65 * HG + hh * 65 + 64],
                                    PS_A[par][0:128, hh * 64:(hh + 1) * 64])
                            for hh in range(HG):
                                i = v.memset(VT[:, ti * 65 * HG + hh * 65 + 64:
                                                ti * 65 * HG + hh * 65 + 65], 1.0)
                            psA_free[par] = ("dv", K.inc(i, "dv"))
            dv_proj = K.val("dv")

            @blk.gpsimd
            def _(gq, dv_proj=dv_proj):
                K.wait(gq, "dv", dv_proj)
                for hh in range(HG):
                    i = gq.iota(KH[hh][64:65, :], pattern=[[1, T]], base=0,
                                channel_multiplier=0,
                                allow_small_or_imprecise_dtypes=True)
                    K.inc(i, "gp")
            gp_iota = K.val("gp")

            for hh in range(HG):
                @blk.scalar
                def _(sc, hh=hh):
                    K.wait(sc, "dv", dv_proj)
                    K.wait(sc, "mm", K.val("mm"))
                    i = sc.activation(SQB[0:64, 0:T], QH[hh][0:64, :], AF.Square)
                    K.inc(i, "ac")
                ac_sq = K.val("ac")
                for tb in range(NQB):
                    @blk.tensor
                    def _(tx, tb=tb, ac_sq=ac_sq):
                        K.wait(tx, "ac", ac_sq)
                        K.wait(tx, "dv", K.val("dv"))
                        i = tx.matmul(PS_B[0][0:1, :], ONES_B[0:64, :],
                                      SQB[0:64, tb * 512:(tb + 1) * 512], start=True, stop=True)
                        K.inc(i, "mm")
                    mmv = K.val("mm")

                    @blk.vector
                    def _(v, tb=tb, mmv=mmv):
                        K.wait(v, "mm", mmv)
                        i = v.tensor_copy(NRM[:, tb * 512:(tb + 1) * 512], PS_B[0][0:1, :])
                        K.inc(i, "dv")
                dv_q2 = K.val("dv")

                @blk.scalar
                def _(sc, dv_q2=dv_q2):
                    K.wait(sc, "dv", dv_q2)
                    i = sc.activation(NRM[:, :], NRM[:, :], AF.Sqrt)
                    K.inc(i, "ac")
                ac_nq = K.val("ac")

                @blk.scalar
                def _(sc, hh=hh, ac_nq=ac_nq):
                    K.wait(sc, "ac", ac_nq)
                    K.wait(sc, "mm", K.val("mm"))
                    i = sc.activation(SQB[0:64, 0:T], KH[hh][0:64, :], AF.Square)
                    K.inc(i, "ac")
                ac_sk = K.val("ac")
                for tb in range(NQB):
                    @blk.tensor
                    def _(tx, tb=tb, ac_sk=ac_sk):
                        K.wait(tx, "ac", ac_sk)
                        K.wait(tx, "dv", K.val("dv"))
                        i = tx.matmul(PS_B[0][0:1, :], ONES_B[0:64, :],
                                      SQB[0:64, tb * 512:(tb + 1) * 512], start=True, stop=True)
                        K.inc(i, "mm")
                    mmv = K.val("mm")

                    @blk.vector
                    def _(v, tb=tb, mmv=mmv):
                        K.wait(v, "mm", mmv)
                        i = v.tensor_copy(NR2[:, tb * 512:(tb + 1) * 512], PS_B[0][0:1, :])
                        K.inc(i, "dv")
                dv_k2 = K.val("dv")

                @blk.vector
                def _(v, dv_k2=dv_k2):
                    K.wait(v, "dv", dv_k2)
                    i = v.tensor_reduce(NR2[:, 0:1], NR2[:, :], AX.X, ALU.max)
                    K.inc(i, "dv")
                dv_km = K.val("dv")

                @blk.scalar
                def _(sc, dv_km=dv_km):
                    K.wait(sc, "dv", dv_km)
                    i = sc.activation(KMX[:], NR2[:, 0:1], AF.Sqrt)
                    K.inc(i, "ac")
                ac_km = K.val("ac")

                @blk.vector
                def _(v, hh=hh, ac_km=ac_km):
                    K.wait(v, "ac", ac_km)
                    i = v.tensor_scalar(NRM[:, :], NRM[:, :], KMX[:], None, ALU.mult)
                    K.inc(i, "dv")
                    K.wait(v, "dv")
                    i = v.tensor_scalar(NRM[:, :], NRM[:, :], -1.0, -1.0, ALU.mult, ALU.add)
                    K.inc(i, "dv")
                    K.wait(v, "dv")
                    i = v.memset(NR2[:, :], -8.0 * slopes[h0 + hh])
                    K.inc(i, "dv")
                dv_rows = K.val("dv")

                @blk.sync
                def _(q, hh=hh, dv_rows=dv_rows):
                    K.wait(q, "dv", dv_rows)
                    i = q.dma_start(out=QH[hh][65:66, :], in_=NRM[0:1, :])
                    K.inc(i, "lx", 16)
                    i = q.dma_start(out=QH[hh][64:65, :], in_=NR2[0:1, :])
                    K.inc(i, "lx", 16)
                    i = q.dma_start(out=KH[hh][65:66, :], in_=onesd)
                    K.inc(i, "lx", 16)
            lx_rows = K.val("lx")

            for hh in range(HG):
                h = h0 + hh
                for qb in range(NQB):
                    nkt = (qb + 1) * 4
                    bq_ = qb % 2
                    @blk.tensor
                    def _(tx):
                        K.wait(tx, "lx", lx_rows)
                        K.wait(tx, "gp", gp_iota)
                        K.wait(tx, "dv", K.val("dv"))
                        K.wait(tx, "ac", K.val("ac"))
                    for kt in range(nkt):
                        par = kt % 2
                        @blk.tensor
                        def _(tx, hh=hh, qb=qb, kt=kt, par=par):
                            if psA_free[par]:
                                K.wait(tx, psA_free[par][0], psA_free[par][1])
                            i = tx.matmul(PS_A[par][:, :],
                                          KH[hh][:, kt * 128:(kt + 1) * 128],
                                          QH[hh][:, qb * 512:(qb + 1) * 512],
                                          start=True, stop=True)
                            K.inc(i, "mm")
                        mmv = K.val("mm")

                        diag = (kt * 128 >= qb * 512)
                        @blk.scalar
                        def _(sc, kt=kt, par=par, mmv=mmv):
                            K.wait(sc, "mm", mmv)
                            K.wait(sc, "mm", K.val("mm"))
                            i = sc.activation(PTS[:, :], PS_A[par][:, :], AF.Exp, scale=0.125)
                            psA_free[par] = ("ac", K.inc(i, "ac"))
                        acv = K.val("ac")

                        dvv = 0
                        if diag:
                            dk = (kt * 128 - qb * 512) // 128
                            @blk.vector
                            def _(v, dk=dk, acv=acv):
                                K.wait(v, "ac", acv)
                                i = v.tensor_tensor(PTS[:, :], PTS[:, :],
                                                    CM[:, dk * 512:(dk + 1) * 512], ALU.mult)
                                K.inc(i, "dv")
                            dvv = K.val("dv")

                        @blk.tensor
                        def _(tx, hh=hh, kt=kt, nkt=nkt, bq_=bq_, dvv=dvv, acv=acv):
                            if dvv > 0:
                                K.wait(tx, "dv", dvv)
                            K.wait(tx, "ac", acv)
                            if kt == 0 and psB_free[bq_]:
                                K.wait(tx, psB_free[bq_][0], psB_free[bq_][1])
                            i = tx.matmul(PS_B[bq_][:, :],
                                          VT[:, kt * 65 * HG + hh * 65:kt * 65 * HG + (hh + 1) * 65],
                                          PTS[:, :],
                                          start=(kt == 0), stop=(kt == nkt - 1))
                            K.inc(i, "mm")
                    mm_pv = K.val("mm")

                    @blk.scalar
                    def _(sc, bq_=bq_, mm_pv=mm_pv):
                        K.wait(sc, "mm", mm_pv)
                        i = sc.copy(PVS[:], PS_B[bq_][:, :])
                        psB_free[bq_] = ("ac", K.inc(i, "ac"))
                    ac_pv = K.val("ac")

                    @blk.vector
                    def _(v, ac_pv=ac_pv):
                        K.wait(v, "ac", ac_pv)
                        i = v.reciprocal(RL[:], PVS[64:65, :])
                        K.inc(i, "dv")
                    dv_rl = K.val("dv")

                    @blk.tensor
                    def _(tx, par=0, dv_rl=dv_rl):
                        K.wait(tx, "dv", dv_rl)
                        if psA_free[0]:
                            K.wait(tx, psA_free[0][0], psA_free[0][1])
                        i = tx.matmul(PS_A[0][0:64, :], ONES_R[0:1, 0:64], RL[:],
                                      start=True, stop=True)
                        K.inc(i, "mm")
                    mm_rl = K.val("mm")

                    @blk.vector
                    def _(v, mm_rl=mm_rl):
                        K.wait(v, "mm", mm_rl)
                        K.wait(v, "so", K.val("so"))
                        i = v.tensor_tensor(AOS[:], PVS[0:64, :], PS_A[0][0:64, :], ALU.mult)
                        psA_free[0] = ("dv", K.inc(i, "dv"))
                    dv_ao = K.val("dv")

                    @blk.sync
                    def _(q, h=h, qb=qb, dv_ao=dv_ao):
                        K.wait(q, "dv", dv_ao)
                        i = q.dma_start(out=aod[h * 64:(h + 1) * 64, qb * 512:(qb + 1) * 512],
                                        in_=AOS[:])
                        K.inc(i, "so", 16)
        dv_attn = K.val("dv")
        so_attn = K.val("so")

        @blk.gpsimd
        def _(gq):
            K.wait(gq, "mm", K.val("mm"))
            for k in range(NDT):
                i = gq.dma_start(out=WOB[:, k * 1024:(k + 1) * 1024],
                                 in_=wo[k * 128:(k + 1) * 128, :])
                K.inc(i, "lwq", 16)
        lw_wo = K.val("lwq")

        for c in range(NCHUNK):
            @blk.sync
            def _(q, c=c):
                K.wait(q, "so", so_attn)
                K.wait(q, "dv", K.val("dv"))
                K.wait(q, "mm", K.val("mm"))
                for k in range(NDT):
                    i = q.dma_start(out=AOC[:, k * TC:(k + 1) * TC],
                                    in_=aod[k * 128:(k + 1) * 128, c * TC:(c + 1) * TC])
                    K.inc(i, "lx", 16)
                for k in range(NDT):
                    i = q.dma_start(out=G1[:, k * TC:(k + 1) * TC],
                                    in_=xT[k * 128:(k + 1) * 128, c * TC:(c + 1) * TC])
                    K.inc(i, "lx", 16)
            lx_xc = K.val("lx")

            for ob in range(NDT):
                par = ob % 2
                @blk.tensor
                def _(tx, ob=ob, par=par, lx_xc=lx_xc):
                    K.wait(tx, "lwq", lw_wo)
                    K.wait(tx, "lx", lx_xc)
                    K.wait(tx, "lw", 16)
                    K.wait(tx, "dv", K.val("dv"))
                    for k in range(NDT):
                        tx.matmul(PS_A[par][:, :], WOB[:, k * 1024 + ob * 128:k * 1024 + (ob + 1) * 128],
                                  AOC[:, k * TC:(k + 1) * TC],
                                  start=(k == 0), stop=False)
                    for k in range(NDT):
                        i = tx.matmul(PS_A[par][:, :], WOB[:, k * 1024 + ob * 128:k * 1024 + (ob + 1) * 128],
                                      bass.AP(tensor=BVB.ap().tensor, offset=k,
                                              ap=[[8, 128], [0, TC]]),
                                      start=False, stop=(k == NDT - 1))
                    K.inc(i, "mm")
                mm_o = K.val("mm")

                @blk.vector
                def _(v, ob=ob, par=par, mm_o=mm_o):
                    K.wait(v, "mm", mm_o)
                    i = v.scalar_tensor_tensor(G1[:, ob * TC:(ob + 1) * TC], PS_A[par][:, :],
                                               BOS[:, ob:ob + 1],
                                               G1[:, ob * TC:(ob + 1) * TC], ALU.add, ALU.add)
                    K.inc(i, "dv")
            dv_o = K.val("dv")

            @blk.sync
            def _(q, c=c, dv_o=dv_o):
                K.wait(q, "dv", dv_o)
                for k in range(NDT):
                    i = q.dma_start(out=yT[k * 128:(k + 1) * 128, c * TC:(c + 1) * TC],
                                    in_=G1[:, k * TC:(k + 1) * TC])
                    K.inc(i, "so", 16)

        @blk.sync
        def _(q):
            K.wait(q, "so", K.val("so"))
    return nc


def dev_run_l0attn(x, inp):
    nc = _build_l0attn_launch()
    cm = np.zeros((4, 128, 512), np.float32)
    for a in range(4):
        for p in range(128):
            cm[a, p, a * 128 + p:] = 1.0
    cm = cm.reshape(4 * 128, 512).astype(ml_dtypes.bfloat16)
    in_maps = []
    for b in range(NCORES):
        in_maps.append({
            "xT": np.ascontiguousarray(x[b].T),
            "wq": np.asarray(inp["L0_attn_Wq"]), "bq": np.asarray(inp["L0_attn_bq"]),
            "wk": np.asarray(inp["L0_attn_Wk"]), "bk": np.asarray(inp["L0_attn_bk"]),
            "wv": np.asarray(inp["L0_attn_Wv"]), "bv": np.asarray(inp["L0_attn_bv"]),
            "wo": np.asarray(inp["L0_attn_Wo"]), "bo": np.asarray(inp["L0_attn_bo"]),
            "cmask": cm,
            "onesd": np.ones((1, T), ml_dtypes.bfloat16),
        })
    res = run_bass_kernel_spmd(nc, in_maps, core_ids=list(range(NCORES)))
    return np.stack([np.asarray(res.results[b]["yT"], np.float32).T for b in range(NCORES)])



def _build_l3attn_launch():
    import math
    nc = bass.Bass("TRN2", target_bir_lowering=False, detect_race_conditions=False)
    xT = nc.dram_tensor("xT", [D, T], F32, kind="ExternalInput").ap()
    wqk = nc.dram_tensor("wqk", [D, 128], F32, kind="ExternalInput").ap()
    bqk = nc.dram_tensor("bqk", [128], F32, kind="ExternalInput").ap()
    gam = nc.dram_tensor("gam", [2, 128], F32, kind="ExternalInput").ap()
    bet = nc.dram_tensor("bet", [2, 128], F32, kind="ExternalInput").ap()
    wv = nc.dram_tensor("wv", [D, D], F32, kind="ExternalInput").ap()
    bv = nc.dram_tensor("bv", [D], F32, kind="ExternalInput").ap()
    rbt = nc.dram_tensor("rbt", [32], F32, kind="ExternalInput").ap()
    ohb = nc.dram_tensor("ohb", [32, T], F32, kind="ExternalInput").ap()
    cmask = nc.dram_tensor("cmask", [4 * 128, 512], BF16, kind="ExternalInput").ap()
    yT = nc.dram_tensor("yT", [D, T], F32, kind="ExternalOutput").ap()
    dgp = nc.dram_tensor("dgp", [2 * T], F32, kind="Internal").ap()
    dgp2 = nc.dram_tensor("dgp2", [128, 2 * T + 128], F32, kind="Internal").ap()

    MU_L = math.sqrt(0.5)
    STD_L = math.sqrt(0.25 * math.pi)
    ERF_SC = 1.0 / (STD_L * math.sqrt(2.0))
    NTT = T // 128

    with ExitStack() as stack:
        blk = stack.enter_context(nc.Block())
        K = Ctx(nc, blk, stack)
        for s_ in ("lx", "lw", "lwq", "lbt", "mm", "dv", "ac", "so"):
            K.sem(s_)
        ONES_F = K.sb("c_ONESF", [128, 1], F32)
        ONES_B = K.sb("c_ONESB", [128, 1], BF16)
        ONES_R = K.sb("c_ONESR", [1, 128], F32)
        P0 = dict(ONES_F=ONES_F, ONES_B=ONES_B, ONES_R=ONES_R)
        XHB = K.sb("c_XHB", [128, NDT * T], BF16)
        QF = K.sb("c_QF", [128, T], BF16)
        KF = K.sb("c_KF", [128, T], BF16)
        VT = K.sb("c_VT", [128, NTT * 1024], BF16)
        AOF = K.sb("c_AOF", [128, NDT * TC], F32)
        G1 = K.sb("c_G1", [128, NDT * TC], F32)
        XC = K.sb("c_XC", [128, TC], F32)
        MU = K.sb("c_MU", [1, TC], F32)
        PTS = K.sb("c_PTS", [128, 512], BF16)
        SB1 = K.sb("c_SB1", [128, 512], F32)
        BT = [K.sb(f"c_BT{i}", [128, 512], BF16) for i in range(2)]
        CM = K.sb("c_CM", [128, 4 * 512], BF16)
        WB = K.sb("c_WB", [128, NDT * 128], BF16)
        WVB = K.sb("c_WVB", [128, NDT * 512], BF16)
        BQKC = K.sb("c_BQKC", [128, 1], F32)
        GB = K.sb("c_GB", [128, 4], F32)
        BVR = K.sb("c_BVR", [128, 1024], BF16)
        TBL = K.sb("c_TBL", [32, 1], F32)
        OHP = K.sb("c_OHP", [32, 512], F32)
        DRow = K.sb("c_DRow", [1, 512], F32)
        BIX = K.sb("c_BIX", [128, 1], U32)
        PS_SUM = stack.enter_context(nc.psum_tensor("c_ps_sum", [1, TC], F32))
        PS_SS = stack.enter_context(nc.psum_tensor("c_ps_ss", [1, TC], F32))
        PS_MUR = stack.enter_context(nc.psum_tensor("c_ps_mur", [128, TC], F32))
        PS_RSR = stack.enter_context(nc.psum_tensor("c_ps_rsr", [128, TC], F32))
        PS_A = [stack.enter_context(nc.psum_tensor(f"c_ps_a{i}", [128, 512], F32)) for i in range(2)]
        PS_B = [stack.enter_context(nc.psum_tensor(f"c_ps_b{i}", [128, 512], F32)) for i in range(2)]

        @blk.vector
        def _(v):
            v.memset(ONES_F[:], 1.0)
            v.memset(ONES_B[:], 1.0)
            i = v.memset(ONES_R[:], 1.0)
            K.inc(i, "dv")

        @blk.sync
        def _(q):
            i = q.dma_start(out=CM[:], in_=bass.AP(tensor=cmask.tensor, offset=0,
                                                   ap=[[512, 128], [128 * 512, 4], [1, 512]]))
            K.inc(i, "lx", 16)
            i = q.dma_start(out=BQKC[:], in_=bass.AP(tensor=bqk.tensor, offset=0,
                                                     ap=[[1, 128], [1, 1]]))
            K.inc(i, "lx", 16)
            i = q.dma_start(out=GB[:, 0:2], in_=bass.AP(tensor=gam.tensor, offset=0,
                                                        ap=[[1, 128], [128, 2]]),
                            allow_slow_non_contiguous=True)
            K.inc(i, "lx", 16)
            i = q.dma_start(out=GB[:, 2:4], in_=bass.AP(tensor=bet.tensor, offset=0,
                                                        ap=[[1, 128], [128, 2]]),
                            allow_slow_non_contiguous=True)
            K.inc(i, "lx", 16)
            i = q.dma_start(out=TBL[:], in_=bass.AP(tensor=rbt.tensor, offset=0,
                                                    ap=[[1, 32], [1, 1]]))
            K.inc(i, "lx", 16)
        lx_p = K.val("lx")

        @blk.gpsimd
        def _(g):
            for k in range(NDT):
                i = g.dma_start(out=WB[:, k * 128:(k + 1) * 128],
                                in_=wqk[k * 128:(k + 1) * 128, :])
                K.inc(i, "lw", 16)

        lw_w = K.val("lw")

        @blk.sync
        def _(q):
            i = q.dma_start(out=AOF[0:1, 0:1024], in_=bass.AP(tensor=bv.tensor, offset=0,
                                                              ap=[[1024, 1], [1, 1024]]))
            K.inc(i, "lx", 16)
        lx_bv = K.val("lx")
        for half in range(2):
            @blk.tensor
            def _(tx, half=half):
                K.wait(tx, "lx", lx_bv)
                K.wait(tx, "ac", K.val("ac"))
                i = tx.matmul(PS_A[half][:, :], ONES_R[:], AOF[0:1, half * 512:(half + 1) * 512],
                              start=True, stop=True)
                K.inc(i, "mm")
            mmv = K.val("mm")

            @blk.scalar
            def _(sc, half=half, mmv=mmv):
                K.wait(sc, "mm", mmv)
                i = sc.copy(BVR[:, half * 512:(half + 1) * 512], PS_A[half][:, :])
                K.inc(i, "ac")
        ac_bvr = K.val("ac")

        @blk.vector
        def _(v):
            i = v.memset(DRow[:], -MU_L)
            K.inc(i, "dv")
        dv_dr = K.val("dv")
        for i_ in range(T // 512):
            @blk.sync
            def _(q, i_=i_, dv_dr=dv_dr):
                K.wait(q, "dv", dv_dr)
                i = q.dma_start(out=dgp[i_ * 512:(i_ + 1) * 512], in_=DRow[0:1, :])
                K.inc(i, "lbt", 16)
        for i_ in range(T // 512):
            @blk.sync
            def _(q, i_=i_):
                K.wait(q, "dv", K.val("dv"))
                i = q.dma_start(out=OHP[:], in_=ohb[:, i_ * 512:(i_ + 1) * 512])
                K.inc(i, "lx", 16)
            lxv = K.val("lx")

            @blk.tensor
            def _(tx, i_=i_, lxv=lxv):
                K.wait(tx, "lx", lxv)
                K.wait(tx, "lx", lx_p)
                K.wait(tx, "dv", K.val("dv"))
                K.wait(tx, "ac", ac_bvr)
                i = tx.matmul(PS_A[0][0:1, :], TBL[:], OHP[:], start=True, stop=True)
                K.inc(i, "mm")
            mmv = K.val("mm")

            @blk.vector
            def _(v, i_=i_, mmv=mmv):
                K.wait(v, "mm", mmv)
                i = v.tensor_scalar(DRow[:], PS_A[0][0:1, :], float(np.sqrt(128.0)), -MU_L,
                                    ALU.mult, ALU.add)
                K.inc(i, "dv")
            dvv = K.val("dv")

            @blk.sync
            def _(q, i_=i_, dvv=dvv):
                K.wait(q, "dvv" if False else "dv", dvv)
                i = q.dma_start(out=dgp[T + i_ * 512:T + (i_ + 1) * 512], in_=DRow[0:1, :])
                K.inc(i, "lbt", 16)
        lbt_d0 = K.val("lbt")

        @blk.sync
        def _(q):
            K.wait(q, "lbt", lbt_d0)
            for p in range(128):
                i = q.dma_start(out=dgp2[p:p + 1, p:p + 2 * T],
                                in_=bass.AP(tensor=dgp.tensor, offset=0, ap=[[2 * T, 1], [1, 2 * T]]))
                K.inc(i, "lbt", 16)
        lbt_d = K.val("lbt")

        dv_prev = K.val("dv")
        for c in range(NCHUNK):
            @blk.sync
            def _(q, c=c):
                K.wait(q, "mm", K.val("mm"))
                K.wait(q, "ac", K.val("ac"))
                K.wait(q, "dv", K.val("dv"))
                for k in range(NDT):
                    i = q.dma_start(out=G1[:, k * TC:(k + 1) * TC],
                                    in_=xT[k * 128:(k + 1) * 128, c * TC:(c + 1) * TC])
                    K.inc(i, "lx", 16)
            lxv = K.val("lx")
            for eng in ("vector", "scalar", "tensor"):
                @getattr(blk, eng)
                def _(e):
                    K.wait(e, "lx", lxv)
            XCl = [G1[:, k * TC:(k + 1) * TC] for k in range(NDT)]
            mm_rep = emit_ln_stats(K, XCl, slice(0, TC), ONES_F, VT[:, 0:NDT * TC], ONES_B,
                                   PS_SUM[:], PS_SS[:], MU, PS_SUM[:],
                                   PS_MUR[:], PS_RSR[:], ONES_R,
                                   dep_data=[("dv", dv_prev)])

            @blk.vector
            def _(v, c=c, mm_rep=mm_rep):
                K.wait(v, "mm", mm_rep)
                for k in range(NDT):
                    dst = XHB[:, k * T + c * TC:k * T + (c + 1) * TC]
                    v.tensor_tensor(XC[:], G1[:, k * TC:(k + 1) * TC], PS_MUR[:], ALU.subtract)
                    i = v.tensor_tensor(dst, XC[:], PS_RSR[:], ALU.mult)
                K.inc(i, "dv")
            dv_prev = K.val("dv")
        dv_ln = K.val("dv")

        psA_free = [None, None]
        psB_free = [None, None]
        for tb in range(T // 512):
            par = tb % 2
            @blk.tensor
            def _(tx, tb=tb, par=par):
                K.wait(tx, "lw", lw_w)
                K.wait(tx, "dv", dv_ln)
                if psA_free[par]:
                    K.wait(tx, psA_free[par][0], psA_free[par][1])
                for k in range(NDT):
                    i = tx.matmul(PS_A[par][:, :], WB[:, k * 128:(k + 1) * 128],
                                  XHB[:, k * T + tb * 512:k * T + (tb + 1) * 512],
                                  start=(k == 0), stop=(k == NDT - 1))
                K.inc(i, "mm")
            mmv = K.val("mm")

            @blk.scalar
            def _(sc, tb=tb, par=par, mmv=mmv):
                K.wait(sc, "mm", mmv)
                K.wait(sc, "lx", lx_p)
                K.wait(sc, "dv", K.val("dv"))
                i = sc.activation(SB1[:, 0:512], PS_A[par][:, :], AF.Silu, bias=BQKC[:])
                K.inc(i, "ac")
            acv = K.val("ac")

            @blk.vector
            def _(v, tb=tb, par=par, acv=acv):
                K.wait(v, "ac", acv)
                v.tensor_scalar(QF[:, tb * 512:(tb + 1) * 512], SB1[:, 0:512],
                                GB[:, 0:1], GB[:, 2:3], ALU.mult, ALU.add)
                i = v.tensor_scalar(KF[:, tb * 512:(tb + 1) * 512], SB1[:, 0:512],
                                    GB[:, 1:2], GB[:, 3:4], ALU.mult, ALU.add)
                psA_free[par] = ("dv", K.inc(i, "dv"))
        dv_qk = K.val("dv")

        for half in range(2):
            mm_cons = K.val("mm")
            @blk.gpsimd
            def _(g, half=half, mm_cons=mm_cons):
                K.wait(g, "mm", mm_cons)
                for k in range(NDT):
                    i = g.dma_start(out=WVB[:, k * 512:(k + 1) * 512],
                                    in_=bass.AP(tensor=wv.tensor,
                                                offset=k * 128 * 1024 + half * 512,
                                                ap=[[1024, 128], [1, 512]]))
                    K.inc(i, "lw", 16)
            lw_wv = K.val("lw")
            for ti in range(NTT):
                par = ti % 2
                @blk.tensor
                def _(tx, ti=ti, half=half, par=par, lw_wv=lw_wv):
                    K.wait(tx, "lw", lw_wv)
                    K.wait(tx, "dv", dv_ln)
                    if psB_free[par]:
                        K.wait(tx, psB_free[par][0], psB_free[par][1])
                    for k in range(NDT):
                        i = tx.matmul(PS_B[par][:, :],
                                      XHB[:, k * T + ti * 128:k * T + (ti + 1) * 128],
                                      WVB[:, k * 512:(k + 1) * 512],
                                      start=(k == 0), stop=(k == NDT - 1))
                    K.inc(i, "mm")
                mmv = K.val("mm")

                @blk.vector
                def _(v, ti=ti, half=half, par=par, mmv=mmv):
                    K.wait(v, "mm", mmv)
                    K.wait(v, "ac", K.val("ac"))
                    i = v.tensor_tensor(SB1[:, :], PS_B[par][:, :],
                                        BVR[:, half * 512:(half + 1) * 512], ALU.add)
                    K.inc(i, "dv")
                dvv = K.val("dv")

                @blk.scalar
                def _(sc, ti=ti, half=half, par=par, dvv=dvv):
                    K.wait(sc, "dv", dvv)
                    i = sc.activation(VT[:, ti * 1024 + half * 512:ti * 1024 + (half + 1) * 512],
                                      SB1[:, :], AF.Silu)
                    psB_free[par] = ("ac", K.inc(i, "ac"))
        dv_v = K.val("ac")

        for ib in range(T // 512):
            njt = (ib * 512 + 512) // 128
            for jt in range(njt):
                par = jt % 2
                dk = jt * 128 - ib * 512
                @blk.gpsimd
                def _(g, ib=ib, jt=jt, par=par):
                    K.wait(g, "lbt", lbt_d)
                    K.wait(g, "dv", K.val("dv"))
                    i = g.dma_start(out=BT[par][:],
                                    in_=bass.AP(tensor=dgp2.tensor,
                                                offset=T + ib * 512 - jt * 128,
                                                ap=[[2 * T + 128, 128], [1, 512]]))
                    K.inc(i, "lbt", 16)
                lbt_v = K.val("lbt")

                @blk.tensor
                def _(tx, ib=ib, jt=jt, par=par):
                    K.wait(tx, "dv", dv_qk)
                    if psA_free[par]:
                        K.wait(tx, psA_free[par][0], psA_free[par][1])
                    i = tx.matmul(PS_A[par][:, :], KF[:, jt * 128:(jt + 1) * 128],
                                  QF[:, ib * 512:(ib + 1) * 512], start=True, stop=True)
                    K.inc(i, "mm")
                mmv = K.val("mm")

                @blk.vector
                def _(v, par=par, mmv=mmv, lbt_v=lbt_v):
                    K.wait(v, "mm", mmv)
                    K.wait(v, "lbt", lbt_v)
                    K.wait(v, "ac", K.val("ac"))
                    i = v.scalar_tensor_tensor(SB1[:, :], PS_A[par][:, :], 1.0 / T,
                                               BT[par][:], ALU.mult, ALU.add)
                    psA_free[par] = ("dv", K.inc(i, "dv"))
                dvv = K.val("dv")

                @blk.scalar
                def _(sc, dvv=dvv):
                    K.wait(sc, "dv", dvv)
                    K.wait(sc, "dv", K.val("dv"))
                    i = sc.activation(PTS[:, :], SB1[:, :], AF.Erf, scale=ERF_SC)
                    K.inc(i, "ac")
                ac_e = K.val("ac")

                @blk.vector
                def _(v, dk=dk, ac_e=ac_e):
                    K.wait(v, "ac", ac_e)
                    i = v.tensor_scalar(PTS[:, :], PTS[:, :], 0.5, 0.5, ALU.mult, ALU.add)
                    K.inc(i, "dv")
                    if 0 <= dk:
                        K.wait(v, "dv")
                        i = v.tensor_tensor(PTS[:, :], PTS[:, :],
                                            CM[:, (dk // 128) * 512:(dk // 128 + 1) * 512],
                                            ALU.mult)
                        K.inc(i, "dv")
                dv_p = K.val("dv")

                for ds in range(NDT):
                    bpar = ds % 2
                    @blk.tensor
                    def _(tx, jt=jt, ds=ds, bpar=bpar, dv_p=dv_p):
                        K.wait(tx, "dv", dv_p)
                        K.wait(tx, "dv", dv_v)
                        if psB_free[bpar]:
                            K.wait(tx, psB_free[bpar][0], psB_free[bpar][1])
                        i = tx.matmul(PS_B[bpar][:, :],
                                      VT[:, jt * 1024 + ds * 128:jt * 1024 + (ds + 1) * 128],
                                      PTS[:, :], start=True, stop=True)
                        K.inc(i, "mm")
                    mm_pv = K.val("mm")

                    @blk.vector
                    def _(v, jt=jt, ds=ds, bpar=bpar, mm_pv=mm_pv):
                        K.wait(v, "mm", mm_pv)
                        dst = AOF[:, ds * TC:(ds + 1) * TC]
                        if jt == 0:
                            i = v.tensor_copy(dst, PS_B[bpar][:, :])
                        else:
                            i = v.tensor_tensor(dst, dst, PS_B[bpar][:, :], ALU.add)
                        psB_free[bpar] = ("dv", K.inc(i, "dv"))
            dv_pv = K.val("dv")

            @blk.sync
            def _(q, ib=ib):
                K.wait(q, "so", K.val("so"))
                K.wait(q, "dv", K.val("dv"))
                for k in range(NDT):
                    i = q.dma_start(out=G1[:, k * TC:(k + 1) * TC],
                                    in_=xT[k * 128:(k + 1) * 128, ib * 512:(ib + 1) * 512])
                    K.inc(i, "lx", 16)
            lx_xc = K.val("lx")

            @blk.vector
            def _(v, ib=ib, dv_pv=dv_pv, lx_xc=lx_xc):
                K.wait(v, "dv", dv_pv)
                K.wait(v, "lx", lx_xc)
                for k in range(NDT):
                    i = v.tensor_tensor(G1[:, k * TC:(k + 1) * TC], G1[:, k * TC:(k + 1) * TC],
                                        AOF[:, k * TC:(k + 1) * TC], ALU.add)
                K.inc(i, "dv")
            dv_o = K.val("dv")

            @blk.sync
            def _(q, ib=ib, dv_o=dv_o):
                K.wait(q, "dv", dv_o)
                for k in range(NDT):
                    i = q.dma_start(out=yT[k * 128:(k + 1) * 128, ib * 512:(ib + 1) * 512],
                                    in_=G1[:, k * TC:(k + 1) * TC])
                    K.inc(i, "so", 16)

        @blk.sync
        def _(q):
            K.wait(q, "so", K.val("so"))
    return nc


def dev_run_l3attn(x, inp):
    import math
    nc = _build_l3attn_launch()
    cm = np.zeros((4, 128, 512), np.float32)
    for a in range(4):
        for p in range(128):
            cm[a, p, a * 128 + p:] = 1.0
    cm = cm.reshape(4 * 128, 512).astype(ml_dtypes.bfloat16)
    NB, MD = 32, 128
    ohb = np.zeros((32, T), np.float32)
    for n in range(T):
        me = NB // 2
        if n < me:
            b = n
        else:
            ns = max(n, 1)
            b = me + int(math.log(ns / me) / math.log(MD / me) * (NB - me))
            b = min(b, NB - 1)
        ohb[b, n] = 1.0
    in_maps = []
    for b in range(NCORES):
        in_maps.append({
            "xT": np.ascontiguousarray(x[b].T),
            "wqk": np.asarray(inp["L3_attn_wqk"]), "bqk": np.asarray(inp["L3_attn_bqk"]),
            "gam": np.asarray(inp["L3_attn_gamma"]), "bet": np.asarray(inp["L3_attn_beta"]),
            "wv": np.asarray(inp["L3_attn_wv"]), "bv": np.asarray(inp["L3_attn_bv"]),
            "rbt": np.asarray(inp["L3_attn_rel_bias"]),
            "ohb": ohb, "cmask": cm,
        })
    res = run_bass_kernel_spmd(nc, in_maps, core_ids=list(range(NCORES)))
    return np.stack([np.asarray(res.results[b]["yT"], np.float32).T for b in range(NCORES)])



def _build_l1attn_launch():
    nc = bass.Bass("TRN2", target_bir_lowering=False, detect_race_conditions=False)
    xT = nc.dram_tensor("xT", [D, T], F32, kind="ExternalInput").ap()
    wq = nc.dram_tensor("wq", [D, D], F32, kind="ExternalInput").ap()
    wk = nc.dram_tensor("wk", [D, D], F32, kind="ExternalInput").ap()
    wv = nc.dram_tensor("wv", [D, D], F32, kind="ExternalInput").ap()
    wa = nc.dram_tensor("wa", [D, 16], F32, kind="ExternalInput").ap()
    ba = nc.dram_tensor("ba", [16], F32, kind="ExternalInput").ap()
    wb = nc.dram_tensor("wb", [D, 16], F32, kind="ExternalInput").ap()
    bb = nc.dram_tensor("bb", [16], F32, kind="ExternalInput").ap()
    wg = nc.dram_tensor("wg", [D, D], F32, kind="ExternalInput").ap()
    wo = nc.dram_tensor("wo", [D, D], F32, kind="ExternalInput").ap()
    gn = nc.dram_tensor("gn", [D], F32, kind="ExternalInput").ap()
    triui = nc.dram_tensor("triui", [128, 128], BF16, kind="ExternalInput").ap()
    trils = nc.dram_tensor("trils", [128, 128], BF16, kind="ExternalInput").ap()
    idenb = nc.dram_tensor("idenb", [128, 128], BF16, kind="ExternalInput").ap()
    idenf = nc.dram_tensor("idenf", [128, 128], F32, kind="ExternalInput").ap()
    yT = nc.dram_tensor("yT", [D, T], F32, kind="ExternalOutput").ap()
    yd = nc.dram_tensor("yd", [D, T], BF16, kind="Internal").ap()

    CK = 128
    NCH = T // CK

    with ExitStack() as stack:
        blk = stack.enter_context(nc.Block())
        K = Ctx(nc, blk, stack)
        for s_ in ("lx", "lw", "lwq", "mm", "dv", "ac", "so"):
            K.sem(s_)
        ONES_F = K.sb("d_ONESF", [128, 1], F32)
        ONES_B = K.sb("d_ONESB", [128, 1], BF16)
        ONES_R = K.sb("d_ONESR", [1, 128], F32)
        ONES_RB = K.sb("d_ONESRB", [1, 128], BF16)
        P0 = dict(ONES_F=ONES_F, ONES_B=ONES_B, ONES_R=ONES_R)
        XHB = K.sb("d_XHB", [128, NDT * T], BF16)
        QF = K.sb("d_QF", [64, T], BF16)
        KF = K.sb("d_KF", [64, T], BF16)
        VF = K.sb("d_VF", [64, T], BF16)
        ABT = K.sb("d_ABT", [128, NCH * 16], F32)
        BBT = K.sb("d_BBT", [128, NCH * 16], F32)
        LAT_ = K.sb("d_LAT", [128, NCH * 16], BF16)
        WBUF = K.sb("d_WBUF", [128, NDT * 192], BF16)
        G1 = K.sb("d_G1", [128, NDT * TC], F32)
        SQB = K.sb("d_SQB", [128, NDT * TC], BF16)
        XC = K.sb("d_XC", [128, TC], F32)
        MU = K.sb("d_MU", [1, TC], F32)
        SBX = K.sb("d_SBX", [128, 512], F32)
        ROWS = K.sb("d_ROWS", [1, 512], F32)
        COLS = K.sb("d_COLS", [128, 12], F32)
        CSC = K.sb("d_CSC", [128, 1], F32)
        GCC = K.sb("d_GCC", [64, 1], F32)
        MT = [K.sb(f"d_MT{i}", [128, 128], BF16) for i in range(7)]
        MTT = [K.sb(f"d_MTT{i}", [128, 128], BF16) for i in range(7)]
        NB_ = K.sb("d_NB", [128, 128], BF16)
        WTB = K.sb("d_WTB", [128, 128], BF16)
        EXD = K.sb("d_EXD", [128, 128], F32)
        VTOK = K.sb("d_VTOK", [128, 64], BF16)
        KDC = K.sb("d_KDC", [64, CK], BF16)
        KDT = K.sb("d_KDT", [128, 64], BF16)
        RB = K.sb("d_RB", [128, 64], F32)
        UB = K.sb("d_UB", [128, 64], BF16)
        UB2 = K.sb("d_UB2", [128, 64], BF16)
        FSC = K.sb("d_FSC", [128, 512], F32)
        YGB = K.sb("d_YGB", [128, NDT * 512], BF16)
        GSB = K.sb("d_GSB", [128, NDT * 512], BF16)
        WGB = K.sb("d_WGB", [128, 64 * 128], BF16)
        WOB = K.sb("d_WOB", [128, 64 * 128], BF16)
        S0 = K.sb("d_S0", [64, 64], F32)
        S0B = K.sb("d_S0B", [64, 64], BF16)
        YCH = K.sb("d_YCH", [128, 64], F32)
        YCB = K.sb("d_YCB", [64, 128], BF16)
        TRL = K.sb("d_TRL", [128, 3 * 128], BF16)
        IDF32 = K.sb("d_IDF32", [128, 128], F32)
        ZAB = K.sb("d_ZAB", [16, 1024], BF16)
        ZT1 = K.sb("d_ZT1", [128, NCH * 16], F32)
        ZT2 = K.sb("d_ZT2", [128, NCH * 16], F32)
        GNC = K.sb("d_GNC", [128, 8], F32)
        PS1 = stack.enter_context(nc.psum_tensor("d_ps1", [128, 512], F32))
        PS2 = stack.enter_context(nc.psum_tensor("d_ps2", [128, 512], F32))
        PS3 = stack.enter_context(nc.psum_tensor("d_ps3", [128, 512], F32))
        PS4 = stack.enter_context(nc.psum_tensor("d_ps4", [128, 128], F32))
        PSR = stack.enter_context(nc.psum_tensor("d_psr", [128, 512], F32))
        PS6 = stack.enter_context(nc.psum_tensor("d_ps6", [128, 128], BF16))
        PS_SUM = stack.enter_context(nc.psum_tensor("d_ps_sum", [1, TC], F32))
        PS_SS = stack.enter_context(nc.psum_tensor("d_ps_ss", [1, TC], F32))

        @blk.vector
        def _(v):
            v.memset(ONES_F[:], 1.0)
            v.memset(ONES_B[:], 1.0)
            v.memset(ONES_RB[:], 1.0)
            i = v.memset(ONES_R[:], 1.0)
            K.inc(i, "dv")

        @blk.sync
        def _(q):
            i = q.dma_start(out=TRL[:, 0:128], in_=triui)
            K.inc(i, "lx", 16)
            i = q.dma_start(out=TRL[:, 128:256], in_=trils)
            K.inc(i, "lx", 16)
            i = q.dma_start(out=TRL[:, 256:384], in_=idenb)
            K.inc(i, "lx", 16)
            i = q.dma_start(out=IDF32[:], in_=idenf)
            K.inc(i, "lx", 16)
            i = q.dma_start(out=GNC[:], in_=bass.AP(tensor=gn.tensor, offset=0,
                                                    ap=[[1, 128], [128, 8]]),
                            allow_slow_non_contiguous=True)
            K.inc(i, "lx", 16)
        lx_tr = K.val("lx")
        TRIUI = TRL[:, 0:128]
        TRILS = TRL[:, 128:256]
        IDB = TRL[:, 256:384]

        dv_prev = K.val("dv")
        for c in range(NCHUNK):
            @blk.sync
            def _(q, c=c):
                K.wait(q, "mm", K.val("mm"))
                K.wait(q, "ac", K.val("ac"))
                K.wait(q, "dv", K.val("dv"))
                for k in range(NDT):
                    i = q.dma_start(out=G1[:, k * TC:(k + 1) * TC],
                                    in_=xT[k * 128:(k + 1) * 128, c * TC:(c + 1) * TC])
                    K.inc(i, "lx", 16)
            lxv = K.val("lx")
            for eng in ("vector", "scalar", "tensor"):
                @getattr(blk, eng)
                def _(e):
                    K.wait(e, "lx", lxv)
            XCl = [G1[:, k * TC:(k + 1) * TC] for k in range(NDT)]
            mm_rep = emit_ln_stats(K, XCl, slice(0, TC), ONES_F, SQB, ONES_B,
                                   PS_SUM[:], PS_SS[:], MU, PS_SUM[:],
                                   PS1[:], PS2[:], ONES_R, dep_data=[("dv", dv_prev)])

            @blk.vector
            def _(v, c=c, mm_rep=mm_rep):
                K.wait(v, "mm", mm_rep)
                for k in range(NDT):
                    dst = XHB[:, k * T + c * TC:k * T + (c + 1) * TC]
                    v.tensor_tensor(XC[:], G1[:, k * TC:(k + 1) * TC], PS1[:, 0:TC], ALU.subtract)
                    i = v.tensor_tensor(dst, XC[:], PS2[:, 0:TC], ALU.mult)
                K.inc(i, "dv")
            dv_prev = K.val("dv")
        dv_ln = K.val("dv")

        @blk.gpsimd
        def _(g):
            for k in range(NDT):
                i = g.dma_start(out=WBUF[:, k * 128:k * 128 + 16], in_=wa[k * 128:(k + 1) * 128, :])
                K.inc(i, "lw", 16)
                i = g.dma_start(out=WBUF[:, k * 128 + 16:k * 128 + 32], in_=wb[k * 128:(k + 1) * 128, :])
                K.inc(i, "lw", 16)
            i = g.dma_start(out=COLS[0:16, 0:1], in_=bass.AP(tensor=ba.tensor, offset=0,
                                                             ap=[[1, 16], [1, 1]]))
            K.inc(i, "lw", 16)
            i = g.dma_start(out=COLS[0:16, 1:2], in_=bass.AP(tensor=bb.tensor, offset=0,
                                                             ap=[[1, 16], [1, 1]]))
            K.inc(i, "lw", 16)
        lw_ab = K.val("lw")

        for tb in range(T // 512):
            @blk.tensor
            def _(tx, tb=tb):
                K.wait(tx, "lw", lw_ab)
                K.wait(tx, "lx", lx_tr)
                K.wait(tx, "dv", dv_ln)
                K.wait(tx, "dv", K.val("dv"))
                K.wait(tx, "ac", K.val("ac"))
                for k in range(NDT):
                    tx.matmul(PS1[0:16, :], WBUF[:, k * 128:k * 128 + 16],
                              XHB[:, k * T + tb * 512:k * T + (tb + 1) * 512],
                              start=(k == 0), stop=(k == NDT - 1))
                for k in range(NDT):
                    i = tx.matmul(PS2[0:16, :], WBUF[:, k * 128 + 16:k * 128 + 32],
                                  XHB[:, k * T + tb * 512:k * T + (tb + 1) * 512],
                                  start=(k == 0), stop=(k == NDT - 1))
                K.inc(i, "mm")
            mmv = K.val("mm")

            @blk.vector
            def _(v, tb=tb, mmv=mmv):
                K.wait(v, "mm", mmv)
                v.tensor_scalar(ZAB[0:16, 0:512], PS1[0:16, :], COLS[0:16, 0:1], None, ALU.add)
                i = v.tensor_scalar(ZAB[0:16, 512:1024], PS2[0:16, :], COLS[0:16, 1:2], None, ALU.add)
                K.inc(i, "dv")
            dvv = K.val("dv")

            for sub in range(4):
                for which in range(2):
                    @blk.tensor
                    def _(tx, tb=tb, sub=sub, which=which, dvv=dvv):
                        K.wait(tx, "dv", dvv)
                        K.wait(tx, "ac", K.val("ac"))
                        i = tx.transpose(PS6[0:128, 0:16],
                                         ZAB[0:16, which * 512 + sub * 128:which * 512 + (sub + 1) * 128],
                                         IDB[0:16, 0:16])
                        K.inc(i, "mm")
                    mmt = K.val("mm")

                    @blk.scalar
                    def _(sc, tb=tb, sub=sub, which=which, mmt=mmt):
                        K.wait(sc, "mm", mmt)
                        c_ = tb * 4 + sub
                        dst = (ZT1 if which == 0 else ZT2)[:, c_ * 16:(c_ + 1) * 16]
                        i = sc.copy(dst, PS6[0:128, 0:16])
                        K.inc(i, "ac")
        ac_z = K.val("ac")

        @blk.scalar
        def _(sc):
            K.wait(sc, "ac", ac_z)
            i = sc.activation(BBT[:], ZT2[:], AF.Sigmoid)
            K.inc(i, "ac")
            K.wait(sc, "ac")
            i = sc.activation(ZT2[:], ZT1[:], AF.Exp, scale=-1.0)
            K.inc(i, "ac")
        ac_e = K.val("ac")

        @blk.vector
        def _(v):
            K.wait(v, "ac", ac_e)
            i = v.tensor_scalar(ZT2[:], ZT2[:], 1.0, None, ALU.add)
            K.inc(i, "dv")
        dv_e = K.val("dv")

        @blk.scalar
        def _(sc):
            K.wait(sc, "dv", dv_e)
            i = sc.activation(ZT2[:], ZT2[:], AF.Ln)
            K.inc(i, "ac")
            K.wait(sc, "ac")
            i = sc.activation(ABT[:], ZT1[:], AF.Sigmoid)
            K.inc(i, "ac")
        ac_l = K.val("ac")

        @blk.vector
        def _(v):
            K.wait(v, "ac", ac_l)
            i = v.tensor_scalar(LAT_[:], ZT2[:], -1.0, None, ALU.mult)
            K.inc(i, "dv")
        dv_gates = K.val("dv")

        psq = [None, None, None]
        for h in range(16):
            mm_cons = K.val("mm")
            @blk.gpsimd
            def _(g, h=h, mm_cons=mm_cons):
                K.wait(g, "mm", mm_cons)
                for k in range(NDT):
                    for j, wsrc in enumerate((wq, wk, wv)):
                        i = g.dma_start(out=WBUF[:, (k * 3 + j) * 64:(k * 3 + j + 1) * 64],
                                        in_=bass.AP(tensor=wsrc.tensor, offset=k * 128 * 1024 + h * 64,
                                                    ap=[[1024, 128], [1, 64]]))
                        K.inc(i, "lwq", 16)
            lw_qkv = K.val("lwq")

            for tb in range(T // 512):
                @blk.tensor
                def _(tx, tb=tb, h=h, lw_qkv=lw_qkv):
                    K.wait(tx, "lwq", lw_qkv)
                    K.wait(tx, "dv", K.val("dv"))
                    K.wait(tx, "ac", K.val("ac"))
                    for k in range(NDT):
                        tx.matmul(PS1[0:64, :], WBUF[:, (k * 3 + 0) * 64:(k * 3 + 1) * 64],
                                  XHB[:, k * T + tb * 512:k * T + (tb + 1) * 512],
                                  start=(k == 0), stop=(k == NDT - 1))
                    for k in range(NDT):
                        tx.matmul(PS2[0:64, :], WBUF[:, (k * 3 + 1) * 64:(k * 3 + 2) * 64],
                                  XHB[:, k * T + tb * 512:k * T + (tb + 1) * 512],
                                  start=(k == 0), stop=(k == NDT - 1))
                    for k in range(NDT):
                        i = tx.matmul(PS3[0:64, :], WBUF[:, (k * 3 + 2) * 64:(k * 3 + 3) * 64],
                                      XHB[:, k * T + tb * 512:k * T + (tb + 1) * 512],
                                      start=(k == 0), stop=(k == NDT - 1))
                    K.inc(i, "mm")
                mmv = K.val("mm")

                @blk.scalar
                def _(sc, mmv=mmv):
                    K.wait(sc, "mm", mmv)
                    K.wait(sc, "dv", K.val("dv"))
                    sc.activation(SQB[0:64, 0:512], PS1[0:64, :], AF.Square)
                    i = sc.activation(SQB[0:64, 512:1024], PS2[0:64, :], AF.Square)
                    K.inc(i, "ac")
                ac_s = K.val("ac")

                @blk.tensor
                def _(tx, ac_s=ac_s):
                    K.wait(tx, "ac", ac_s)
                    K.wait(tx, "dv", K.val("dv"))
                    tx.matmul(PS_SUM[0:1, 0:512], ONES_B[0:64, :], SQB[0:64, 0:512],
                              start=True, stop=True)
                    i = tx.matmul(PS_SS[0:1, 0:512], ONES_B[0:64, :], SQB[0:64, 512:1024],
                                  start=True, stop=True)
                    K.inc(i, "mm")
                mm_n = K.val("mm")

                @blk.scalar
                def _(sc, mm_n=mm_n):
                    K.wait(sc, "mm", mm_n)
                    sc.activation(MU[0:1, 0:512], PS_SUM[0:1, 0:512], AF.Sqrt)
                    i = sc.activation(XC[0:1, 0:512], PS_SS[0:1, 0:512], AF.Sqrt)
                    K.inc(i, "ac")
                ac_n = K.val("ac")

                @blk.vector
                def _(v, ac_n=ac_n):
                    K.wait(v, "ac", ac_n)
                    v.tensor_scalar(MU[0:1, 0:512], MU[0:1, 0:512], 1e-12, None, ALU.max)
                    i = v.tensor_scalar(XC[0:1, 0:512], XC[0:1, 0:512], 1e-12, None, ALU.max)
                    K.inc(i, "dv")
                    K.wait(v, "dv")
                    v.reciprocal(MU[0:1, 0:512], MU[0:1, 0:512])
                    i = v.reciprocal(XC[0:1, 0:512], XC[0:1, 0:512])
                    K.inc(i, "dv")
                dv_r = K.val("dv")

                @blk.tensor
                def _(tx, dv_r=dv_r):
                    K.wait(tx, "dv", dv_r)
                    K.wait(tx, "ac", K.val("ac"))
                    i = tx.matmul(PSR[0:64, :], ONES_R[0:1, 0:64], MU[0:1, 0:512],
                                  start=True, stop=True)
                    K.inc(i, "mm")
                mq = K.val("mm")

                @blk.scalar
                def _(sc, mq=mq):
                    K.wait(sc, "mm", mq)
                    i = sc.copy(SBX[0:64, :], PSR[0:64, :])
                    K.inc(i, "ac")
                aq = K.val("ac")

                @blk.tensor
                def _(tx, aq=aq):
                    K.wait(tx, "ac", aq)
                    i = tx.matmul(PSR[0:64, :], ONES_R[0:1, 0:64], XC[0:1, 0:512],
                                  start=True, stop=True)
                    K.inc(i, "mm")
                mk = K.val("mm")

                @blk.scalar
                def _(sc, mk=mk):
                    K.wait(sc, "mm", mk)
                    i = sc.copy(SBX[64:128, :], PSR[0:64, :])
                    K.inc(i, "ac")
                ak = K.val("ac")

                @blk.vector
                def _(v, tb=tb, ak=ak, mmv=mmv):
                    K.wait(v, "ac", ak)
                    v.tensor_tensor(QF[:, tb * 512:(tb + 1) * 512], PS1[0:64, :],
                                    SBX[0:64, :], ALU.mult)
                    v.tensor_tensor(KF[:, tb * 512:(tb + 1) * 512], PS2[0:64, :],
                                    SBX[64:128, :], ALU.mult)
                    i = v.tensor_copy(VF[:, tb * 512:(tb + 1) * 512], PS3[0:64, :])
                    K.inc(i, "dv")
            dv_qkv = K.val("dv")

            @blk.vector
            def _(v):
                K.wait(v, "dv", dv_qkv)
                v.memset(S0[:], 0.0)
                i = v.memset(S0B[:], 0.0)
                K.inc(i, "dv")

            for c in range(NCH):
                h_ = h
                sl = slice(c * CK, (c + 1) * CK)
                col = c * 16 + h_
                Qc = QF[:, sl]
                Kc = KF[:, sl]
                Vc = VF[:, sl]

                @blk.tensor
                def _(tx, col=col):
                    K.wait(tx, "dv", K.val("dv"))
                    K.wait(tx, "ac", K.val("ac"))
                    tx.matmul(PS4[0:128, 0:1], TRIUI, LAT_[:, col:col + 1],
                              start=True, stop=True)
                    i = tx.matmul(PS_SUM[0:1, 0:128], LAT_[:, col:col + 1], TRIUI,
                                  start=True, stop=True)
                    K.inc(i, "mm")
                mm_cs = K.val("mm")

                @blk.scalar
                def _(sc, mm_cs=mm_cs):
                    K.wait(sc, "mm", mm_cs)
                    sc.copy(CSC[:], PS4[0:128, 0:1])
                    i = sc.copy(ROWS[0:1, 0:128], PS_SUM[0:1, 0:128])
                    K.inc(i, "ac")
                ac_row = K.val("ac")

                @blk.vector
                def _(v, col=col, ac_row=ac_row):
                    K.wait(v, "ac", ac_row)
                    i = v.tensor_scalar(ROWS[0:1, 128:256], ROWS[0:1, 0:128], -1.0, None, ALU.mult)
                    K.inc(i, "dv")
                    K.wait(v, "dv")
                    i = v.reciprocal(COLS[:, 2:3], ABT[:, col:col + 1])
                    K.inc(i, "dv")
                    K.wait(v, "dv")
                    i = v.tensor_tensor(COLS[:, 3:4], BBT[:, col:col + 1], COLS[:, 2:3], ALU.mult)
                    K.inc(i, "dv")
                dv_c1 = K.val("dv")

                @blk.scalar
                def _(sc, dv_c1=dv_c1):
                    K.wait(sc, "dv", dv_c1)
                    sc.activation(COLS[:, 6:7], CSC[:], AF.Exp)
                    i = sc.activation(ROWS[0:1, 256:384], ROWS[0:1, 128:256], AF.Exp,
                                      bias=ROWS[0:1, 127:128])
                    K.inc(i, "ac")
                ac_c2 = K.val("ac")

                @blk.vector
                def _(v, col=col, ac_c2=ac_c2):
                    K.wait(v, "ac", ac_c2)
                    i = v.tensor_tensor(COLS[:, 7:8], COLS[:, 3:4], COLS[:, 6:7], ALU.mult)
                    K.inc(i, "dv")
                    K.wait(v, "dv")
                    i = v.tensor_scalar(COLS[:, 8:9], COLS[:, 7:8], -1.0, None, ALU.mult)
                    K.inc(i, "dv")
                dv_cols = K.val("dv")

                @blk.tensor
                def _(tx, dv_cols=dv_cols):
                    K.wait(tx, "dv", dv_cols)
                    i = tx.matmul(PS4[0:64, 0:1], ONES_R[0:1, 0:64], ROWS[0:1, 127:128],
                                  start=True, stop=True)
                    K.inc(i, "mm")
                mmg = K.val("mm")

                @blk.scalar
                def _(sc, mmg=mmg):
                    K.wait(sc, "mm", mmg)
                    i = sc.activation(GCC[:], PS4[0:64, 0:1], AF.Exp)
                    K.inc(i, "ac")
                ac_gcc = K.val("ac")

                @blk.tensor
                def _(tx, ac_gcc=ac_gcc):
                    K.wait(tx, "ac", ac_gcc)
                    K.wait(tx, "dv", K.val("dv"))
                    i = tx.matmul(PS4[:, :], Kc, Kc, start=True, stop=True)
                    K.inc(i, "mm")
                    tx.matmul(PSR[:, 0:128], ROWS[0:1, 0:128].bitcast(F32) if False else ROWS[0:1, 0:128],
                              ONES_R[0:1, 0:128], start=True, stop=False) if False else None
                    tx.matmul(PSR[:, 0:128], ROWS[0:1, 0:128], ONES_R[:], start=True, stop=False)
                    i = tx.matmul(PSR[:, 0:128], ONES_R[:], ROWS[0:1, 128:256], start=False, stop=True)
                    K.inc(i, "mm")
                mm_nd = K.val("mm")

                @blk.vector
                def _(v, mm_nd=mm_nd):
                    K.wait(v, "mm", mm_nd)
                    i = v.tensor_tensor(EXD[:], PSR[:, 0:128], TRILS, ALU.mult)
                    K.inc(i, "dv")
                dv_e1 = K.val("dv")

                @blk.scalar
                def _(sc, dv_e1=dv_e1):
                    K.wait(sc, "dv", dv_e1)
                    i = sc.activation(NB_[:], EXD[:], AF.Exp)
                    K.inc(i, "ac")
                ac_e1 = K.val("ac")

                @blk.vector
                def _(v, col=col, ac_e1=ac_e1):
                    K.wait(v, "ac", ac_e1)
                    i = v.tensor_tensor(WTB[:], NB_[:], TRILS, ALU.mult)
                    K.inc(i, "dv")
                    K.wait(v, "dv")
                    i = v.tensor_tensor(NB_[:], WTB[:], PS4[:, :], ALU.mult)
                    K.inc(i, "dv")
                    K.wait(v, "dv")
                    i = v.tensor_scalar(MT[0][:], NB_[:], COLS[:, 3:4], -1.0, ALU.mult, ALU.mult)
                    K.inc(i, "dv")
                dv_n0 = K.val("dv")

                @blk.tensor
                def _(tx, dv_n0=dv_n0):
                    K.wait(tx, "dv", dv_n0)
                    K.wait(tx, "ac", K.val("ac"))
                    i = tx.transpose(PS6[:, :], MT[0][:], IDB)
                    K.inc(i, "mm")
                mmt0 = K.val("mm")

                @blk.scalar
                def _(sc, mmt0=mmt0):
                    K.wait(sc, "mm", mmt0)
                    i = sc.copy(MTT[0][:], PS6[:, :])
                    K.inc(i, "ac")

                for j in range(1, 7):
                    acp = K.val("ac")
                    @blk.tensor
                    def _(tx, j=j, acp=acp):
                        K.wait(tx, "ac", acp)
                        i = tx.matmul(PS4[:, :], MTT[j - 1][:], MT[j - 1][:],
                                      start=True, stop=True)
                        K.inc(i, "mm")
                    mmq = K.val("mm")

                    @blk.scalar
                    def _(sc, j=j, mmq=mmq):
                        K.wait(sc, "mm", mmq)
                        i = sc.copy(MT[j][:], PS4[:, :])
                        K.inc(i, "ac")
                    acq = K.val("ac")

                    @blk.tensor
                    def _(tx, j=j, acq=acq):
                        K.wait(tx, "ac", acq)
                        i = tx.transpose(PS6[:, :], MT[j][:], IDB)
                        K.inc(i, "mm")
                    mmq2 = K.val("mm")

                    @blk.scalar
                    def _(sc, j=j, mmq2=mmq2):
                        K.wait(sc, "mm", mmq2)
                        i = sc.copy(MTT[j][:], PS6[:, :])
                        K.inc(i, "ac")

                acr = K.val("ac")
                @blk.tensor
                def _(tx, acr=acr):
                    K.wait(tx, "ac", acr)
                    K.wait(tx, "dv", K.val("dv"))
                    tx.matmul(PS4[:, 0:64], Kc, S0B[:], start=True, stop=True)
                    i = tx.transpose(PS6[:, 0:64], Vc, IDB[0:64, 0:64])
                    K.inc(i, "mm")
                mmr = K.val("mm")

                @blk.scalar
                def _(sc, mmr=mmr):
                    K.wait(sc, "mm", mmr)
                    i = sc.copy(VTOK[:, :], PS6[:, 0:64])
                    K.inc(i, "ac")
                acv = K.val("ac")

                @blk.vector
                def _(v, col=col, acv=acv):
                    K.wait(v, "ac", acv)
                    v.tensor_scalar(SBX[:, 0:64], VTOK[:, :], BBT[:, col:col + 1], None, ALU.mult)
                    i = v.tensor_scalar(SBX[:, 64:128], PS4[:, 0:64], COLS[:, 8:9], None, ALU.mult)
                    K.inc(i, "dv")
                    K.wait(v, "dv")
                    i = v.tensor_tensor(RB[:, :], SBX[:, 0:64], SBX[:, 64:128], ALU.add)
                    K.inc(i, "dv")
                dv_rb = K.val("dv")

                @blk.scalar
                def _(sc, dv_rb=dv_rb):
                    K.wait(sc, "dv", dv_rb)
                    i = sc.copy(UB[:, :], RB[:, :])
                    K.inc(i, "ac")

                cur, nxt = UB, UB2
                for j in range(6, -1, -1):
                    acu = K.val("ac")
                    dvu = K.val("dv")
                    @blk.tensor
                    def _(tx, j=j, acu=acu, dvu=dvu, cur=cur):
                        K.wait(tx, "ac", acu)
                        K.wait(tx, "dv", dvu)
                        i = tx.matmul(PS4[:, 0:64], MTT[j][:], cur[:, :], start=True, stop=True)
                        K.inc(i, "mm")
                    mmu = K.val("mm")

                    @blk.vector
                    def _(v, j=j, mmu=mmu, cur=cur, nxt=nxt):
                        K.wait(v, "mm", mmu)
                        i = v.tensor_tensor(nxt[:, :], cur[:, :], PS4[:, 0:64], ALU.add)
                        K.inc(i, "dv")
                    cur, nxt = nxt, cur
                UF = cur

                dvw = K.val("dv")
                @blk.tensor
                def _(tx, dvw=dvw):
                    K.wait(tx, "dv", dvw)
                    K.wait(tx, "ac", K.val("ac"))
                    i = tx.matmul(PS4[:, :], Kc, Qc, start=True, stop=True)
                    K.inc(i, "mm")
                    tx.matmul(PSR[:, 128:256], ONES_R[:], ROWS[0:1, 0:128], start=True, stop=False)
                    i = tx.matmul(PSR[:, 128:256], ROWS[0:1, 128:256], ONES_R[:], start=False, stop=True)
                    K.inc(i, "mm")
                mmw = K.val("mm")

                @blk.vector
                def _(v, mmw=mmw):
                    K.wait(v, "mm", mmw)
                    i = v.tensor_tensor(EXD[:], PSR[:, 128:256], TRIUI, ALU.mult)
                    K.inc(i, "dv")
                dvwe = K.val("dv")

                @blk.scalar
                def _(sc, dvwe=dvwe):
                    K.wait(sc, "dv", dvwe)
                    i = sc.activation(NB_[:], EXD[:], AF.Exp)
                    K.inc(i, "ac")
                acwe = K.val("ac")

                @blk.vector
                def _(v, acwe=acwe):
                    K.wait(v, "ac", acwe)
                    i = v.tensor_tensor(MT[0][:], NB_[:], TRIUI, ALU.mult)
                    K.inc(i, "dv")
                    K.wait(v, "dv")
                    i = v.tensor_tensor(WTB[:], MT[0][:], PS4[:, :], ALU.mult)
                    K.inc(i, "dv")
                dvwm = K.val("dv")

                @blk.tensor
                def _(tx, dvwm=dvwm):
                    K.wait(tx, "dv", dvwm)
                    i = tx.matmul(PS4[:, 0:64], Qc, S0B[:], start=True, stop=True)
                    K.inc(i, "mm")
                mmq0 = K.val("mm")

                @blk.scalar
                def _(sc, mmq0=mmq0):
                    K.wait(sc, "mm", mmq0)
                    i = sc.copy(SBX[:, 0:64], PS4[:, 0:64])
                    K.inc(i, "ac")
                acq0 = K.val("ac")

                @blk.tensor
                def _(tx, acq0=acq0, UF=UF):
                    K.wait(tx, "ac", acq0)
                    i = tx.matmul(PS4[:, 0:64], WTB[:], UF[:, :], start=True, stop=True)
                    K.inc(i, "mm")
                mmo2 = K.val("mm")

                @blk.vector
                def _(v, mmo2=mmo2):
                    K.wait(v, "mm", mmo2)
                    i = v.tensor_scalar(SBX[:, 64:128], SBX[:, 0:64], COLS[:, 6:7], None, ALU.mult)
                    K.inc(i, "dv")
                    K.wait(v, "dv")
                    i = v.tensor_tensor(YCH[:, :], SBX[:, 64:128], PS4[:, 0:64], ALU.add)
                    K.inc(i, "dv")
                dvy = K.val("dv")

                @blk.scalar
                def _(sc, dvy=dvy):
                    K.wait(sc, "dv", dvy)
                    i = sc.copy(VTOK[:, :], YCH[:, :])
                    K.inc(i, "ac")
                acy = K.val("ac")

                @blk.tensor
                def _(tx, acy=acy):
                    K.wait(tx, "ac", acy)
                    i = tx.transpose(PS6[0:64, 0:128], VTOK[:, :], IDB)
                    K.inc(i, "mm")
                    i = tx.matmul(PSR[0:64, 256:384], ONES_R[0:1, 0:64], ROWS[0:1, 256:384],
                                  start=True, stop=True)
                    K.inc(i, "mm")
                mmyt = K.val("mm")

                @blk.scalar
                def _(sc, mmyt=mmyt):
                    K.wait(sc, "mm", mmyt)
                    i = sc.copy(YCB[:, :], PS6[0:64, 0:128])
                    K.inc(i, "ac")
                acyt = K.val("ac")

                @blk.sync
                def _(q, h=h, c=c, acyt=acyt):
                    K.wait(q, "ac", acyt)
                    i = q.dma_start(out=yd[h * 64:(h + 1) * 64, c * CK:(c + 1) * CK],
                                    in_=YCB[:, :])
                    K.inc(i, "so", 16)

                @blk.vector
                def _(v, acyt=acyt):
                    K.wait(v, "ac", acyt)
                    i = v.tensor_tensor(KDC[:, :], Kc, PSR[0:64, 256:384], ALU.mult)
                    K.inc(i, "dv")
                dvkd = K.val("dv")

                @blk.tensor
                def _(tx, dvkd=dvkd):
                    K.wait(tx, "dv", dvkd)
                    i = tx.transpose(PS6[:, 0:64], KDC[:, :], IDB[0:64, 0:64])
                    K.inc(i, "mm")
                mmkd = K.val("mm")

                @blk.scalar
                def _(sc, mmkd=mmkd):
                    K.wait(sc, "mm", mmkd)
                    i = sc.copy(KDT[:, :], PS6[:, 0:64])
                    K.inc(i, "ac")
                ackd = K.val("ac")

                @blk.tensor
                def _(tx, ackd=ackd, UF=UF):
                    K.wait(tx, "ac", ackd)
                    i = tx.matmul(PS4[0:64, 0:64], KDT[:, :], UF[:, :], start=True, stop=True)
                    K.inc(i, "mm")
                mms = K.val("mm")

                @blk.vector
                def _(v, mms=mms):
                    K.wait(v, "mm", mms)
                    i = v.tensor_scalar(SBX[0:64, 0:64], S0[:, :], GCC[:, :], None, ALU.mult)
                    K.inc(i, "dv")
                    K.wait(v, "dv")
                    i = v.tensor_tensor(S0[:, :], SBX[0:64, 0:64], PS4[0:64, 0:64], ALU.add)
                    K.inc(i, "dv")
                dvs = K.val("dv")

                @blk.scalar
                def _(sc, dvs=dvs):
                    K.wait(sc, "dv", dvs)
                    i = sc.copy(S0B[:, :], S0[:, :])
                    K.inc(i, "ac")

        so_heads = K.val("so")

        @blk.gpsimd
        def _(g):
            K.wait(g, "so", so_heads)
            for k in range(NDT):
                for o in range(NDT):
                    i = g.dma_start(out=WGB[:, (k * 8 + o) * 128:(k * 8 + o + 1) * 128],
                                    in_=wg[k * 128:(k + 1) * 128, o * 128:(o + 1) * 128])
                    K.inc(i, "lw", 16)
                    i = g.dma_start(out=WOB[:, (k * 8 + o) * 128:(k * 8 + o + 1) * 128],
                                    in_=wo[k * 128:(k + 1) * 128, o * 128:(o + 1) * 128])
                    K.inc(i, "lw", 16)
        lw_fin = K.val("lw")

        for tb in range(T // 512):
            dv0 = K.val("dv"); ac0 = K.val("ac"); mm0 = K.val("mm"); so0 = K.val("so")
            @blk.sync
            def _(q, tb=tb, dv0=dv0, ac0=ac0, mm0=mm0, so0=so0):
                K.wait(q, "dv", dv0)
                K.wait(q, "ac", ac0)
                K.wait(q, "mm", mm0)
                K.wait(q, "so", so0)
                for k in range(NDT):
                    i = q.dma_start(out=YGB[:, k * 512:(k + 1) * 512],
                                    in_=yd[k * 128:(k + 1) * 128, tb * 512:(tb + 1) * 512])
                    K.inc(i, "lx", 16)
                    i = q.dma_start(out=G1[:, k * TC:(k + 1) * TC],
                                    in_=xT[k * 128:(k + 1) * 128, tb * 512:(tb + 1) * 512])
                    K.inc(i, "lx", 16)
            lx_f = K.val("lx")

            @blk.scalar
            def _(sc, lx_f=lx_f):
                K.wait(sc, "lx", lx_f)
                for k in range(NDT):
                    i = sc.activation(SQB[:, k * 512:(k + 1) * 512],
                                      YGB[:, k * 512:(k + 1) * 512], AF.Square)
                K.inc(i, "ac")
            ac_sq = K.val("ac")

            @blk.tensor
            def _(tx, ac_sq=ac_sq):
                K.wait(tx, "ac", ac_sq)
                K.wait(tx, "dv", K.val("dv"))
                for k in range(NDT):
                    i = tx.matmul(PS_SUM[0:1, :], ONES_B[:], SQB[:, k * 512:(k + 1) * 512],
                                  start=(k == 0), stop=(k == NDT - 1))
                K.inc(i, "mm")
            mm_ss = K.val("mm")

            @blk.vector
            def _(v, mm_ss=mm_ss):
                K.wait(v, "mm", mm_ss)
                i = v.tensor_scalar(XC[0:1, :], PS_SUM[0:1, :], 1.0 / 1024.0, 1e-5,
                                    ALU.mult, ALU.add)
                K.inc(i, "dv")
            dv_ms = K.val("dv")

            @blk.scalar
            def _(sc, dv_ms=dv_ms):
                K.wait(sc, "dv", dv_ms)
                i = sc.activation(MU[0:1, :], XC[0:1, :], AF.Sqrt)
                K.inc(i, "ac")
            ac_rt = K.val("ac")

            @blk.vector
            def _(v, ac_rt=ac_rt):
                K.wait(v, "ac", ac_rt)
                i = v.reciprocal(ROWS[0:1, 0:512], MU[0:1, 0:512])
                K.inc(i, "dv")
            dv_rc = K.val("dv")

            @blk.tensor
            def _(tx, dv_rc=dv_rc):
                K.wait(tx, "dv", dv_rc)
                i = tx.matmul(PSR[:, :], ONES_R[:], ROWS[0:1, 0:512], start=True, stop=True)
                K.inc(i, "mm")
            mm_rr = K.val("mm")

            @blk.scalar
            def _(sc, mm_rr=mm_rr):
                K.wait(sc, "mm", mm_rr)
                i = sc.copy(SBX[:, :], PSR[:, :])
                K.inc(i, "ac")
            ac_rr = K.val("ac")

            for o in range(NDT):
                acg = K.val("ac")
                @blk.tensor
                def _(tx, tb=tb, o=o, acg=acg, lw_fin=lw_fin):
                    K.wait(tx, "lw", lw_fin)
                    K.wait(tx, "ac", acg)
                    for k in range(NDT):
                        i = tx.matmul(PS1[:, :], WGB[:, (k * 8 + o) * 128:(k * 8 + o + 1) * 128],
                                      XHB[:, k * T + tb * 512:k * T + (tb + 1) * 512],
                                      start=(k == 0), stop=(k == NDT - 1))
                    K.inc(i, "mm")
                mmg2 = K.val("mm")

                @blk.scalar
                def _(sc, o=o, mmg2=mmg2):
                    K.wait(sc, "mm", mmg2)
                    i = sc.activation(GSB[:, o * 512:(o + 1) * 512], PS1[:, :], AF.Silu)
                    K.inc(i, "ac")
            ac_gate = K.val("ac")

            for k in range(NDT):
                acn = K.val("ac"); dvn = K.val("dv")
                @blk.vector
                def _(v, k=k, acn=acn, dvn=dvn):
                    K.wait(v, "ac", acn)
                    K.wait(v, "dv", dvn)
                    i = v.tensor_tensor(XC[:, :], YGB[:, k * 512:(k + 1) * 512], SBX[:, :], ALU.mult)
                    K.inc(i, "dv")
                    K.wait(v, "dv")
                    i = v.tensor_scalar(FSC[:, :], XC[:, :], GNC[:, k:k + 1], None, ALU.mult)
                    K.inc(i, "dv")
                    K.wait(v, "dv")
                    i = v.tensor_tensor(SQB[:, k * 512:(k + 1) * 512], FSC[:, :],
                                        GSB[:, k * 512:(k + 1) * 512], ALU.mult)
                    K.inc(i, "dv")
            dv_yn = K.val("dv")

            obuf_so = {}
            for o in range(NDT):
                dvo = K.val("dv"); aco = K.val("ac")
                @blk.tensor
                def _(tx, tb=tb, o=o, dvo=dvo, aco=aco):
                    K.wait(tx, "dv", dvo)
                    K.wait(tx, "ac", aco)
                    for k in range(NDT):
                        i = tx.matmul(PS2[:, :], WOB[:, (k * 8 + o) * 128:(k * 8 + o + 1) * 128],
                                      SQB[:, k * 512:(k + 1) * 512],
                                      start=(k == 0), stop=(k == NDT - 1))
                    K.inc(i, "mm")
                mmo = K.val("mm")

                buf = XC if (o % 2 == 0) else FSC
                @blk.vector
                def _(v, o=o, mmo=mmo, buf=buf):
                    K.wait(v, "mm", mmo)
                    if (o, ) and obuf_so.get(o % 2) is not None:
                        K.wait(v, "so", obuf_so[o % 2])
                    i = v.tensor_tensor(buf[:, :], PS2[:, :], G1[:, o * TC:(o + 1) * TC], ALU.add)
                    K.inc(i, "dv")
                dvob = K.val("dv")

                @blk.sync
                def _(q, tb=tb, o=o, dvob=dvob, buf=buf):
                    K.wait(q, "dv", dvob)
                    i = q.dma_start(out=yT[o * 128:(o + 1) * 128, tb * 512:(tb + 1) * 512],
                                    in_=buf[:, :])
                    K.inc(i, "so", 16)
                obuf_so[o % 2] = K.val("so")

        so_fin = K.val("so")
        @blk.gpsimd
        def _(g):
            g.wait_ge(K.sems["so"], so_fin)
    return nc


def dev_run_l1attn(x, inp, pre="L1_attn_"):
    import ml_dtypes
    nc = _build_l1attn_launch()
    B_ = x.shape[0]
    tri_i = np.triu(np.ones((128, 128), np.float32)).astype(ml_dtypes.bfloat16)
    tri_ls = np.tril(np.ones((128, 128), np.float32), -1).astype(ml_dtypes.bfloat16)
    ide_b = np.eye(128).astype(ml_dtypes.bfloat16)
    ide_f = np.eye(128).astype(np.float32)
    feeds = []
    for b in range(B_):
        feeds.append({
            "xT": np.ascontiguousarray(x[b].T),
            "wq": np.asarray(inp[pre + "Wq"], np.float32),
            "wk": np.asarray(inp[pre + "Wk"], np.float32),
            "wv": np.asarray(inp[pre + "Wv"], np.float32),
            "wa": np.asarray(inp[pre + "Wa"], np.float32),
            "ba": np.asarray(inp[pre + "ba"], np.float32),
            "wb": np.asarray(inp[pre + "Wb"], np.float32),
            "bb": np.asarray(inp[pre + "bb"], np.float32),
            "wg": np.asarray(inp[pre + "Wg"], np.float32),
            "wo": np.asarray(inp[pre + "Wo"], np.float32),
            "gn": np.asarray(inp[pre + "g_norm"], np.float32),
            "triui": tri_i, "trils": tri_ls, "idenb": ide_b, "idenf": ide_f,
        })
    res = run_bass_kernel_spmd(nc, feeds, core_ids=list(range(B_)))
    out = np.stack([np.asarray(res.results[b]["yT"], np.float32).T for b in range(B_)])
    return out


def _build_l2attn_launch():
    nc = bass.Bass("TRN2", target_bir_lowering=False, detect_race_conditions=False)
    TT_ = 512
    NB = 8
    xs = nc.dram_tensor("xs", [NB, D, TT_], F32, kind="ExternalInput").ap()
    wqkv = nc.dram_tensor("wqkv", [3 * D, D], F32, kind="ExternalInput").ap()
    bqkv = nc.dram_tensor("bqkv", [3 * D], F32, kind="ExternalInput").ap()
    wproj = nc.dram_tensor("wproj", [D, D], F32, kind="ExternalInput").ap()
    bproj = nc.dram_tensor("bproj", [D], F32, kind="ExternalInput").ap()
    mskd = nc.dram_tensor("mskd", [128, 1024], BF16, kind="ExternalInput").ap()
    idenb = nc.dram_tensor("idenb", [128, 128], BF16, kind="ExternalInput").ap()
    ys = nc.dram_tensor("ys", [NB, D, TT_], F32, kind="ExternalOutput").ap()

    with ExitStack() as stack:
        blk = stack.enter_context(nc.Block())
        K = Ctx(nc, blk, stack)
        for s_ in ("lx", "lw", "mm", "dv", "ac", "so"):
            K.sem(s_)
        ONES_F = K.sb("e_ONESF", [128, 1], F32)
        ONES_B = K.sb("e_ONESB", [128, 1], BF16)
        ONES_R = K.sb("e_ONESR", [1, 128], F32)
        ONES_RB = K.sb("e_ONESRB", [1, 128], BF16)
        WQB = K.sb("e_WQB", [128, 8 * 3072], BF16)
        WPB = K.sb("e_WPB", [128, 8 * 1024], BF16)
        WQS = K.sb("e_WQS", [128, 1024], BF16)
        BQR = K.sb("e_BQR", [1, 3072], BF16)
        BPC = K.sb("e_BPC", [128, 8], F32)
        MSK = K.sb("e_MSK", [128, 1024], BF16)
        IDB2 = K.sb("e_IDB", [128, 128], BF16)
        QTM = K.sb("e_QTM", [128, NB * 1024], BF16)
        KTM = K.sb("e_KTM", [128, NB * 1024], BF16)
        VTM = K.sb("e_VTM", [128, NB * 1024], BF16)
        G1 = K.sb("e_G1", [128, 1024], F32)
        SQB2 = K.sb("e_SQB2", [128, 1024], BF16)
        XT2 = K.sb("e_XT2", [128, 128], F32)
        MU2 = K.sb("e_MU2", [1, 128], F32)
        XLN = K.sb("e_XLN", [128, 1024], BF16)
        PRD = K.sb("e_PRD", [128, 1024], F32)
        FLD = K.sb("e_FLD", [128, 512], F32)
        EXC = K.sb("e_EXC", [128, 1024], F32)
        SCB = K.sb("e_SCB", [128, 1024], F32)
        DEN = K.sb("e_DEN", [128, 128], F32)
        DEN2 = K.sb("e_DEN2", [128, 128], F32)
        RC = K.sb("e_RC", [128, 128], F32)
        ACCA = K.sb("e_ACCA", [128, 1024], F32)
        ACCB = K.sb("e_ACCB", [128, 1024], F32)
        OUTB = K.sb("e_OUTB", [128, 1024], BF16)
        OTT = K.sb("e_OTT", [128, 1024], BF16)
        OB2A = K.sb("e_OB2A", [128, 128], F32)
        OB2B = K.sb("e_OB2B", [128, 128], F32)
        PS1 = stack.enter_context(nc.psum_tensor("e_ps1", [128, 512], F32))
        PS2 = stack.enter_context(nc.psum_tensor("e_ps2", [128, 512], F32))
        PS4 = stack.enter_context(nc.psum_tensor("e_ps4", [128, 128], F32))
        PS6 = stack.enter_context(nc.psum_tensor("e_ps6", [128, 128], BF16))
        PSMU = stack.enter_context(nc.psum_tensor("e_psmu", [128, 128], F32))
        PSRS = stack.enter_context(nc.psum_tensor("e_psrs", [128, 128], F32))
        PS_SUM = stack.enter_context(nc.psum_tensor("e_ps_sum", [1, 128], F32))
        PS_SS = stack.enter_context(nc.psum_tensor("e_ps_ss", [1, 128], F32))

        def sap(buf, offset, ap):
            return bass.AP(tensor=buf.ap().tensor, offset=offset, ap=ap)

        @blk.vector
        def _(v):
            v.memset(ONES_F[:], 1.0)
            v.memset(ONES_B[:], 1.0)
            v.memset(ONES_RB[:], 1.0)
            v.memset(ONES_R[:], 1.0)
            i = v.memset(SCB[:], 0.0)
            K.inc(i, "dv")
        dv_init = K.val("dv")

        @blk.sync
        def _(q):
            i = q.dma_start(out=MSK[:], in_=mskd)
            K.inc(i, "lw", 16)
            i = q.dma_start(out=IDB2[:], in_=idenb)
            K.inc(i, "lw", 16)
            i = q.dma_start(out=BPC[:], in_=bass.AP(tensor=bproj.tensor, offset=0,
                                                    ap=[[1, 128], [128, 8]]),
                            allow_slow_non_contiguous=True)
            K.inc(i, "lw", 16)

        @blk.gpsimd
        def _(g):
            i = g.dma_start(out=BQR[:], in_=bass.AP(tensor=bqkv.tensor, offset=0,
                                                    ap=[[3072, 1], [1, 3072]]))
            K.inc(i, "lw", 16)
        lw_c = K.val("lw")

        for phase, (wsrc, nstrip, wdst, wcols) in enumerate(
                ((wqkv, 24, WQB, 3072), (wproj, 8, WPB, 1024))):
            for st in range(nstrip):
                acp = K.val("ac")
                @blk.gpsimd
                def _(g, wsrc=wsrc, st=st, acp=acp):
                    K.wait(g, "ac", acp)
                    i = g.dma_start(out=WQS[:], in_=wsrc[st * 128:(st + 1) * 128, :])
                    K.inc(i, "lw", 16)
                lws = K.val("lw")

                for k in range(8):
                    acpk = K.val("ac")
                    @blk.tensor
                    def _(tx, k=k, lws=lws, lw_c=lw_c, acpk=acpk):
                        K.wait(tx, "lw", lws)
                        K.wait(tx, "lw", lw_c)
                        K.wait(tx, "ac", acpk)
                        i = tx.transpose(PS6[:, :], WQS[:, k * 128:(k + 1) * 128], IDB2[:])
                        K.inc(i, "mm")
                    mmk = K.val("mm")

                    @blk.scalar
                    def _(sc, st=st, k=k, mmk=mmk, wdst=wdst, wcols=wcols):
                        K.wait(sc, "mm", mmk)
                        i = sc.copy(wdst[:, k * wcols + st * 128:k * wcols + (st + 1) * 128],
                                    PS6[:, :])
                        K.inc(i, "ac")
        lw_w = K.val("lw")
        ac_w = K.val("ac")

        for tt in range(TT_ // 128):
            for n in range(NB):
                dvp = K.val("dv"); acp = K.val("ac"); mmp = K.val("mm")
                @blk.sync
                def _(q, n=n, tt=tt, dvp=dvp, acp=acp, mmp=mmp):
                    K.wait(q, "dv", dvp)
                    K.wait(q, "ac", acp)
                    K.wait(q, "mm", mmp)
                    for k in range(8):
                        i = q.dma_start(out=G1[:, k * 128:(k + 1) * 128],
                                        in_=bass.AP(tensor=xs.tensor,
                                                    offset=n * D * TT_ + k * 128 * TT_ + tt * 128,
                                                    ap=[[TT_, 128], [1, 128]]))
                        K.inc(i, "lx", 16)
                lxg = K.val("lx")

                @blk.scalar
                def _(sc, lxg=lxg):
                    K.wait(sc, "lx", lxg)
                    for k in range(8):
                        i = sc.activation(SQB2[:, k * 128:(k + 1) * 128],
                                          G1[:, k * 128:(k + 1) * 128], AF.Square)
                    K.inc(i, "ac")
                ac_sq = K.val("ac")

                @blk.tensor
                def _(tx, ac_sq=ac_sq, lxg=lxg):
                    K.wait(tx, "lx", lxg)
                    K.wait(tx, "dv", K.val("dv"))
                    for k in range(8):
                        i = tx.matmul(PS_SUM[:, :], ONES_F[:], G1[:, k * 128:(k + 1) * 128],
                                      start=(k == 0), stop=(k == 7))
                    K.inc(i, "mm")
                    K.wait(tx, "ac", ac_sq)
                    for k in range(8):
                        i = tx.matmul(PS_SS[:, :], ONES_B[:], SQB2[:, k * 128:(k + 1) * 128],
                                      start=(k == 0), stop=(k == 7))
                    K.inc(i, "mm")
                mm_st = K.val("mm")

                @blk.vector
                def _(v, mm_st=mm_st):
                    K.wait(v, "mm", mm_st)
                    i = v.tensor_scalar(MU2[:], PS_SUM[:, :], 1.0 / 1024.0, None, ALU.mult)
                    K.inc(i, "dv")
                dv_mu = K.val("dv")

                @blk.tensor
                def _(tx, dv_mu=dv_mu):
                    K.wait(tx, "dv", dv_mu)
                    i = tx.matmul(PSMU[:, :], ONES_R[:], MU2[:], start=True, stop=True)
                    K.inc(i, "mm")
                mm_mur = K.val("mm")

                @blk.vector
                def _(v, mm_mur=mm_mur):
                    v.tensor_tensor(PS2[0:1, 0:128], MU2[:], MU2[:], ALU.mult)
                    K.wait(v, "mm", mm_mur)
                    v.tensor_scalar(MU2[:], PS_SS[:, :], 1.0 / 1024.0, 1e-5, ALU.mult, ALU.add)
                    i = v.tensor_tensor(MU2[:], MU2[:], PS2[0:1, 0:128], ALU.subtract)
                    K.inc(i, "dv")
                dv_var = K.val("dv")

                @blk.scalar
                def _(sc, dv_var=dv_var):
                    K.wait(sc, "dv", dv_var)
                    i = sc.activation(MU2[:], MU2[:], AF.Sqrt)
                    K.inc(i, "ac")
                ac_sd = K.val("ac")

                @blk.vector
                def _(v, ac_sd=ac_sd):
                    K.wait(v, "ac", ac_sd)
                    i = v.reciprocal(MU2[:], MU2[:])
                    K.inc(i, "dv")
                dv_rs = K.val("dv")

                @blk.tensor
                def _(tx, dv_rs=dv_rs):
                    K.wait(tx, "dv", dv_rs)
                    i = tx.matmul(PSRS[:, :], ONES_R[:], MU2[:], start=True, stop=True)
                    K.inc(i, "mm")
                mm_rsr = K.val("mm")

                @blk.vector
                def _(v, mm_rsr=mm_rsr):
                    K.wait(v, "mm", mm_rsr)
                    for k in range(8):
                        v.tensor_tensor(XT2[:], G1[:, k * 128:(k + 1) * 128], PSMU[:, :],
                                        ALU.subtract)
                        i = v.tensor_tensor(XLN[:, k * 128:(k + 1) * 128], XT2[:], PSRS[:, :],
                                            ALU.mult)
                    K.inc(i, "dv")
                dv_ln = K.val("dv")

                for og in range(6):
                    acq = K.val("ac")
                    @blk.tensor
                    def _(tx, n=n, og=og, dv_ln=dv_ln, acq=acq, lw_w=lw_w, ac_w=ac_w):
                        K.wait(tx, "dv", dv_ln)
                        K.wait(tx, "lw", lw_w)
                        K.wait(tx, "ac", ac_w)
                        K.wait(tx, "ac", acq)
                        for k in range(8):
                            tx.matmul(PS1[:, :], XLN[:, k * 128:(k + 1) * 128],
                                      WQB[:, k * 3072 + og * 512:k * 3072 + (og + 1) * 512],
                                      start=(k == 0), stop=False)
                        i = tx.matmul(PS1[:, :], ONES_RB[:], BQR[0:1, og * 512:(og + 1) * 512],
                                      start=False, stop=True)
                        K.inc(i, "mm")
                    mmq = K.val("mm")

                    @blk.scalar
                    def _(sc, n=n, og=og, mmq=mmq):
                        K.wait(sc, "mm", mmq)
                        dst = (QTM, KTM, VTM)[og // 2]
                        i = sc.copy(dst[:, n * 1024 + (og % 2) * 512:n * 1024 + (og % 2 + 1) * 512],
                                    PS1[:, :])
                        K.inc(i, "ac")
            ac_qkv = K.val("ac")

            pairs = [(n, m) for n in range(NB) for m in (range(4) if n < 4 else range(8))]
            @blk.vector
            def _(v, tt=tt, ac_qkv=ac_qkv, dv_init=dv_init):
                K.wait(v, "ac", ac_qkv)
                K.wait(v, "dv", dv_init)
                for (n, m) in pairs:
                    i = v.tensor_tensor(PRD[:], QTM[:, n * 1024:(n + 1) * 1024],
                                        KTM[:, m * 1024:(m + 1) * 1024], ALU.mult)
                    K.inc(i, "dv"); K.wait(v, "dv")
                    def fence():
                        j = v.memset(DEN2[0:1, 0:1], 0.0)
                        K.inc(j, "dv"); K.wait(v, "dv")
                    i = v.tensor_tensor(FLD[:, 0:512],
                                        sap(PRD, 0, [[1024, 128], [64, 16], [1, 32]]),
                                        sap(PRD, 32, [[1024, 128], [64, 16], [1, 32]]), ALU.add)
                    fence()
                    i = v.tensor_tensor(PRD[:, 0:256],
                                        sap(FLD, 0, [[512, 128], [32, 16], [1, 16]]),
                                        sap(FLD, 16, [[512, 128], [32, 16], [1, 16]]), ALU.add)
                    fence()
                    i = v.tensor_tensor(FLD[:, 0:128],
                                        sap(PRD, 0, [[1024, 128], [16, 16], [1, 8]]),
                                        sap(PRD, 8, [[1024, 128], [16, 16], [1, 8]]), ALU.add)
                    fence()
                    i = v.tensor_tensor(PRD[:, 0:64],
                                        sap(FLD, 0, [[512, 128], [8, 16], [1, 4]]),
                                        sap(FLD, 4, [[512, 128], [8, 16], [1, 4]]), ALU.add)
                    fence()
                    i = v.tensor_tensor(FLD[:, 0:32],
                                        sap(PRD, 0, [[1024, 128], [4, 16], [1, 2]]),
                                        sap(PRD, 2, [[1024, 128], [4, 16], [1, 2]]), ALU.add)
                    fence()
                    i = v.tensor_tensor(sap(SCB, n * 128 + m, [[1024, 128], [8, 16]]),
                                        sap(FLD, 0, [[512, 128], [2, 16], [1, 1]]),
                                        sap(FLD, 1, [[512, 128], [2, 16], [1, 1]]), ALU.add)
                    fence()
                K.inc(v.memset(DEN2[0:1, 0:1], 0.0), "dv")
            dv_sc = K.val("dv")

            @blk.scalar
            def _(sc, dv_sc=dv_sc):
                K.wait(sc, "dv", dv_sc)
                i = sc.activation(EXC[:], SCB[:], AF.Exp, scale=0.125)
                K.inc(i, "ac")
            ac_ex = K.val("ac")

            @blk.vector
            def _(v, ac_ex=ac_ex):
                K.wait(v, "ac", ac_ex)
                i = v.tensor_tensor(SCB[:], EXC[:], MSK[:], ALU.mult)
                K.inc(i, "dv")
                K.wait(v, "dv")
                i = v.tensor_tensor(DEN[:], sap(SCB, 0, [[1024, 128], [8, 128]]),
                                    sap(SCB, 1, [[1024, 128], [8, 128]]), ALU.add)
                K.inc(i, "dv")
                cur, nxt = DEN, DEN2
                for m in range(2, 8):
                    K.wait(v, "dv")
                    i = v.tensor_tensor(nxt[:], cur[:], sap(SCB, m, [[1024, 128], [8, 128]]),
                                        ALU.add)
                    K.inc(i, "dv")
                    cur, nxt = nxt, cur
                K.wait(v, "dv")
                i = v.reciprocal(RC[:], cur[:])
                K.inc(i, "dv")
            dv_rc = K.val("dv")

            ob_so = {}
            for n in range(NB):
                dvn = K.val("dv"); acn = K.val("ac")
                @blk.vector
                def _(v, n=n, dvn=dvn, acn=acn):
                    K.wait(v, "dv", dvn)
                    K.wait(v, "ac", acn)
                    ms = list(range(4)) if n < 4 else list(range(8))
                    acc, oth = ACCA, ACCB
                    for ji, m in enumerate(ms):
                        asl = sap(SCB, n * 128 + m, [[1024, 128], [8, 16], [0, 64]])
                        if ji == 0:
                            i = v.tensor_tensor(acc[:], VTM[:, m * 1024:(m + 1) * 1024], asl,
                                                ALU.mult)
                        else:
                            i = v.tensor_tensor(PRD[:], VTM[:, m * 1024:(m + 1) * 1024], asl,
                                                ALU.mult)
                            K.inc(i, "dv")
                            K.wait(v, "dv")
                            i = v.tensor_tensor(oth[:], acc[:], PRD[:], ALU.add)
                            acc, oth = oth, acc
                        K.inc(i, "dv")
                        K.wait(v, "dv")
                    i = v.tensor_tensor(EXC[:], acc[:],
                                        sap(RC, n * 16, [[128, 128], [1, 16], [0, 64]]), ALU.mult)
                    K.inc(i, "dv")
                dv_o = K.val("dv")

                @blk.scalar
                def _(sc, dv_o=dv_o):
                    K.wait(sc, "dv", dv_o)
                    i = sc.copy(OUTB[:], EXC[:])
                    K.inc(i, "ac")
                ac_ob = K.val("ac")

                sop = K.val("so")
                @blk.sync
                def _(q, n=n, tt=tt, dv_o=dv_o, sop=sop):
                    K.wait(q, "dv", dv_o)
                    K.wait(q, "so", sop)
                    for k in range(8):
                        i = q.dma_start(out=G1[:, k * 128:(k + 1) * 128],
                                        in_=bass.AP(tensor=xs.tensor,
                                                    offset=n * D * TT_ + k * 128 * TT_ + tt * 128,
                                                    ap=[[TT_, 128], [1, 128]]))
                        K.inc(i, "lx", 16)
                lxr = K.val("lx")

                for ct in range(8):
                    acc2 = K.val("ac")
                    @blk.tensor
                    def _(tx, ct=ct, ac_ob=ac_ob, acc2=acc2):
                        K.wait(tx, "ac", ac_ob)
                        K.wait(tx, "ac", acc2)
                        i = tx.transpose(PS6[:, :], OUTB[:, ct * 128:(ct + 1) * 128], IDB2[:])
                        K.inc(i, "mm")
                    mmt = K.val("mm")

                    @blk.scalar
                    def _(sc, ct=ct, mmt=mmt):
                        K.wait(sc, "mm", mmt)
                        i = sc.copy(OTT[:, ct * 128:(ct + 1) * 128], PS6[:, :])
                        K.inc(i, "ac")
                ac_ot = K.val("ac")

                for o in range(8):
                    dvp2 = K.val("dv")
                    @blk.tensor
                    def _(tx, o=o, ac_ot=ac_ot, dvp2=dvp2):
                        K.wait(tx, "ac", ac_ot)
                        K.wait(tx, "dv", dvp2)
                        for kc in range(8):
                            i = tx.matmul(PS4[:, :],
                                          WPB[:, kc * 1024 + o * 128:kc * 1024 + (o + 1) * 128],
                                          OTT[:, kc * 128:(kc + 1) * 128],
                                          start=(kc == 0), stop=(kc == 7))
                        K.inc(i, "mm")
                    mmp2 = K.val("mm")

                    obuf = OB2A if (o % 2 == 0) else OB2B
                    @blk.vector
                    def _(v, o=o, mmp2=mmp2, obuf=obuf, lxr=lxr):
                        K.wait(v, "mm", mmp2)
                        K.wait(v, "lx", lxr)
                        if ob_so.get(o % 2) is not None:
                            K.wait(v, "so", ob_so[o % 2])
                        i = v.scalar_tensor_tensor(obuf[:], PS4[:, :], BPC[:, o:o + 1],
                                                   G1[:, o * 128:(o + 1) * 128],
                                                   ALU.add, ALU.add)
                        K.inc(i, "dv")
                    dv_ob = K.val("dv")

                    @blk.sync
                    def _(q, n=n, o=o, tt=tt, dv_ob=dv_ob, obuf=obuf):
                        K.wait(q, "dv", dv_ob)
                        i = q.dma_start(out=bass.AP(tensor=ys.tensor,
                                                    offset=n * D * TT_ + o * 128 * TT_ + tt * 128,
                                                    ap=[[TT_, 128], [1, 128]]),
                                        in_=obuf[:])
                        K.inc(i, "so", 16)
                    ob_so[o % 2] = K.val("so")

        so_fin = K.val("so")
        @blk.gpsimd
        def _(g):
            g.wait_ge(K.sems["so"], so_fin)
    return nc


def dev_run_l2attn(x, inp, pre="L2_attn_"):
    import ml_dtypes
    nc = _build_l2attn_launch()
    TT_ = 512
    msk = np.zeros((8, 16, 8), np.float32)
    for n in range(8):
        for m in range(8):
            if n >= 4 or m < 4:
                msk[n, :, m] = 1.0
    mrow = msk.reshape(1, 1024)
    mfull = np.broadcast_to(mrow, (128, 1024)).astype(ml_dtypes.bfloat16)
    ide_b = np.eye(128).astype(ml_dtypes.bfloat16)
    feeds = []
    for c in range(8):
        xs = np.ascontiguousarray(
            np.stack([x[n, c * TT_:(c + 1) * TT_, :].T for n in range(8)]))
        feeds.append({
            "xs": xs,
            "wqkv": np.asarray(inp[pre + "Wqkv"], np.float32),
            "bqkv": np.asarray(inp[pre + "bqkv"], np.float32),
            "wproj": np.asarray(inp[pre + "Wproj"], np.float32),
            "bproj": np.asarray(inp[pre + "bproj"], np.float32),
            "mskd": np.ascontiguousarray(mfull), "idenb": ide_b,
        })
    res = run_bass_kernel_spmd(nc, feeds, core_ids=list(range(8)))
    out = np.zeros_like(x)
    for c in range(8):
        ysc = np.asarray(res.results[c]["ys"], np.float32)
        for n in range(8):
            out[n, c * TT_:(c + 1) * TT_, :] = ysc[n].T
    return out
